# version 08
# baseline (speedup 1.0000x reference)
import contextlib
import numpy as np

import concourse.bass as bass
import concourse.mybir as mybir
from concourse.bass_utils import run_bass_kernel_spmd

F32 = mybir.dt.float32
BF16 = mybir.dt.bfloat16
AF = mybir.ActivationFunctionType
ALU = mybir.AluOpType
AX = mybir.AxisListType

B_FULL = 16
NCORES = 8
B_LOC = B_FULL // NCORES
T = 2048
D = 1024
NH = 16
HD = 64
NTOK = B_LOC * T
NCH = D // 128
TT = 512
NTILES = NTOK // TT
E = 16
EH = 256
SH = 2048
SOFT_C = 12.0
NEGBIG = -1.0e30

ENGINES = ("sync", "gpsimd", "vector", "scalar", "tensor")


class Prog:

    def __init__(self):
        self.q = {e: [] for e in ENGINES}
        self.counts = {}

    def val(self, s):
        return self.counts.get(s, 0)

    def op(self, eng, fn, wait=(), inc=()):
        wait = [(s, v) for (s, v) in wait if v and v > 0]
        inc = list(inc)
        self.q[eng].append((wait, fn, inc))
        out = []
        for s, d in inc:
            self.counts[s] = self.counts.get(s, 0) + d
            out.append(self.counts[s])
        return out[0] if len(out) == 1 else out

    def build(self, nc, stack):
        sems = {s: stack.enter_context(nc.semaphore(s)) for s in self.counts}

        def mk(ename):
            def emit(eng):
                for wait, fn, inc in self.q[ename]:
                    for s, v in wait:
                        eng.wait_ge(sems[s], v)
                    inst = fn(eng)
                    for s, d in inc:
                        inst.then_inc(sems[s], d)
            return emit

        block = stack.enter_context(nc.Block())
        for ename in ENGINES:
            if self.q[ename]:
                getattr(block, ename)(mk(ename))
        return sems


class Env:
    pass


def _sb(nc, stack, name, shape, dtype=F32):
    return stack.enter_context(nc.sbuf_tensor(name, shape, dtype))


def _psum(nc, stack, name):
    return stack.enter_context(nc.psum_tensor(name, [128, 512], F32))


def strided(ap, extra_off, dims):
    base = ap[:, :] if not isinstance(ap, bass.AP) else ap
    return bass.AP(base.tensor, base.offset + extra_off, [list(base.ap[0])] + dims)



def emit_setup(P, nc, stack, env):
    dram = lambda n, s, d=F32, k="ExternalInput": nc.dram_tensor(n, s, d, kind=k).ap()

    env.xT = dram("xT", [D, NTOK])
    env.wa = {k: dram("wa_" + k, [D, D]) for k in ("q", "k", "v", "o")}
    env.ba = {k: dram("ba_" + k, [1, D]) for k in ("q", "k", "v", "o")}
    env.keyT = dram("keyT", [D, 2 * T])
    env.valT = dram("valT", [D, 2 * T])
    env.wb = {k: dram("wb_" + k, [D, D]) for k in ("q", "k", "v", "o")}
    env.bb = {k: dram("bb_" + k, [1, D]) for k in ("q", "k", "v", "o")}
    env.nw = dram("nw", [1, D])
    env.ng = dram("ng", [1, D])
    env.routerT = dram("routerT", [D, E])
    env.cbias = dram("cbias", [1, E])
    env.wsgT = dram("wsgT", [SH // 128, D, 128])
    env.wsuT = dram("wsuT", [SH // 128, D, 128])
    env.wsdT = dram("wsdT", [NCH, SH, 128])
    env.gw = dram("gw", [E, 2, D, 128])
    env.uw = dram("uw", [E, 2, D, 128])
    env.dw = dram("dw", [E, EH, D])

    dint = lambda n, s, d: nc.dram_tensor(n, s, d, kind="Internal").ap()
    env.wab = {k: dint("wab_" + k, [D, D], BF16) for k in env.wa}
    env.wbb = {k: dint("wbb_" + k, [D, D], BF16) for k in env.wb}
    env.keyTb = dint("keyTb", [D, 2 * T], BF16)
    env.valTb = dint("valTb", [D, 2 * T], BF16)
    env.routerTb = dint("routerTb", [D, E], BF16)
    env.wsgTb = dint("wsgTb", [SH // 128, D, 128], BF16)
    env.wsuTb = dint("wsuTb", [SH // 128, D, 128], BF16)
    env.wsdTb = dint("wsdTb", [NCH, SH, 128], BF16)
    env.gwb = dint("gwb", [E, 2, D, 128], BF16)
    env.uwb = dint("uwb", [E, 2, D, 128], BF16)
    env.dwb = dint("dwb", [E, EH, D], BF16)
    env.qscr = dint("qscr", [D, T], BF16)
    env.kbs = dint("kbs", [D, 2 * T], BF16)
    env.vbs = dint("vbs", [D, 2 * T], BF16)
    env.o3 = dint("o3", [D, NTOK], BF16)
    env.xs = dint("xs", [D, NTOK], F32)
    env.srows = dint("srows", [2, NTOK], F32)
    env.ssrow = dint("ssrow", [1, NTOK], F32)
    env.rrow = dint("rrow", [1, NTOK], F32)
    env.out = dram("out", [D, NTOK], F32, k="ExternalOutput")

    groups = {
        "caA": [("wa_q", env.wa["q"], env.wab["q"]), ("wa_k", env.wa["k"], env.wab["k"]),
                ("wa_v", env.wa["v"], env.wab["v"]), ("wa_o", env.wa["o"], env.wab["o"])],
        "caB": [("wb_q", env.wb["q"], env.wbb["q"]), ("wb_k", env.wb["k"], env.wbb["k"]),
                ("wb_v", env.wb["v"], env.wbb["v"]), ("wb_o", env.wb["o"], env.wbb["o"]),
                ("keyT", env.keyT, env.keyTb), ("valT", env.valT, env.valTb)],
        "caF": [("routerT", env.routerT, env.routerTb),
                ("wsgT", env.wsgT, env.wsgTb), ("wsuT", env.wsuT, env.wsuTb),
                ("wsdT", env.wsdT, env.wsdTb),
                ("gw", env.gw, env.gwb), ("uw", env.uw, env.uwb),
                ("dw", env.dw, env.dwb)],
    }
    env.cast_grp = {}
    for grp, casts in groups.items():
        for name, src, dst in casts:
            if len(src.shape) >= 3:
                for i in range(src.shape[0]):
                    P.op("gpsimd", lambda g, s=src, d=dst, i=i:
                         g.dma_start(out=d[i], in_=s[i]), inc=[(grp, 16)])
            else:
                P.op("gpsimd", lambda g, s=src, d=dst:
                     g.dma_start(out=d[:, :], in_=s[:, :]), inc=[(grp, 16)])
        env.cast_grp[grp] = P.val(grp)
    env.cw = lambda grp: (grp, env.cast_grp[grp])

    env.identf = _sb(nc, stack, "identf", [128, 128])
    env.ones_col_b = _sb(nc, stack, "ones_col_b", [128, 1], BF16)
    env.ones_row_f = _sb(nc, stack, "ones_row_f", [1, 128])
    env.ones_row_b = _sb(nc, stack, "ones_row_b", [1, 128], BF16)
    env.hsum_b = _sb(nc, stack, "hsum_b", [128, 2], BF16)
    env.eps1 = _sb(nc, stack, "eps1", [1, 1])
    env.negC = _sb(nc, stack, "negC", [128, 1])
    env.bcol = {}
    for nm in ("ba_q", "ba_k", "ba_o", "bb_q", "bb_k", "bb_v", "bb_o", "nw", "ng"):
        env.bcol[nm] = _sb(nc, stack, "bcol_" + nm, [128, NCH])
    env.bqb_s = _sb(nc, stack, "bqb_s", [128, NCH])
    env.cbrow = _sb(nc, stack, "cbrow", [1, E])
    env.cb128 = _sb(nc, stack, "cb128", [128, E])

    vv = lambda fn, **kw: P.op("vector", fn, **kw)
    g = lambda fn, **kw: P.op("gpsimd", fn, **kw)
    vv(lambda e: e.memset(env.identf[:, :], 0.0))
    vv(lambda e: e.memset(env.ones_col_b[:, :], 1.0))
    vv(lambda e: e.memset(env.ones_row_f[:, :], 1.0))
    vv(lambda e: e.memset(env.ones_row_b[:, :], 1.0))
    vv(lambda e: e.memset(env.eps1[:, :], 1e-5))
    vv(lambda e: e.memset(env.negC[:, :], -SOFT_C))
    vv(lambda e: e.memset(env.hsum_b[:, 0:1], 0.0))
    tok_ms = vv(lambda e: e.memset(env.hsum_b[:, 1:2], 1.0), inc=[("v", 1)])
    g(lambda e: e.affine_select(out=env.identf[:, :], in_=env.identf[:, :],
                                compare_op=ALU.not_equal, fill=1.0, base=0,
                                pattern=[[-1, 128]], channel_multiplier=1),
      wait=[("v", tok_ms)])
    g(lambda e: e.affine_select(out=env.hsum_b[:, 0:1], in_=env.hsum_b[:, 0:1],
                                compare_op=ALU.is_ge, fill=1.0, base=-64,
                                pattern=[[0, 1]], channel_multiplier=1))
    g(lambda e: e.affine_select(out=env.hsum_b[:, 1:2], in_=env.hsum_b[:, 1:2],
                                compare_op=ALU.is_ge, fill=0.0, base=-64,
                                pattern=[[0, 1]], channel_multiplier=1))
    env.tok_consts = g(lambda e: e.nop(), inc=[("g", 1)])

    for nm, srcap in [("ba_q", env.ba["q"]), ("ba_k", env.ba["k"]), ("ba_o", env.ba["o"]),
                      ("bb_q", env.bb["q"]), ("bb_k", env.bb["k"]), ("bb_v", env.bb["v"]),
                      ("bb_o", env.bb["o"]), ("nw", env.nw), ("ng", env.ng)]:
        dst = env.bcol[nm]
        src = bass.AP(srcap.tensor, 0, [[1, 128], [128, NCH]])
        P.op("sync", lambda e, d=dst, s=src: e.dma_start(
            out=d[:, :], in_=s, allow_slow_non_contiguous=True), inc=[("bia", 16)])
    P.op("sync", lambda e: e.dma_start(out=env.cbrow[:, :], in_=env.cbias[:, :]),
         inc=[("bia", 16)])
    tok_bld = P.val("bia")
    tok_bqs = P.op("vector", lambda v: v.tensor_scalar(
        env.bqb_s[:, :], env.bcol["bb_q"][:, :], 1.0 / np.sqrt(HD), None, ALU.mult),
        wait=[("bia", tok_bld)], inc=[("v", 1)])
    with contextlib.ExitStack() as setup_st:
        pst = setup_st.enter_context(nc.psum_tensor("ps_setup", [128, 512], F32))
        ttok = P.op("tensor", lambda t: t.matmul(
            pst[:, 0:E], env.ones_row_f[:, :], env.cbrow[0:1, :], start=True, stop=True),
            wait=[("bia", tok_bld), ("g", env.tok_consts), ("v", tok_bqs)], inc=[("t", 1)])
        env.tok_bcasts = P.op("vector", lambda v: v.tensor_copy(env.cb128[:, :], pst[:, 0:E]),
                              wait=[("t", ttok)], inc=[("v", 1)])



def emit_stats(P, nc, stack, env, src, src_ready, tag=""):
    with contextlib.ExitStack() as st:
        xt = [_sb(nc, st, f"sa{tag}_x{i}", [128, NCH * TT]) for i in range(2)]
        xb = _sb(nc, st, "sa" + tag + "_xb", [128, NCH * TT], BF16)
        sq = _sb(nc, st, "sa" + tag + "_sq", [128, NCH * TT], BF16)
        row = [_sb(nc, st, f"sa{tag}_row{i}", [1, 2 * TT]) for i in range(2)]
        tmp = _sb(nc, st, "sa" + tag + "_tmp", [1, TT])
        p_mu = _psum(nc, st, "sa" + tag + "_pmu")
        p_is = _psum(nc, st, "sa" + tag + "_pis")
        free_x = [0, 0]
        free_row = [0, 0]
        tok_is_mm_prev = 0
        tok_var_prev = 0
        for i in range(NTILES):
            sl = i % 2
            xsem = "xt%d" % sl
            rsem = "ss%d" % sl
            t0 = i * TT
            tok_ld = P.op("sync", lambda e, sl=sl, t0=t0: e.dma_start(
                out=xt[sl].rearrange("p (c t) -> p c t", t=TT),
                in_=src[:, t0:t0 + TT].rearrange("(c p) t -> p c t", p=128)),
                wait=src_ready(i) + [("v", free_x[sl])], inc=[(xsem, 16)])
            tok_cast = P.op("vector", lambda v, sl=sl: v.tensor_copy(xb[:, :], xt[sl][:, :]),
                            wait=[(xsem, tok_ld), ("t", tok_is_mm_prev)], inc=[("v", 1)])
            free_x[sl] = tok_cast
            tok_sq = P.op("scalar", lambda s: s.activation(sq[:, :], xb[:, :], AF.Square),
                          wait=[("v", tok_cast)], inc=[("s", 1)])
            tok_mu_mm = 0
            for c in range(NCH):
                tok_mu_mm = P.op("tensor", lambda t, c=c: t.matmul(
                    p_mu[0:1, 0:TT], env.ones_col_b[:, :], xb[:, c * TT:(c + 1) * TT],
                    start=(c == 0), stop=(c == NCH - 1)),
                    wait=[("v", tok_cast), ("v", tok_var_prev)] if c == 0 else [],
                    inc=[("t", 1)] if c == NCH - 1 else [])
            tok_is_mm = 0
            for c in range(NCH):
                tok_is_mm = P.op("tensor", lambda t, c=c: t.matmul(
                    p_is[0:1, 0:TT], env.ones_col_b[:, :], sq[:, c * TT:(c + 1) * TT],
                    start=(c == 0), stop=(c == NCH - 1)),
                    wait=[("s", tok_sq)] if c == 0 else [],
                    inc=[("t", 1)] if c == NCH - 1 else [])
            tok_is_mm_prev = tok_is_mm
            tv = P.op("vector", lambda v, sl=sl: v.tensor_scalar(
                row[sl][0:1, 0:TT], p_mu[0:1, 0:TT], 1.0 / D, None, ALU.mult),
                wait=[("t", tok_mu_mm), (rsem, free_row[sl])], inc=[("v", 1)])
            tv = P.op("vector", lambda v, sl=sl: v.tensor_mul(
                tmp[0:1, 0:TT], row[sl][0:1, 0:TT], row[sl][0:1, 0:TT]),
                wait=[("v", tv)], inc=[("v", 1)])
            tok_var = tok_var_prev = P.op("vector", lambda v, sl=sl: v.scalar_tensor_tensor(
                row[sl][0:1, TT:2 * TT], p_is[0:1, 0:TT], 1.0 / D, tmp[0:1, 0:TT],
                ALU.mult, ALU.subtract), wait=[("t", tok_is_mm), ("v", tv)], inc=[("v", 1)])
            ts_ = P.op("scalar", lambda s, sl=sl: s.activation(
                row[sl][0:1, TT:2 * TT], row[sl][0:1, TT:2 * TT], AF.Ln,
                bias=env.eps1[0:1, 0:1]), wait=[("v", tok_var)], inc=[("s", 1)])
            tok_inv = P.op("scalar", lambda s, sl=sl: s.activation(
                row[sl][0:1, TT:2 * TT], row[sl][0:1, TT:2 * TT], AF.Exp, scale=-0.5),
                wait=[("s", ts_)], inc=[("s", 1)])
            P.op("gpsimd", lambda e, sl=sl, t0=t0: e.dma_start(
                out=env.srows[0:1, t0:t0 + TT], in_=row[sl][0:1, 0:TT]),
                wait=[("s", tok_inv)], inc=[(rsem, 16)])
            free_row[sl] = P.op("gpsimd", lambda e, sl=sl, t0=t0: e.dma_start(
                out=env.srows[1:2, t0:t0 + TT], in_=row[sl][0:1, TT:2 * TT]),
                inc=[(rsem, 16)])
    return [("ss0", P.val("ss0")), ("ss1", P.val("ss1"))]



def emit_ffn(P, nc, stack, env, src, dst, src_ready, stats_done, phase, free_dws=0):
    store_toks = {}
    with contextlib.ExitStack() as st:
        xt = [_sb(nc, st, f"f{phase}_x{i}", [128, NCH * TT]) for i in range(2)]
        xn = _sb(nc, st, f"f{phase}_xn", [128, NCH * TT], BF16)
        hsh = _sb(nc, st, f"f{phase}_hsh", [128, 16 * TT], BF16)
        hall = _sb(nc, st, f"f{phase}_hall", [128, 32 * TT], BF16)
        dws = _sb(nc, st, f"f{phase}_dws", [128, E * 2048], BF16)
        geb = [_sb(nc, st, f"f{phase}_ge{i}", [128, 2048], BF16) for i in range(2)]
        ueb = [_sb(nc, st, f"f{phase}_ue{i}", [128, 2048], BF16) for i in range(2)]
        wsg = [_sb(nc, st, f"f{phase}_wsg{i}", [128, D], BF16) for i in range(2)]
        wsu = [_sb(nc, st, f"f{phase}_wsu{i}", [128, D], BF16) for i in range(2)]
        wsd = [_sb(nc, st, f"f{phase}_wsd{i}", [128, SH], BF16) for i in range(2)]
        srow = [_sb(nc, st, f"f{phase}_sr0", [1, 2 * TT])]
        tmpf = _sb(nc, st, f"f{phase}_tmpf", [128, TT])
        slact = _sb(nc, st, f"f{phase}_slact", [128, TT], BF16)
        sgact = slact
        thh = _sb(nc, st, f"f{phase}_thh", [128, TT], BF16)
        cbsb = _sb(nc, st, f"f{phase}_cbsb", [128, TT], BF16)
        xout = [_sb(nc, st, f"f{phase}_xo{i}", [128, TT]) for i in range(2)]
        rtball = _sb(nc, st, f"f{phase}_rtball", [128, NCH * E], BF16)
        rlg = _sb(nc, st, f"f{phase}_rlg", [128, 4 * E])
        rsc = _sb(nc, st, f"f{phase}_rsc", [128, 4 * E])
        rsb = _sb(nc, st, f"f{phase}_rsb", [128, 4 * E])
        rsbm = _sb(nc, st, f"f{phase}_rsbm", [128, 4 * E])
        gm16 = _sb(nc, st, f"f{phase}_gm16", [128, 4 * E])
        rw = rlg
        combine = _sb(nc, st, f"f{phase}_comb", [128, 4 * E])
        rgm = _sb(nc, st, f"f{phase}_rgm", [128, 16])
        rt1 = _sb(nc, st, f"f{phase}_rt1", [128, 16])
        rt2 = _sb(nc, st, f"f{phase}_rt2", [128, 16])
        rt3 = _sb(nc, st, f"f{phase}_rt3", [128, 16])
        rt4 = _sb(nc, st, f"f{phase}_rt4", [128, 16])
        g4a = _sb(nc, st, f"f{phase}_g4a", [128, 4])
        g4b = _sb(nc, st, f"f{phase}_g4b", [128, 4])
        g4c = _sb(nc, st, f"f{phase}_g4c", [128, 4])
        g4d = _sb(nc, st, f"f{phase}_g4d", [128, 4])
        gthr = _sb(nc, st, f"f{phase}_gthr", [128, 4])
        thr = _sb(nc, st, f"f{phase}_thr", [128, 4])
        msk = _sb(nc, st, f"f{phase}_msk", [128, E])
        dnm = _sb(nc, st, f"f{phase}_dnm", [128, 4])
        cT = _sb(nc, st, f"f{phase}_cT", [16, TT], BF16)
        selcb = _sb(nc, st, f"f{phase}_selcb", [16, E * 128], BF16)

        p_g = _psum(nc, st, f"f{phase}_pg")
        p_u = _psum(nc, st, f"f{phase}_pu")
        p_cb = _psum(nc, st, f"f{phase}_pcb")
        p_mu = _psum(nc, st, f"f{phase}_pmu")
        p_is = _psum(nc, st, f"f{phase}_pis")
        p_d0 = _psum(nc, st, f"f{phase}_pd0")
        p_d1 = _psum(nc, st, f"f{phase}_pd1")
        p_r = _psum(nc, st, f"f{phase}_pr")

        tok_sel_ms = P.op("vector", lambda e: e.memset(selcb[:, :], 1.0),
                          inc=[("v", 1)])
        for e_i in range(E):
            P.op("gpsimd", lambda e, e_i=e_i: e.affine_select(
                out=selcb[:, e_i * 128:(e_i + 1) * 128],
                in_=selcb[:, e_i * 128:(e_i + 1) * 128],
                compare_op=ALU.is_equal, fill=0.0, base=-e_i,
                pattern=[[0, 128]], channel_multiplier=1),
                wait=[("v", tok_sel_ms)] if e_i == 0 else [])
        tok_selcb = P.op("gpsimd", lambda e: e.nop(), inc=[("g", 1)])
        tok_rt = P.op("sync", lambda e: e.dma_start(
            out=rtball.rearrange("p (c e) -> p c e", e=E),
            in_=env.routerTb[:, :].rearrange("(c p) e -> p c e", p=128)),
            wait=[env.cw("caF")], inc=[("rt", 16)])
        for e_i in range(E):
            P.op("sync", lambda e, e_i=e_i: e.dma_start(
                out=dws[:, e_i * 2048:(e_i + 1) * 2048].rearrange("p (k c) -> p k c", k=2),
                in_=env.dwb[e_i].rearrange("(k p) c -> p k c", p=128)),
                wait=[env.cw("caF"), ("t", free_dws)], inc=[("dwl", 16)])
        tok_dwl = P.val("dwl")

        free_x = [0, 0]
        free_sr = [0]
        free_ge = [0, 0]; free_ue = [0, 0]
        free_wsg = [0, 0]; free_wsu = [0, 0]; free_wsd = [0, 0]
        free_xo = [0, 0]
        c_rows = lambda mc: slice(mc * 128, (mc + 1) * 128)
        bz = env.tok_bcasts
        rd_pg = ("v", bz); rd_pu = ("v", bz); rd_pcb = ("v", bz)
        rd_pmuis = ("v", bz); rd_pr = ("v", bz)
        rd_pd = {0: ("v", bz), 1: ("v", bz)}
        tok_store_prev = 0
        tok_tensor_tile_prev = 0

        for i in range(NTILES):
            sl = i % 2
            rl = 0
            t0 = i * TT
            xsem = "xt%d" % sl
            tok_xld = P.op("sync", lambda e, sl=sl, t0=t0: e.dma_start(
                out=xt[sl].rearrange("p (c t) -> p c t", t=TT),
                in_=src[:, t0:t0 + TT].rearrange("(c p) t -> p c t", p=128)),
                wait=src_ready(i) + [("v", free_x[sl])], inc=[(xsem, 16)])
            P.op("sync", lambda e, rl=rl, t0=t0: e.dma_start(
                out=srow[rl][0:1, 0:TT], in_=env.srows[0:1, t0:t0 + TT]),
                wait=stats_done(i) + [("t", free_sr[rl])], inc=[("sr", 16)])
            tok_srld = P.op("sync", lambda e, rl=rl, t0=t0: e.dma_start(
                out=srow[rl][0:1, TT:2 * TT], in_=env.srows[1:2, t0:t0 + TT]),
                inc=[("sr", 16)])
            P.op("tensor", lambda t, rl=rl: t.matmul(
                p_mu[:, 0:TT], env.ones_row_f[:, :], srow[rl][0:1, 0:TT],
                start=True, stop=True),
                wait=[("sr", tok_srld), (rd_pmuis[0], rd_pmuis[1])])
            tok_bc = P.op("tensor", lambda t, rl=rl: t.matmul(
                p_is[:, 0:TT], env.ones_row_f[:, :], srow[rl][0:1, TT:2 * TT],
                start=True, stop=True), inc=[("t", 1)])
            free_sr[rl] = tok_bc
            for c in range(NCH):
                P.op("vector", lambda v, sl=sl, c=c: v.tensor_sub(
                    tmpf[:, :], xt[sl][:, c * TT:(c + 1) * TT], p_mu[:, 0:TT]),
                    wait=([("t", tok_bc), (xsem, tok_xld), ("t", tok_tensor_tile_prev)]
                          if c == 0 else []))
                P.op("vector", lambda v, c=c: v.tensor_mul(
                    xn[:, c * TT:(c + 1) * TT], tmpf[:, :], p_is[:, 0:TT]))
            tok_xn = P.op("vector", lambda v: v.engine_nop(), inc=[("v", 1)])
            rd_pmuis = ("v", tok_xn)

            tok_rl = 0
            for gtok in range(4):
                for c in range(NCH):
                    tok_rl = P.op("tensor", lambda t, gtok=gtok, c=c: t.matmul(
                        p_r[:, gtok * E:(gtok + 1) * E],
                        xn[:, c * TT + gtok * 128: c * TT + gtok * 128 + 128],
                        rtball[:, c * E:(c + 1) * E],
                        start=(c == 0), stop=(c == NCH - 1)),
                        wait=([("v", tok_xn), ("rt", tok_rt), (rd_pr[0], rd_pr[1])]
                              if (gtok == 0 and c == 0) else []),
                        inc=[("t", 1)] if (gtok == 3 and c == NCH - 1) else [])
            tok_tanh = P.op("scalar", lambda s: s.activation(
                rlg[:, :], p_r[:, 0:4 * E], AF.Tanh, scale=0.5),
                wait=[("t", tok_rl)], inc=[("s", 1)])
            rd_pr = ("s", tok_tanh)
            _lastv = [P.val("v")]
            def vv(fn, wait=(), **kw):
                tok = P.op("vector", fn, wait=list(wait) + [("v", _lastv[0])],
                           inc=[("v", 1)])
                _lastv[0] = tok
                return tok
            vv(lambda v: v.tensor_scalar(rsc[:, :], rlg[:, :], 0.5, 0.5, ALU.mult, ALU.add),
               wait=[("s", tok_tanh)])
            for gtok in range(4):
                vv(lambda v, gtok=gtok: v.tensor_add(
                    rsb[:, gtok * E:(gtok + 1) * E], rsc[:, gtok * E:(gtok + 1) * E],
                    env.cb128[:, :]))
            s3 = lambda ap, j: strided(ap[:, :], j, [[4, 16]])
            s4 = lambda ap, j: strided(ap[:, :], j, [[4, 4]])
            vv(lambda v: v.tensor_tensor(rt1[:, :], s3(rsb, 0), s3(rsb, 1), ALU.max))
            vv(lambda v: v.tensor_tensor(rt2[:, :], s3(rsb, 0), s3(rsb, 1), ALU.min))
            vv(lambda v: v.tensor_tensor(rt3[:, :], s3(rsb, 2), s3(rsb, 3), ALU.max))
            vv(lambda v: v.tensor_tensor(rt4[:, :], s3(rsb, 2), s3(rsb, 3), ALU.min))
            vv(lambda v: v.tensor_tensor(rgm[:, :], rt1[:, :], rt3[:, :], ALU.min))
            vv(lambda v: v.tensor_tensor(rt2[:, :], rt2[:, :], rt4[:, :], ALU.max))
            vv(lambda v: v.tensor_tensor(rgm[:, :], rgm[:, :], rt2[:, :], ALU.max))
            vv(lambda v: v.tensor_tensor(rt1[:, :], rt1[:, :], rt3[:, :], ALU.max))
            vv(lambda v: v.tensor_add(rgm[:, :], rt1[:, :], rgm[:, :]))
            vv(lambda v: v.tensor_tensor(g4a[:, :], s4(rgm, 0), s4(rgm, 1), ALU.max))
            vv(lambda v: v.tensor_tensor(g4b[:, :], s4(rgm, 0), s4(rgm, 1), ALU.min))
            vv(lambda v: v.tensor_tensor(g4c[:, :], s4(rgm, 2), s4(rgm, 3), ALU.max))
            vv(lambda v: v.tensor_tensor(g4d[:, :], s4(rgm, 2), s4(rgm, 3), ALU.min))
            vv(lambda v: v.tensor_tensor(gthr[:, :], g4a[:, :], g4c[:, :], ALU.min))
            vv(lambda v: v.tensor_tensor(g4b[:, :], g4b[:, :], g4d[:, :], ALU.max))
            vv(lambda v: v.tensor_tensor(gthr[:, :], gthr[:, :], g4b[:, :], ALU.max))
            for b in range(4):
                for j in range(4):
                    dstv = strided(gm16[:, :], b * 16 + j, [[4, 4]])
                    vv(lambda v, b=b, dstv=dstv: v.tensor_scalar(
                        dstv, rgm[:, b * 4:(b + 1) * 4], gthr[:, b:b + 1], None, ALU.is_ge))
            vv(lambda v: v.tensor_mul(rsbm[:, :], rsb[:, :], gm16[:, :]))
            vv(lambda v: v.tensor_scalar(gm16[:, :], gm16[:, :], 1.0e30, -1.0e30,
                                         ALU.mult, ALU.add))
            vv(lambda v: v.tensor_add(rsbm[:, :], rsbm[:, :], gm16[:, :]))
            vv(lambda v: v.tensor_copy(rw[:, :], rsbm[:, :]))
            for it in range(4):
                vv(lambda v: v.reduce_max(
                    thr.rearrange("p (b one) -> p b one", one=1),
                    rw.rearrange("p (b e) -> p b e", e=E), axis=AX.X))
                if it < 3:
                    for b in range(4):
                        vv(lambda v, b=b: v.tensor_scalar(
                            msk[:, :], rw[:, b * E:(b + 1) * E], thr[:, b:b + 1],
                            None, ALU.is_ge))
                        vv(lambda v, b=b: v.scalar_tensor_tensor(
                            rw[:, b * E:(b + 1) * E], msk[:, :], NEGBIG,
                            rw[:, b * E:(b + 1) * E], ALU.mult, ALU.add))
            for b in range(4):
                vv(lambda v, b=b: v.tensor_scalar(
                    combine[:, b * E:(b + 1) * E], rsbm[:, b * E:(b + 1) * E],
                    thr[:, b:b + 1], None, ALU.is_ge))
            vv(lambda v: v.tensor_mul(combine[:, :], combine[:, :], rsc[:, :]))
            vv(lambda v: v.reduce_sum(
                dnm.rearrange("p (b one) -> p b one", one=1),
                combine.rearrange("p (b e) -> p b e", e=E), axis=AX.X))
            vv(lambda v: v.tensor_scalar(dnm[:, :], dnm[:, :], 1e-20, None, ALU.add))
            vv(lambda v: v.reciprocal(dnm[:, :], dnm[:, :]))
            for b in range(4):
                vv(lambda v, b=b: v.tensor_scalar(
                    combine[:, b * E:(b + 1) * E], combine[:, b * E:(b + 1) * E],
                    dnm[:, b:b + 1], None, ALU.mult))
            tok_comb = P.op("vector", lambda v: v.engine_nop(), inc=[("v", 1)])
            for b in range(4):
                ttok = P.op("tensor", lambda t, b=b: t.transpose(
                    p_cb[0:E, 0:128], combine[:, b * E:(b + 1) * E], env.identf[:, :]),
                    wait=[("v", tok_comb), (rd_pcb[0], rd_pcb[1])] if b == 0
                    else [(rd_pcb[0], rd_pcb[1])], inc=[("t", 1)])
                stok = P.op("scalar", lambda s, b=b: s.copy(
                    cT[:, b * 128:(b + 1) * 128], p_cb[0:E, 0:128]),
                    wait=[("t", ttok)], inc=[("s", 1)])
                rd_pcb = ("s", stok)
            tok_cT = P.val("s")
            tok_hsh_parts = 0
            for mc in range(SH // 128):
                wslot = mc % 2
                tok_wg = P.op("sync", lambda e, mc=mc, wslot=wslot: e.dma_start(
                    out=wsg[wslot].rearrange("p (c x) -> p c x", x=128),
                    in_=env.wsgTb[mc].rearrange("(c p) x -> p c x", p=128)),
                    wait=[env.cw("caF"), ("t", free_wsg[wslot])],
                    inc=[("wg%d" % wslot, 16)])
                tok_wu = P.op("sync", lambda e, mc=mc, wslot=wslot: e.dma_start(
                    out=wsu[wslot].rearrange("p (c x) -> p c x", x=128),
                    in_=env.wsuTb[mc].rearrange("(c p) x -> p c x", p=128)),
                    wait=[env.cw("caF"), ("t", free_wsu[wslot])],
                    inc=[("wu%d" % wslot, 16)])
                tok_gmm = 0
                for c in range(NCH):
                    tok_gmm = P.op("tensor", lambda t, c=c, wslot=wslot: t.matmul(
                        p_g[:, 0:TT], wsg[wslot][:, c * 128:(c + 1) * 128],
                        xn[:, c * TT:(c + 1) * TT],
                        start=(c == 0), stop=(c == NCH - 1)),
                        wait=([("wg%d" % wslot, tok_wg), ("v", tok_xn),
                               (rd_pg[0], rd_pg[1])] if c == 0 else []),
                        inc=[("t", 1)] if c == NCH - 1 else [])
                free_wsg[wslot] = tok_gmm
                tok_sg = P.op("scalar", lambda s: s.activation(
                    sgact[:, :], p_g[:, 0:TT], AF.Silu),
                    wait=[("t", tok_gmm), ("v", tok_hsh_parts)], inc=[("s", 1)])
                rd_pg = ("s", tok_sg)
                tok_umm = 0
                for c in range(NCH):
                    tok_umm = P.op("tensor", lambda t, c=c, wslot=wslot: t.matmul(
                        p_u[:, 0:TT], wsu[wslot][:, c * 128:(c + 1) * 128],
                        xn[:, c * TT:(c + 1) * TT],
                        start=(c == 0), stop=(c == NCH - 1)),
                        wait=([("wu%d" % wslot, tok_wu), (rd_pu[0], rd_pu[1])]
                              if c == 0 else []),
                        inc=[("t", 1)] if c == NCH - 1 else [])
                free_wsu[wslot] = tok_umm
                tok_hsh_parts = P.op("vector", lambda v, mc=mc: v.scalar_tensor_tensor(
                    hsh[:, mc * TT:(mc + 1) * TT], p_u[:, 0:TT], 1.0, sgact[:, :],
                    ALU.mult, ALU.mult),
                    wait=[("t", tok_umm), ("s", tok_sg)], inc=[("v", 1)])
                rd_pu = ("v", tok_hsh_parts)
            tok_hsh = tok_hsh_parts

            tok_h_parts = 0
            for e_i in range(E):
                wslot = e_i % 2
                tok_ge = P.op("sync", lambda e, e_i=e_i, wslot=wslot: e.dma_start(
                    out=geb[wslot].rearrange("p (h c x) -> p h c x", h=2, x=128),
                    in_=env.gwb[e_i].rearrange("h (c p) x -> p h c x", p=128)),
                    wait=[env.cw("caF"), ("t", free_ge[wslot])],
                    inc=[("ge%d" % wslot, 16)])
                tok_ue = P.op("sync", lambda e, e_i=e_i, wslot=wslot: e.dma_start(
                    out=ueb[wslot].rearrange("p (h c x) -> p h c x", h=2, x=128),
                    in_=env.uwb[e_i].rearrange("h (c p) x -> p h c x", p=128)),
                    wait=[env.cw("caF"), ("t", free_ue[wslot])],
                    inc=[("ue%d" % wslot, 16)])
                tok_cbmm = P.op("tensor", lambda t, e_i=e_i: t.matmul(
                    p_cb[:, 0:TT], selcb[:, e_i * 128:(e_i + 1) * 128], cT[0:16, 0:TT],
                    start=True, stop=True),
                    wait=[("s", tok_cT), ("g", tok_selcb), (rd_pcb[0], rd_pcb[1])],
                    inc=[("t", 1)])
                tok_cbev = P.op("scalar", lambda s: s.copy(cbsb[:, :], p_cb[:, 0:TT]),
                                wait=[("t", tok_cbmm), ("v", tok_h_parts)], inc=[("s", 1)])
                rd_pcb = ("s", tok_cbev)
                for hc in range(2):
                    tok_gmm = 0
                    for c in range(NCH):
                        tok_gmm = P.op("tensor", lambda t, c=c, hc=hc, wslot=wslot: t.matmul(
                            p_g[:, 0:TT],
                            geb[wslot][:, hc * 1024 + c * 128: hc * 1024 + c * 128 + 128],
                            xn[:, c * TT:(c + 1) * TT],
                            start=(c == 0), stop=(c == NCH - 1)),
                            wait=([("ge%d" % wslot, tok_ge), (rd_pg[0], rd_pg[1])]
                                  if c == 0 else []),
                            inc=[("t", 1)] if c == NCH - 1 else [])
                    if hc == 1:
                        free_ge[wslot] = tok_gmm
                    tok_sl = P.op("scalar", lambda s, hc=hc: s.activation(
                        slact[:, :], p_g[:, 0:TT], AF.Silu),
                        wait=[("t", tok_gmm), ("v", tok_h_parts)], inc=[("s", 1)])
                    rd_pg = ("s", tok_sl)
                    tok_umm = 0
                    for c in range(NCH):
                        tok_umm = P.op("tensor", lambda t, c=c, hc=hc, wslot=wslot: t.matmul(
                            p_u[:, 0:TT],
                            ueb[wslot][:, hc * 1024 + c * 128: hc * 1024 + c * 128 + 128],
                            xn[:, c * TT:(c + 1) * TT],
                            start=(c == 0), stop=(c == NCH - 1)),
                            wait=([("ue%d" % wslot, tok_ue), (rd_pu[0], rd_pu[1])]
                                  if c == 0 else []),
                            inc=[("t", 1)] if c == NCH - 1 else [])
                    if hc == 1:
                        free_ue[wslot] = tok_umm
                    tok_last_xn_rd = tok_umm
                    P.op("vector", lambda v, hc=hc: v.scalar_tensor_tensor(
                        thh[:, :], p_u[:, 0:TT], 1.0, slact[:, :],
                        ALU.mult, ALU.mult),
                        wait=[("t", tok_umm), ("s", max(tok_sl, tok_cbev))])
                    tok_h_parts = P.op("vector", lambda v, e_i=e_i, hc=hc: v.tensor_mul(
                        hall[:, (e_i * 2 + hc) * TT:(e_i * 2 + hc + 1) * TT],
                        thh[:, :], cbsb[:, :]), inc=[("v", 1)])
                    rd_pu = ("v", tok_h_parts)
            tok_hall = tok_h_parts

            tok_last_dmm = 0
            for mc in range(NCH):
                dslot = mc % 2
                pd = p_d0 if dslot == 0 else p_d1
                tok_wd = P.op("sync", lambda e, mc=mc, dslot=dslot: e.dma_start(
                    out=wsd[dslot].rearrange("p (c x) -> p c x", x=128),
                    in_=env.wsdTb[mc].rearrange("(c p) x -> p c x", p=128)),
                    wait=[env.cw("caF"), ("t", free_wsd[dslot])],
                    inc=[("wd%d" % dslot, 16)])
                for kc in range(SH // 128):
                    P.op("tensor", lambda t, kc=kc, dslot=dslot, pd=pd: t.matmul(
                        pd[:, 0:TT], wsd[dslot][:, kc * 128:(kc + 1) * 128],
                        hsh[:, kc * TT:(kc + 1) * TT],
                        start=(kc == 0), stop=False),
                        wait=([("wd%d" % dslot, tok_wd), ("v", tok_hsh),
                               (rd_pd[dslot][0], rd_pd[dslot][1])] if kc == 0 else []))
                tok_dmm = 0
                for e_i in range(E):
                    for kc in range(2):
                        last = (e_i == E - 1 and kc == 1)
                        tok_dmm = P.op("tensor", lambda t, e_i=e_i, kc=kc, mc=mc, last=last,
                                       pd=pd: t.matmul(
                                           pd[:, 0:TT],
                                           dws[:, e_i * 2048 + kc * 1024 + mc * 128:
                                                e_i * 2048 + kc * 1024 + mc * 128 + 128],
                                           hall[:, (e_i * 2 + kc) * TT:(e_i * 2 + kc + 1) * TT],
                                           start=False, stop=last),
                                       wait=([("v", tok_hall), ("dwl", tok_dwl)]
                                             if (e_i == 0 and kc == 0 and mc == 0) else []),
                                       inc=[("t", 1)] if last else [])
                free_wsd[dslot] = tok_dmm
                tok_last_dmm = tok_dmm
                tok_ev = P.op("vector", lambda v, mc=mc, sl=sl, pd=pd: v.scalar_tensor_tensor(
                    xout[mc % 2][:, :], pd[:, 0:TT], 1.0,
                    xt[sl][:, mc * TT:(mc + 1) * TT], ALU.mult, ALU.add),
                    wait=[("t", tok_dmm), ("xo%d" % (mc % 2), free_xo[mc % 2])],
                    inc=[("v", 1)])
                rd_pd[dslot] = ("v", tok_ev)
                free_xo[mc % 2] = P.op("gpsimd", lambda e, mc=mc, t0=t0: e.dma_start(
                    out=dst[c_rows(mc), t0:t0 + TT], in_=xout[mc % 2][:, :]),
                    wait=[("v", tok_ev)], inc=[("xo%d" % (mc % 2), 16)])
            free_x[sl] = P.val("v")
            tok_tensor_tile_prev = tok_last_dmm
            store_toks[i] = [("xo0", free_xo[0]), ("xo1", free_xo[1])]
    return store_toks, P.val("t")


def emit_barrier(P):
    snap = [(s, P.val(s)) for s in P.counts]
    for e in ENGINES:
        P.op(e, lambda x: x.nop(), wait=snap)





def emit_attn_a(P, nc, stack, env, src, dst, src_ready, stats_done):
    store_toks = {}
    NKC = T // 128
    NQG = T // TT
    with contextlib.ExitStack() as st:
        masks = _sb(nc, st, "aa_masks", [128, 4 * TT])
        selz = _sb(nc, st, "aa_selz", [128, NCH * 128])
        kts = _sb(nc, st, "aa_kts", [128, NCH * T], BF16)
        vts = _sb(nc, st, "aa_vts", [128, NKC * 16 * 80], BF16)
        warena = [_sb(nc, st, f"aa_w{i}", [128, NCH * 1024], BF16) for i in range(3)]
        xt = _sb(nc, st, "aa_xt", [128, NCH * TT])
        xn = _sb(nc, st, "aa_xn", [128, NCH * TT], BF16)
        qstage = _sb(nc, st, "aa_qst", [128, TT], BF16)
        qsl = _sb(nc, st, "aa_qsl", [128, NCH * TT], BF16)
        pt = [_sb(nc, st, f"aa_pt{i}", [128, 2 * TT], BF16) for i in range(2)]
        oraw = _sb(nc, st, "aa_oraw", [128, NCH * TT], BF16)
        onrm = _sb(nc, st, "aa_onrm", [128, NCH * TT], BF16)
        zbuf = _sb(nc, st, "aa_zbuf", [128, TT])
        srow = _sb(nc, st, "aa_srow", [1, 2 * TT])
        xo = [_sb(nc, st, f"aa_xo{i}", [128, TT]) for i in range(2)]
        bva_row = _sb(nc, st, "aa_bvar", [1, D])
        bva128 = _sb(nc, st, "aa_bva", [128, D])
        pd = [st.enter_context(nc.psum_tensor(f"aa_pd{i}", [128, 1024], F32))
              for i in range(2)]
        ps4 = [_psum(nc, st, f"aa_p{i}") for i in range(4)]
        ps = [pd[0][:, 0:512], pd[0][:, 512:1024],
              pd[1][:, 0:512], pd[1][:, 512:1024],
              ps4[0][:, :], ps4[1][:, :], ps4[2][:, :], ps4[3][:, :]]

        tok_bv = P.op("sync", lambda e: e.dma_start(out=bva_row[:, :], in_=env.ba["v"][:, :]),
                      inc=[("bv", 16)])
        tok_bvv = env.tok_bcasts
        for half in range(2):
            tb = P.op("tensor", lambda t, half=half: t.matmul(
                ps[7][:, 0:TT], env.ones_row_f[:, :],
                bva_row[0:1, half * TT:(half + 1) * TT], start=True, stop=True),
                wait=[("bv", tok_bv), ("g", env.tok_consts), ("v", tok_bvv)],
                inc=[("t", 1)])
            tok_bvv = P.op("vector", lambda v, half=half: v.tensor_copy(
                bva128[:, half * TT:(half + 1) * TT], ps[7][:, 0:TT]),
                wait=[("t", tb)], inc=[("v", 1)])
        rd_p7_init = ("v", tok_bvv)

        tok_ms = P.op("vector", lambda v: v.memset(masks[:, :], NEGBIG), inc=[("v", 1)])
        for j in range(4):
            off = j * 128
            P.op("gpsimd", lambda e, j=j, off=off: e.affine_select(
                out=masks[:, j * TT:(j + 1) * TT], in_=masks[:, j * TT:(j + 1) * TT],
                compare_op=ALU.is_ge, fill=0.0, base=off - 1,
                pattern=[[-1, TT]], channel_multiplier=1),
                wait=[("v", tok_ms)] if j == 0 else [])
        tok_sz = P.op("vector", lambda v: v.memset(selz[:, :], 1.0), inc=[("v", 1)])
        first_sel = True
        for c in range(NCH):
            for half in range(2):
                h = 2 * c + half
                P.op("gpsimd", lambda e, c=c, half=half, h=h: e.affine_select(
                    out=selz[:, c * 128 + half * 64: c * 128 + half * 64 + 64],
                    in_=selz[:, c * 128 + half * 64: c * 128 + half * 64 + 64],
                    compare_op=ALU.is_equal, fill=0.0, base=-(64 + h),
                    pattern=[[0, 64]], channel_multiplier=1),
                    wait=[("v", tok_sz)] if first_sel else [])
                first_sel = False
        tok_consts = P.op("gpsimd", lambda e: e.nop(), inc=[("g", 1)])

        P.op("vector", lambda v: v.memset(vts[:, :], 0.0))
        for h in range(NH):
            base0 = vts[:, :]
            ap = bass.AP(base0.tensor, base0.offset + h * 80 + 64 + h,
                         [list(base0.ap[0]), [1280, NKC], [1, 1]])
            P.op("vector", lambda v, ap=ap: v.memset(ap, 1.0))
        tok_vts_ones = P.op("vector", lambda v: v.engine_nop(), inc=[("v", 1)])

        kstage = [_sb(nc, st, f"aa_kst{i}", [128, TT], BF16) for i in range(2)]

        def kv_gen():
            free_kst = [0, 0]
            rd_kv = ("v", env.tok_bcasts)
            for which, wsrc, kvsrc, kvdst, bname in (
                    (0, env.wbb["k"], env.keyTb, env.kbs, "bb_k"),
                    (1, env.wbb["v"], env.valTb, env.vbs, "bb_v")):
                tok_wl = P.op("sync", lambda e, wsrc=wsrc: e.dma_start(
                    out=warena[1].rearrange("p (c x) -> p c x", x=1024),
                    in_=wsrc.rearrange("(c p) x -> p c x", p=128)),
                    wait=[env.cw("caB"), ("t", P.val("t"))], inc=[("kw1", 16)])
                for rt in range(2 * T // TT):
                    tok_kv = P.op("sync", lambda e, kvsrc=kvsrc, rt=rt: e.dma_start(
                        out=warena[2][:, 0:NCH * TT].rearrange("p (c t) -> p c t", t=TT),
                        in_=kvsrc[:, rt * TT:(rt + 1) * TT]
                        .rearrange("(c p) t -> p c t", p=128)),
                        wait=[env.cw("caB"), ("t", P.val("t"))], inc=[("kv", 16)])
                    for mc in range(NCH):
                        tok_mm = 0
                        for c in range(NCH):
                            tok_mm = P.op("tensor", lambda t, mc=mc, c=c: t.matmul(
                                ps[6][:, 0:TT],
                                warena[1][:, c * 1024 + mc * 128: c * 1024 + mc * 128 + 128],
                                warena[2][:, c * TT:(c + 1) * TT],
                                start=(c == 0), stop=(c == NCH - 1)),
                                wait=([("kw1", tok_wl), ("kv", tok_kv),
                                       (rd_kv[0], rd_kv[1])] if c == 0 else []),
                                inc=[("t", 1)] if c == NCH - 1 else [])
                        ksl2 = mc % 2
                        tok_ev = P.op("scalar", lambda sc, mc=mc, bname=bname,
                                      ksl2=ksl2: sc.activation(
                            kstage[ksl2][:, :], ps[6][:, 0:TT], AF.Identity,
                            bias=env.bcol[bname][:, mc:mc + 1]),
                            wait=[("t", tok_mm), ("kvs", free_kst[ksl2])], inc=[("s", 1)])
                        rd_kv = ("s", tok_ev)
                        free_kst[ksl2] = P.op(
                            "gpsimd", lambda e, kvdst=kvdst, mc=mc, rt=rt, ksl2=ksl2:
                            e.dma_start(
                                out=kvdst[mc * 128:(mc + 1) * 128, rt * TT:(rt + 1) * TT],
                                in_=kstage[ksl2][:, :]),
                            wait=[("s", P.val("s"))], inc=[("kvs", 16)])
                        yield
            while True:
                yield

        kvg = kv_gen()
        free_w = [0, 0, 0]
        free_qst = 0
        free_xo = [0, 0]
        free_xt = 0
        free_qsl = 0
        rd_ps = [("v", env.tok_bcasts)] * 8
        rd_ps[7] = rd_p7_init
        rd_pd = [("v", env.tok_bcasts), ("v", env.tok_bcasts)]
        rd_ps_av = list(rd_ps)
        rd_pt = [0, 0]
        tok_kts_rd = 0
        tok_vts_rd = 0
        tok_xn_rd = 0
        tok_srow_rd = 0
        for s_i in range(B_LOC):
            base = s_i * T
            tok_w = [0, 0, 0]
            for wi, wname in enumerate(("q", "k", "v")):
                tok_w[wi] = P.op("sync", lambda e, wi=wi, wname=wname: e.dma_start(
                    out=warena[wi].rearrange("p (c x) -> p c x", x=1024),
                    in_=env.wab[wname].rearrange("(c p) x -> p c x", p=128)),
                    wait=[env.cw("caA"), ("t", free_w[wi])], inc=[("kw%d" % wi, 16)])
            tok_kts_wr = 0
            tok_vts_wr = 0
            for j in range(NQG):
                t0 = base + j * TT
                tok_xld = P.op("sync", lambda e, t0=t0: e.dma_start(
                    out=xt.rearrange("p (c t) -> p c t", t=TT),
                    in_=src[:, t0:t0 + TT].rearrange("(c p) t -> p c t", p=128)),
                    wait=src_ready(t0 // TT) + [("v", free_xt)], inc=[("ax", 16)])
                P.op("sync", lambda e, t0=t0: e.dma_start(
                    out=srow[0:1, 0:TT], in_=env.srows[0:1, t0:t0 + TT]),
                    wait=stats_done(t0 // TT) + [("t", tok_srow_rd)], inc=[("sr", 16)])
                tok_sr = P.op("sync", lambda e, t0=t0: e.dma_start(
                    out=srow[0:1, TT:2 * TT], in_=env.srows[1:2, t0:t0 + TT]),
                    inc=[("sr", 16)])
                P.op("tensor", lambda t: t.matmul(
                    ps[0][:, 0:TT], env.ones_row_f[:, :], srow[0:1, 0:TT],
                    start=True, stop=True),
                    wait=[("sr", tok_sr), (rd_ps[0][0], rd_ps[0][1])])
                tok_bc = P.op("tensor", lambda t: t.matmul(
                    ps[1][:, 0:TT], env.ones_row_f[:, :], srow[0:1, TT:2 * TT],
                    start=True, stop=True),
                    wait=[(rd_ps[1][0], rd_ps[1][1])], inc=[("t", 1)])
                tok_srow_rd = tok_bc
                for c in range(NCH):
                    P.op("vector", lambda v, c=c: v.tensor_sub(
                        xo[0][:, :], xt[:, c * TT:(c + 1) * TT], ps[0][:, 0:TT]),
                        wait=([("t", tok_bc), ("ax", tok_xld), ("t", tok_xn_rd),
                               ("xo0", free_xo[0])] if c == 0 else []))
                    P.op("vector", lambda v, c=c: v.tensor_mul(
                        xn[:, c * TT:(c + 1) * TT], xo[0][:, :], ps[1][:, 0:TT]))
                tok_xn = P.op("vector", lambda v: v.engine_nop(), inc=[("v", 1)])
                free_xt = tok_xn
                rd_ps[0] = ("v", tok_xn)
                rd_ps[1] = ("v", tok_xn)
                for wi in range(2):
                    for mc in range(NCH):
                        pb = 2 + (mc % 2)
                        tok_mm = 0
                        for c in range(NCH):
                            tok_mm = P.op("tensor", lambda t, wi=wi, mc=mc, c=c, pb=pb: t.matmul(
                                ps[pb][:, 0:TT],
                                warena[wi][:, c * 1024 + mc * 128: c * 1024 + mc * 128 + 128],
                                xn[:, c * TT:(c + 1) * TT],
                                start=(c == 0), stop=(c == NCH - 1)),
                                wait=([("kw%d" % wi, tok_w[wi]), ("v", tok_xn),
                                       (rd_ps[pb][0], rd_ps[pb][1])] if c == 0 else []),
                                inc=[("t", 1)] if c == NCH - 1 else [])
                        if wi == 0:
                            tok_ev = P.op("scalar", lambda sc, mc=mc, pb=pb: sc.activation(
                                qstage[:, :], ps[pb][:, 0:TT], AF.Identity,
                                bias=env.bcol["ba_q"][:, mc:mc + 1]),
                                wait=[("t", tok_mm), ("qst", free_qst)], inc=[("s", 1)])
                            rd_ps[pb] = ("s", tok_ev)
                            free_qst = P.op("gpsimd", lambda e, mc=mc, j=j: e.dma_start(
                                out=env.qscr[mc * 128:(mc + 1) * 128, j * TT:(j + 1) * TT],
                                in_=qstage[:, :]),
                                wait=[("s", tok_ev)] + ([("qsl", P.val("qsl"))]
                                                        if (j == 0 and mc == 0) else []),
                                inc=[("qst", 16)])
                        else:
                            tok_ev = P.op("scalar", lambda sc, mc=mc, pb=pb, j=j: sc.activation(
                                kts[:, mc * T + j * TT: mc * T + (j + 1) * TT],
                                ps[pb][:, 0:TT], AF.Identity,
                                bias=env.bcol["ba_k"][:, mc:mc + 1]),
                                wait=[("t", tok_mm)] + ([("t", tok_kts_rd)]
                                                        if (mc == 0 and j == 0) else []),
                                inc=[("s", 1)])
                            rd_ps[pb] = ("s", tok_ev)
                tok_kts_wr = P.val("s")
                for tsl in range(4):
                    kcv = j * 4 + tsl
                    for half in range(2):
                        pb = 4 + half
                        tok_mm = 0
                        for c in range(NCH):
                            tok_mm = P.op("tensor", lambda t, tsl=tsl, half=half, c=c, pb=pb: t.matmul(
                                ps[pb][:, 0:TT],
                                xn[:, c * TT + tsl * 128: c * TT + tsl * 128 + 128],
                                warena[2][:, c * 1024 + half * 512: c * 1024 + (half + 1) * 512],
                                start=(c == 0), stop=(c == NCH - 1)),
                                wait=([("kw2", tok_w[2]), (rd_ps[pb][0], rd_ps[pb][1]),
                                       ("v", tok_vts_ones)] if c == 0 else []),
                                inc=[("t", 1)] if c == NCH - 1 else [])
                        dstb = vts[:, :]
                        dap = bass.AP(dstb.tensor, dstb.offset + kcv * 1280 + half * 8 * 80,
                                      [list(dstb.ap[0]), [80, 8], [1, 64]])
                        tok_ev = P.op("vector", lambda v, pb=pb, dap=dap, half=half: v.tensor_add(
                            dap, ps[pb][:, 0:TT].rearrange("p (h d) -> p h d", d=64),
                            bva128[:, half * 512:(half + 1) * 512]
                            .rearrange("p (h d) -> p h d", d=64)),
                            wait=[("t", tok_mm)] + ([("t", tok_vts_rd)]
                                                    if (j == 0 and tsl == 0 and half == 0)
                                                    else []),
                            inc=[("v", 1)])
                        rd_ps[pb] = ("v", tok_ev)
                tok_vts_wr = P.val("v")
                tok_xn_rd = P.val("t")
            rd_pd[0] = rd_ps[0]
            rd_pd[1] = (rd_ps[2][0], max(rd_ps[2][1], rd_ps[3][1]))
            rd_ps_av[4] = rd_ps[4]
            rd_ps_av[5] = rd_ps[5]
            tok_w3 = P.op("sync", lambda e: e.dma_start(
                out=warena[0].rearrange("p (c x) -> p c x", x=1024),
                in_=env.wab["o"].rearrange("(c p) x -> p c x", p=128)),
                wait=[env.cw("caA"), ("t", P.val("t"))], inc=[("kw0", 16)])
            tok_zb_rd = 0
            for qg in range(NQG):
                n_kc = 4 * (qg + 1)
                tok_q = 0
                for h in range(NH):
                    tok_q = P.op("sync", lambda e, h=h, qg=qg: e.dma_start(
                        out=qsl[(h % 2) * 64:(h % 2) * 64 + 64,
                                (h // 2) * TT:(h // 2 + 1) * TT],
                        in_=env.qscr[h * 64:(h + 1) * 64, qg * TT:(qg + 1) * TT]),
                        wait=([("qst", free_qst), ("t", free_qsl)] if h == 0 else []),
                        inc=[("qsl", 16)])
                tok_xld = P.op("sync", lambda e, qg=qg, base=base: e.dma_start(
                    out=xt.rearrange("p (c t) -> p c t", t=TT),
                    in_=src[:, base + qg * TT: base + (qg + 1) * TT]
                    .rearrange("(c p) t -> p c t", p=128)),
                    wait=[("v", free_xt)], inc=[("ax", 16)])
                tok_zz = P.op("vector", lambda v: v.memset(zbuf[64:80, 0:TT], 0.0),
                              wait=[("t", tok_zb_rd)], inc=[("v", 1)])
                for h in range(NH):
                    pb_av = 4 + (h % 2)
                    n_pair = n_kc // 2
                    tok_exp = [0] * n_pair
                    tok_av = 0

                    def emit_sc_pair(pr):
                        X = pr % 2
                        for sub in range(2):
                            kcv = 2 * pr + sub
                            tok_sc = P.op("tensor", lambda t, h=h, kcv=kcv, X=X, sub=sub: t.matmul(
                                pd[X][:, sub * TT:(sub + 1) * TT],
                                kts[(h % 2) * 64:(h % 2) * 64 + 64,
                                    (h // 2) * T + kcv * 128: (h // 2) * T + kcv * 128 + 128],
                                qsl[(h % 2) * 64:(h % 2) * 64 + 64,
                                    (h // 2) * TT:(h // 2 + 1) * TT],
                                start=True, stop=True),
                                wait=([("qsl", tok_q), ("s", tok_kts_wr)]
                                      if (kcv == 0 and h == 0) else [])
                                + ([(rd_pd[X][0], rd_pd[X][1])] if sub == 0 else []),
                                inc=[("t", 1)])
                            diag = kcv >= qg * 4
                            if diag:
                                off_j = kcv - qg * 4
                                tok_mask[sub] = P.op("vector",
                                    lambda v, X=X, sub=sub, off_j=off_j: v.tensor_add(
                                        pd[X][:, sub * TT:(sub + 1) * TT],
                                        pd[X][:, sub * TT:(sub + 1) * TT],
                                        masks[:, off_j * TT:(off_j + 1) * TT]),
                                    wait=[("t", tok_sc), ("g", tok_consts)], inc=[("v", 1)])
                            else:
                                tok_mask[sub] = 0
                            tok_sc_last[0] = tok_sc
                        w = [("v", tok_mask[0]), ("v", tok_mask[1]), ("t", tok_sc_last[0]),
                             ("t", rd_pt[X])]
                        tok_exp[pr] = P.op("scalar", lambda sc, X=X: sc.activation(
                            pt[X][:, :], pd[X][:, 0:2 * TT], AF.Exp,
                            bias=env.negC[:, 0:1], scale=0.125),
                            wait=w, inc=[("s", 1)])
                        rd_pd[X] = ("s", tok_exp[pr])

                    def emit_av_pair(pr):
                        nonlocal tok_av
                        X = pr % 2
                        for sub in range(2):
                            kcv = 2 * pr + sub
                            tok_av = P.op("tensor", lambda t, h=h, kcv=kcv, X=X, sub=sub,
                                          pb_av=pb_av,
                                          first=(kcv == 0), last=(kcv == n_kc - 1): t.matmul(
                                              ps[pb_av][0:80, 0:TT],
                                              vts[:, kcv * 1280 + h * 80: kcv * 1280 + h * 80 + 80],
                                              pt[X][:, sub * TT:(sub + 1) * TT],
                                              start=first, stop=last),
                                          wait=[("s", tok_exp[pr])]
                                          + ([(rd_ps_av[pb_av][0], rd_ps_av[pb_av][1])]
                                             if kcv == 0 else [])
                                          + ([("v", tok_vts_wr)] if (kcv == 0 and h == 0) else []),
                                          inc=[("t", 1)])
                        rd_pt[X] = tok_av

                    tok_mask = [0, 0]
                    tok_sc_last = [0]
                    skew = 2 if n_pair >= 3 else 1
                    for pr in range(skew):
                        emit_sc_pair(pr)
                    for pr in range(skew, n_pair):
                        emit_sc_pair(pr)
                        emit_av_pair(pr - skew)
                    for pr in range(n_pair - skew, n_pair):
                        emit_av_pair(pr)
                    tok_z = P.op("vector", lambda v, pb_av=pb_av: v.tensor_add(
                        zbuf[64:80, 0:TT], zbuf[64:80, 0:TT], ps[pb_av][64:80, 0:TT]),
                        wait=[("t", tok_av), ("v", tok_zz)], inc=[("v", 1)])
                    tok_oraw = P.op("vector", lambda v, h=h, pb_av=pb_av: v.tensor_copy(
                        oraw[(h % 2) * 64:(h % 2) * 64 + 64, (h // 2) * TT:(h // 2 + 1) * TT],
                        ps[pb_av][0:64, 0:TT]),
                        wait=[("v", tok_z)], inc=[("v", 1)])
                    rd_ps_av[pb_av] = ("v", tok_oraw)
                    next(kvg)
                free_qsl = P.val("t")
                tz1 = P.op("scalar", lambda sc: sc.activation(
                    zbuf[64:80, 0:TT], zbuf[64:80, 0:TT], AF.Ln),
                    wait=[("v", P.val("v"))], inc=[("s", 1)])
                tz2 = P.op("scalar", lambda sc: sc.activation(
                    zbuf[64:80, 0:TT], zbuf[64:80, 0:TT], AF.Exp, scale=-1.0),
                    wait=[("s", tz1)], inc=[("s", 1)])
                for c in range(NCH):
                    pb = c % 2
                    tok_zb = P.op("tensor", lambda t, c=c, pb=pb: t.matmul(
                        ps[pb][:, 0:TT], selz[64:80, c * 128:(c + 1) * 128],
                        zbuf[64:80, 0:TT], start=True, stop=True),
                        wait=[("s", tz2), ("g", tok_consts),
                              (rd_pd[pb][0], rd_pd[pb][1])], inc=[("t", 1)])
                    tok_on = P.op("vector", lambda v, c=c, pb=pb: v.scalar_tensor_tensor(
                        onrm[:, c * TT:(c + 1) * TT], ps[pb][:, 0:TT], 1.0,
                        oraw[:, c * TT:(c + 1) * TT], ALU.mult, ALU.mult),
                        wait=[("t", tok_zb)], inc=[("v", 1)])
                    rd_pd[pb] = ("v", tok_on)
                tok_on = P.val("v")
                tok_zb_rd = P.val("t")
                for mc in range(NCH):
                    pb = 7 if mc % 2 == 0 else 4
                    tok_mm = 0
                    for c in range(NCH):
                        tok_mm = P.op("tensor", lambda t, mc=mc, c=c, pb=pb: t.matmul(
                            ps[pb][:, 0:TT],
                            warena[0][:, c * 1024 + mc * 128: c * 1024 + mc * 128 + 128],
                            onrm[:, c * TT:(c + 1) * TT],
                            start=(c == 0), stop=(c == NCH - 1)),
                            wait=([("kw0", tok_w3), ("v", tok_on),
                                   (rd_ps_av[pb][0], rd_ps_av[pb][1])] if c == 0 else []),
                            inc=[("t", 1)] if c == NCH - 1 else [])
                    tok_ev = P.op("vector", lambda v, mc=mc, pb=pb: v.scalar_tensor_tensor(
                        xo[mc % 2][:, :], ps[pb][:, 0:TT], env.bcol["ba_o"][:, mc:mc + 1],
                        xt[:, mc * TT:(mc + 1) * TT], ALU.add, ALU.add),
                        wait=[("t", tok_mm), ("ax", tok_xld),
                              ("xo%d" % (mc % 2), free_xo[mc % 2])], inc=[("v", 1)])
                    rd_ps_av[pb] = ("v", tok_ev)
                    free_xo[mc % 2] = P.op("gpsimd", lambda e, mc=mc, qg=qg, base=base: e.dma_start(
                        out=dst[mc * 128:(mc + 1) * 128, base + qg * TT: base + (qg + 1) * TT],
                        in_=xo[mc % 2][:, :]),
                        wait=[("v", tok_ev)], inc=[("xo%d" % (mc % 2), 16)])
                free_xt = P.val("v")
                store_toks[4 * s_i + qg] = [("xo0", free_xo[0]), ("xo1", free_xo[1])]
            tok_kts_rd = P.val("t")
            tok_vts_rd = P.val("t")
            tok_xn_rd = P.val("t")
            free_w = [P.val("t")] * 3
            rd_ps[0] = rd_pd[0]
            rd_ps[1] = rd_pd[0]
            rd_ps[2] = rd_pd[1]
            rd_ps[3] = rd_pd[1]
            for _b in range(4, 8):
                rd_ps[_b] = rd_ps_av[_b]
        env.kv_done = ("kvs", P.val("kvs"))
    return store_toks





def emit_attn_b(P, nc, stack, env, src, dst, src_ready, stats_done):
    store_toks = {}
    with contextlib.ExitStack() as st:
        warena = _sb(nc, st, "ab_w", [128, NCH * 1024], BF16)
        xt = _sb(nc, st, "ab_xt", [128, NCH * TT])
        xn = _sb(nc, st, "ab_xn", [128, NCH * TT], BF16)
        qb = _sb(nc, st, "ab_qb", [128, NCH * TT], BF16)
        kv0 = _sb(nc, st, "ab_kv0", [128, NCH * TT], BF16)
        kv1 = _sb(nc, st, "ab_kv1", [128, NCH * TT], BF16)
        vv0 = _sb(nc, st, "ab_vv0", [128, NCH * TT], BF16)
        vv1 = _sb(nc, st, "ab_vv1", [128, NCH * TT], BF16)
        tmp = _sb(nc, st, "ab_tmp", [128, NCH * TT], BF16)
        tmp2 = _sb(nc, st, "ab_tmp2", [128, NCH * TT], BF16)
        pb2 = _sb(nc, st, "ab_pb", [128, 4 * TT], BF16)
        o3s = _sb(nc, st, "ab_o3", [128, NCH * TT], BF16)
        sq = _sb(nc, st, "ab_sq", [128, NCH * TT], BF16)
        hblk = _sb(nc, st, "ab_hblk", [128, 128], BF16)
        srow = _sb(nc, st, "ab_srow", [1, 2 * TT])
        rrst = _sb(nc, st, "ab_rrst", [1, TT])
        t1b = _sb(nc, st, "ab_t1", [128, TT], BF16)
        t2b = _sb(nc, st, "ab_t2", [128, TT], BF16)
        sig = [_sb(nc, st, f"ab_sig{i}", [128, TT], BF16) for i in range(2)]
        xo = [_sb(nc, st, f"ab_xo{i}", [128, TT]) for i in range(2)]
        ps = [_psum(nc, st, f"ab_p{i}") for i in range(8)]

        P.op("vector", lambda v: v.memset(hblk[:, 0:64], 0.0))
        tok_hb = P.op("vector", lambda v: v.memset(hblk[:, 64:128], 1.0), inc=[("v", 1)])
        P.op("gpsimd", lambda e: e.affine_select(
            out=hblk[:, 0:64], in_=hblk[:, 0:64], compare_op=ALU.is_ge,
            fill=1.0, base=-64, pattern=[[0, 64]], channel_multiplier=1),
            wait=[("v", tok_hb)])
        P.op("gpsimd", lambda e: e.affine_select(
            out=hblk[:, 64:128], in_=hblk[:, 64:128], compare_op=ALU.is_ge,
            fill=0.0, base=-64, pattern=[[0, 64]], channel_multiplier=1))
        tok_hblk = P.op("gpsimd", lambda e: e.nop(), inc=[("g", 1)])

        tok_kvdone = env.kv_done[1]
        rd_pre_last = {2: ("v", env.tok_bcasts), 3: ("v", env.tok_bcasts)}
        tok_wq = P.op("sync", lambda e: e.dma_start(
            out=warena.rearrange("p (c x) -> p c x", x=1024),
            in_=env.wbb["q"].rearrange("(c p) x -> p c x", p=128)),
            wait=[env.cw("caB"), ("t", P.val("t"))], inc=[("kw0", 16)])

        free_xt = 0
        tok_ss_prev = 0
        tok_o3st_prev = 0
        tok_sq_prev = 0
        free_rrst = 0
        rd_ps = [("v", env.tok_bcasts)] * 8
        rd_ps[2] = rd_pre_last[2]
        rd_ps[3] = rd_pre_last[3]
        tok_srow_rd = 0
        tok_kvld = 0
        free_kvld = 0
        o3_toks = {}
        tok_wq = P.op("sync", lambda e: e.dma_start(
            out=warena.rearrange("p (c x) -> p c x", x=1024),
            in_=env.wbb["q"].rearrange("(c p) x -> p c x", p=128)),
            wait=[env.cw("caB"), ("t", P.val("t"))], inc=[("kw0", 16)])
        tile_list = []
        for j in range(NTILES // 2):
            for l in range(B_LOC):
                tile_list.append((j, l))
        fronts = {}

        def emit_d1_front(idx, gate_t=0):
            nonlocal tok_kvld, tok_srow_rd, free_xt, free_kvld
            j, l = tile_list[idx]
            i = l * (NTILES // 2) + j
            t0 = i * TT
            if l == 0:
                tok_kvld = P.op("sync", lambda e, j=j: e.dma_start(
                    out=kv0.rearrange("p (c t) -> p c t", t=TT),
                    in_=env.kbs[:, j * TT:(j + 1) * TT].rearrange("(c p) t -> p c t", p=128)),
                    wait=[("kvs", tok_kvdone), ("t", free_kvld)], inc=[("kv", 16)])
                P.op("sync", lambda e, j=j: e.dma_start(
                    out=kv1.rearrange("p (c t) -> p c t", t=TT),
                    in_=env.kbs[:, T + j * TT: T + (j + 1) * TT]
                    .rearrange("(c p) t -> p c t", p=128)), inc=[("kv", 16)])
                P.op("sync", lambda e, j=j: e.dma_start(
                    out=vv0.rearrange("p (c t) -> p c t", t=TT),
                    in_=env.vbs[:, j * TT:(j + 1) * TT].rearrange("(c p) t -> p c t", p=128)),
                    inc=[("kv", 16)])
                tok_kvld = P.op("sync", lambda e, j=j: e.dma_start(
                    out=vv1.rearrange("p (c t) -> p c t", t=TT),
                    in_=env.vbs[:, T + j * TT: T + (j + 1) * TT]
                    .rearrange("(c p) t -> p c t", p=128)), inc=[("kv", 16)])
            tok_xld = P.op("sync", lambda e, t0=t0: e.dma_start(
                out=xt.rearrange("p (c t) -> p c t", t=TT),
                in_=src[:, t0:t0 + TT].rearrange("(c p) t -> p c t", p=128)),
                wait=src_ready(i) + [("v", free_xt)], inc=[("ax", 16)])
            P.op("sync", lambda e, t0=t0: e.dma_start(
                out=srow[0:1, 0:TT], in_=env.srows[0:1, t0:t0 + TT]),
                wait=stats_done(i) + [("t", tok_srow_rd)], inc=[("sr", 16)])
            tok_sr = P.op("sync", lambda e, t0=t0: e.dma_start(
                out=srow[0:1, TT:2 * TT], in_=env.srows[1:2, t0:t0 + TT]),
                inc=[("sr", 16)])
            P.op("tensor", lambda t: t.matmul(
                ps[0][:, 0:TT], env.ones_row_f[:, :], srow[0:1, 0:TT],
                start=True, stop=True),
                wait=[("sr", tok_sr), (rd_ps[0][0], rd_ps[0][1])])
            tok_bc = P.op("tensor", lambda t: t.matmul(
                ps[1][:, 0:TT], env.ones_row_f[:, :], srow[0:1, TT:2 * TT],
                start=True, stop=True),
                wait=[(rd_ps[1][0], rd_ps[1][1])], inc=[("t", 1)])
            tok_srow_rd = tok_bc
            for c in range(NCH):
                P.op("vector", lambda v, c=c: v.tensor_sub(
                    xo[0][:, :], xt[:, c * TT:(c + 1) * TT], ps[0][:, 0:TT]),
                    wait=([("t", tok_bc), ("ax", tok_xld)] if c == 0 else []))
                P.op("vector", lambda v, c=c: v.tensor_mul(
                    xn[:, c * TT:(c + 1) * TT], xo[0][:, :], ps[1][:, 0:TT]))
            tok_xn = P.op("vector", lambda v: v.engine_nop(), inc=[("v", 1)])
            free_xt = tok_xn
            rd_ps[0] = ("v", tok_xn)
            rd_ps[1] = ("v", tok_xn)
            for mc in range(NCH):
                pbk = 2 + (mc % 2)
                tok_mm = 0
                for c in range(NCH):
                    tok_mm = P.op("tensor", lambda t, mc=mc, c=c, pbk=pbk: t.matmul(
                        ps[pbk][:, 0:TT],
                        warena[:, c * 1024 + mc * 128: c * 1024 + mc * 128 + 128],
                        xn[:, c * TT:(c + 1) * TT],
                        start=(c == 0), stop=(c == NCH - 1)),
                        wait=([("kw0", tok_wq), ("v", tok_xn),
                               (rd_ps[pbk][0], rd_ps[pbk][1])] if c == 0 else []),
                        inc=[("t", 1)] if c == NCH - 1 else [])
                tok_ev = P.op("scalar", lambda sc, mc=mc, pbk=pbk: sc.activation(
                    qb[:, mc * TT:(mc + 1) * TT], ps[pbk][:, 0:TT], AF.Identity,
                    bias=env.bqb_s[:, mc:mc + 1], scale=0.125),
                    wait=[("t", tok_mm)], inc=[("s", 1)])
                rd_ps[pbk] = ("s", tok_ev)
            tok_qb = P.val("s")
            tok_tmp = 0
            for c in range(NCH):
                P.op("vector", lambda v, c=c: v.tensor_mul(
                    tmp[:, c * TT:(c + 1) * TT], qb[:, c * TT:(c + 1) * TT],
                    kv0[:, c * TT:(c + 1) * TT]),
                    wait=([("s", tok_qb), ("kv", tok_kvld), ("t", gate_t)]
                          if c == 0 else []))
                tok_tmp = P.op("vector", lambda v, c=c: v.tensor_mul(
                    tmp2[:, c * TT:(c + 1) * TT],
                    qb[:, c * TT:(c + 1) * TT], kv1[:, c * TT:(c + 1) * TT]),
                    inc=[("v", 1)])
            return i, t0, tok_tmp

        def emit_d1_back(idx, front_info, interleave=None):
            nonlocal tok_ss_prev, tok_o3st_prev, tok_sq_prev, free_rrst, free_kvld
            i, t0, tok_tmp = front_info
            tok_o3 = 0
            rd_pb2 = [0, 0]
            for c in range(NCH):
                sl2 = c % 2
                pbk = 4 + (c % 2)
                tok_l0 = P.op("tensor", lambda t, c=c, pbk=pbk: t.matmul(
                    ps[pbk][:, 0:TT], hblk[:, :], tmp[:, c * TT:(c + 1) * TT],
                    start=True, stop=True),
                    wait=([("v", tok_tmp), ("g", tok_hblk)] if c == 0 else [])
                    + [(rd_ps[pbk][0], rd_ps[pbk][1])], inc=[("t", 1)])
                tok_l1 = P.op("tensor", lambda t, c=c: t.matmul(
                    ps[6][:, 0:TT], hblk[:, :], tmp2[:, c * TT:(c + 1) * TT],
                    start=True, stop=True),
                    wait=[(rd_ps[6][0], rd_ps[6][1])], inc=[("t", 1)])
                tok_r0 = P.op("scalar", lambda sc, pbk=pbk, sl2=sl2: sc.activation(
                    pb2[:, sl2 * 2 * TT: sl2 * 2 * TT + TT], ps[pbk][:, 0:TT], AF.Relu),
                    wait=[("t", tok_l0), ("v", rd_pb2[sl2])], inc=[("s", 1)])
                rd_ps[pbk] = ("s", tok_r0)
                tok_r1 = P.op("scalar", lambda sc, sl2=sl2: sc.activation(
                    pb2[:, sl2 * 2 * TT + TT: sl2 * 2 * TT + 2 * TT], ps[6][:, 0:TT],
                    AF.Relu),
                    wait=[("t", tok_l1)], inc=[("s", 1)])
                rd_ps[6] = ("s", tok_r1)
                P.op("vector", lambda v, c=c, sl2=sl2: v.tensor_mul(
                    t1b[:, :], pb2[:, sl2 * 2 * TT: sl2 * 2 * TT + TT],
                    vv0[:, c * TT:(c + 1) * TT]),
                    wait=[("s", tok_r1)])
                P.op("vector", lambda v, c=c, sl2=sl2: v.tensor_mul(
                    t2b[:, :], pb2[:, sl2 * 2 * TT + TT: sl2 * 2 * TT + 2 * TT],
                    vv1[:, c * TT:(c + 1) * TT]))
                tok_o3 = P.op("vector", lambda v, c=c: v.tensor_add(
                    o3s[:, c * TT:(c + 1) * TT], t1b[:, :], t2b[:, :]),
                    inc=[("v", 1)])
                rd_pb2[sl2] = tok_o3
            free_kvld = P.val("t")
            if interleave is not None:
                interleave(P.val("t"))
            tok_o3st = P.op("gpsimd", lambda e, t0=t0: e.dma_start(
                out=env.o3[:, t0:t0 + TT].rearrange("(c p) t -> p c t", p=128),
                in_=o3s.rearrange("p (c t) -> p c t", t=TT)),
                wait=[("v", tok_o3)], inc=[("o3s", 16)])
            tok_o3st_prev = tok_o3st
            o3_toks[i] = tok_o3st
            tok_sq = tok_sq_prev = P.op("scalar", lambda sc: sc.activation(
                sq[:, :], o3s[:, :], AF.Square),
                wait=[("v", tok_o3)], inc=[("s", 1)])
            tok_ss = 0
            for c in range(NCH):
                tok_ss = P.op("tensor", lambda t, c=c: t.matmul(
                    ps[7][0:1, 0:TT], env.ones_col_b[:, :], sq[:, c * TT:(c + 1) * TT],
                    start=(c == 0), stop=(c == NCH - 1)),
                    wait=[("s", tok_sq), (rd_ps[7][0], rd_ps[7][1])] if c == 0 else [],
                    inc=[("t", 1)] if c == NCH - 1 else [])
            tok_ss_prev = tok_ss
            tv = P.op("vector", lambda v: v.tensor_scalar(
                rrst[0:1, 0:TT], ps[7][0:1, 0:TT], 1e-30, None, ALU.add),
                wait=[("t", tok_ss), ("rrs", free_rrst)], inc=[("v", 1)])
            rd_ps[7] = ("v", tv)
            ts1 = P.op("scalar", lambda sc: sc.activation(
                rrst[0:1, 0:TT], rrst[0:1, 0:TT], AF.Ln),
                wait=[("v", tv)], inc=[("s", 1)])
            ts2 = P.op("scalar", lambda sc: sc.activation(
                rrst[0:1, 0:TT], rrst[0:1, 0:TT], AF.Exp, scale=0.5),
                wait=[("s", ts1)], inc=[("s", 1)])
            tv2 = P.op("vector", lambda v: v.tensor_scalar(
                rrst[0:1, 0:TT], rrst[0:1, 0:TT], 1.0 / 32.0, 1e-8,
                ALU.mult, ALU.add), wait=[("s", ts2)], inc=[("v", 1)])
            ts3 = P.op("scalar", lambda sc: sc.activation(
                rrst[0:1, 0:TT], rrst[0:1, 0:TT], AF.Ln),
                wait=[("v", tv2)], inc=[("s", 1)])
            ts4 = P.op("scalar", lambda sc: sc.activation(
                rrst[0:1, 0:TT], rrst[0:1, 0:TT], AF.Exp, scale=-1.0),
                wait=[("s", ts3)], inc=[("s", 1)])
            free_rrst = P.op("gpsimd", lambda e, t0=t0: e.dma_start(
                out=env.rrow[0:1, t0:t0 + TT], in_=rrst[0:1, 0:TT]),
                wait=[("s", ts4)], inc=[("rrs", 16)])

        fronts[0] = emit_d1_front(0)
        for idx in range(len(tile_list)):
            if idx + 1 < len(tile_list):
                def ilv(gate_t, idx=idx):
                    fronts[idx + 1] = emit_d1_front(idx + 1, gate_t=gate_t)
                emit_d1_back(idx, fronts[idx], interleave=ilv)
            else:
                emit_d1_back(idx, fronts[idx])
        tok_d1_done = [(sn, P.val(sn)) for sn in ("o3s", "rrs")]

        tok_wo = P.op("sync", lambda e: e.dma_start(
            out=warena.rearrange("p (c x) -> p c x", x=1024),
            in_=env.wbb["o"].rearrange("(c p) x -> p c x", p=128)),
            wait=[env.cw("caB"), ("t", P.val("t"))], inc=[("kw0", 16)])
        o3d = [_sb(nc, st, f"ab_o3d{i}", [128, NCH * TT], BF16) for i in range(2)]
        xtd = [_sb(nc, st, f"ab_xtd{i}", [128, NCH * TT]) for i in range(2)]
        qind = [_sb(nc, st, f"ab_qin{i}", [128, NCH * TT], BF16) for i in range(2)]
        free_xo = [0, 0]
        free_xtd = [0, 0]
        free_o3d = [0, 0]
        free_qind = [0, 0]
        tok_rr_rd = 0
        toks_q = {}

        def emit_d2_front(i):
            nonlocal tok_rr_rd
            sl = i % 2
            t0 = i * TT
            tok_xld = P.op("sync", lambda e, t0=t0, sl=sl: e.dma_start(
                out=xtd[sl].rearrange("p (c t) -> p c t", t=TT),
                in_=src[:, t0:t0 + TT].rearrange("(c p) t -> p c t", p=128)),
                wait=[("v", free_xtd[sl])], inc=[("ax", 16)])
            tok_o3ld = P.op("sync", lambda e, t0=t0, sl=sl: e.dma_start(
                out=o3d[sl].rearrange("p (c t) -> p c t", t=TT),
                in_=env.o3[:, t0:t0 + TT].rearrange("(c p) t -> p c t", p=128)),
                wait=tok_d1_done + [("s", free_o3d[sl][0] if False else 0),
                                    ("v", free_o3d[sl])], inc=[("o3l", 16)])
            tok_rr = P.op("sync", lambda e, t0=t0: e.dma_start(
                out=srow[0:1, 0:TT], in_=env.rrow[0:1, t0:t0 + TT]),
                wait=[("t", tok_rr_rd)], inc=[("sr", 16)])
            tok_rb = P.op("tensor", lambda t: t.matmul(
                ps[0][:, 0:TT], env.ones_row_f[:, :], srow[0:1, 0:TT],
                start=True, stop=True),
                wait=[("sr", tok_rr), (rd_ps[0][0], rd_ps[0][1])], inc=[("t", 1)])
            tok_rr_rd = tok_rb
            free_o3d_guard = free_o3d[sl]
            tok_qin = 0
            for c in range(NCH):
                tok_sg = P.op("scalar", lambda sc, c=c, sl=sl: sc.activation(
                    sig[sl][:, :], o3d[sl][:, c * TT:(c + 1) * TT], AF.Sigmoid,
                    scale=env.bcol["ng"][:, c:c + 1]),
                    wait=([("o3l", tok_o3ld), ("v", free_o3d_guard)] if c == 0
                          else [("v", tok_qin)]),
                    inc=[("s", 1)])
                tv1 = P.op("vector", lambda v, c=c, sl=sl: v.scalar_tensor_tensor(
                    t1b[:, :], ps[0][:, 0:TT], 1.0, o3d[sl][:, c * TT:(c + 1) * TT],
                    ALU.mult, ALU.mult),
                    wait=[("t", tok_rb), ("s", tok_sg)]
                    + ([("t", free_qind[sl])] if c == 0 else []), inc=[("v", 1)])
                tok_qin = P.op("vector", lambda v, c=c, sl=sl: v.scalar_tensor_tensor(
                    qind[sl][:, c * TT:(c + 1) * TT], t1b[:, :], env.bcol["nw"][:, c:c + 1],
                    sig[sl][:, :], ALU.mult, ALU.mult),
                    wait=[("v", tv1)], inc=[("v", 1)])
            rd_ps[0] = ("v", tok_qin)
            free_o3d[sl] = tok_qin
            return tok_qin, tok_xld

        tok_front = {0: emit_d2_front(0)}
        for i in range(NTILES):
            sl = i % 2
            t0 = i * TT
            tok_qin, tok_xld = tok_front[i]
            if i + 1 < NTILES:
                pass
            def emit_wo_mm(mc):
                pbk = 2 + (mc % 2)
                tok_mm = 0
                for c in range(NCH):
                    tok_mm = P.op("tensor", lambda t, mc=mc, c=c, pbk=pbk, sl=sl: t.matmul(
                        ps[pbk][:, 0:TT],
                        warena[:, c * 1024 + mc * 128: c * 1024 + mc * 128 + 128],
                        qind[sl][:, c * TT:(c + 1) * TT],
                        start=(c == 0), stop=(c == NCH - 1)),
                        wait=([("kw0", tok_wo), ("v", tok_qin),
                               (rd_ps[pbk][0], rd_ps[pbk][1])] if c == 0 else []),
                        inc=[("t", 1)] if c == NCH - 1 else [])
                return pbk, tok_mm

            def emit_wo_ev(mc, pbk, tok_mm):
                tok_ev = P.op("vector", lambda v, mc=mc, pbk=pbk, sl=sl: v.scalar_tensor_tensor(
                    xo[mc % 2][:, :], ps[pbk][:, 0:TT], env.bcol["bb_o"][:, mc:mc + 1],
                    xtd[sl][:, mc * TT:(mc + 1) * TT], ALU.add, ALU.add),
                    wait=[("t", tok_mm), ("ax", tok_xld),
                          ("xo%d" % (mc % 2), free_xo[mc % 2])], inc=[("v", 1)])
                rd_ps[pbk] = ("v", tok_ev)
                free_xo[mc % 2] = P.op("gpsimd", lambda e, mc=mc, t0=t0: e.dma_start(
                    out=dst[mc * 128:(mc + 1) * 128, t0:t0 + TT], in_=xo[mc % 2][:, :]),
                    wait=[("v", tok_ev)], inc=[("xo%d" % (mc % 2), 16)])

            mm_info = {}
            mm_info[0] = emit_wo_mm(0)
            mm_info[1] = emit_wo_mm(1)
            if i + 1 < NTILES:
                tok_front[i + 1] = emit_d2_front(i + 1)
            for mc in range(2, NCH):
                emit_wo_ev(mc - 2, *mm_info[mc - 2])
                mm_info[mc] = emit_wo_mm(mc)
            emit_wo_ev(NCH - 2, *mm_info[NCH - 2])
            emit_wo_ev(NCH - 1, *mm_info[NCH - 1])
            free_qind[sl] = mm_info[NCH - 1][1]
            free_xtd[sl] = P.val("v")
            store_toks[i] = [("xo0", free_xo[0]), ("xo1", free_xo[1])]
    return store_toks


def build_nc(phases=("full",)):
    nc = bass.Bass("TRN2", target_bir_lowering=False, detect_race_conditions=False)
    P = Prog()
    env = Env()
    stack = contextlib.ExitStack()
    emit_setup(P, nc, stack, env)
    nobar = lambda i: []
    if "full" in phases:
        sw = emit_stats(P, nc, stack, env, env.xT, nobar, tag="0")
        emit_barrier(P)
        emit_attn_a(P, nc, stack, env, env.xT, env.xs, nobar, lambda i: list(sw))
        emit_barrier(P)
        sw = emit_stats(P, nc, stack, env, env.xs, nobar, tag="1")
        emit_barrier(P)
        emit_ffn(P, nc, stack, env, env.xs, env.xs, nobar, lambda i: list(sw), phase=0)
        emit_barrier(P)
        sw = emit_stats(P, nc, stack, env, env.xs, nobar, tag="2")
        emit_barrier(P)
        emit_attn_b(P, nc, stack, env, env.xs, env.xs, nobar, lambda i: list(sw))
        emit_barrier(P)
        sw = emit_stats(P, nc, stack, env, env.xs, nobar, tag="3")
        emit_barrier(P)
        emit_ffn(P, nc, stack, env, env.xs, env.out, nobar, lambda i: list(sw), phase=1)
    else:
        sw = emit_stats(P, nc, stack, env, env.xT, nobar)
        emit_barrier(P)
        if "attn_a" in phases:
            emit_attn_a(P, nc, stack, env, env.xT, env.out, nobar, lambda i: list(sw))
        elif "attn_b" in phases:
            emit_attn_b(P, nc, stack, env, env.xT, env.out, nobar, lambda i: list(sw))
        else:
            emit_ffn(P, nc, stack, env, env.xT, env.out, nobar, lambda i: list(sw), phase=0)
    final = [(sname, P.val(sname)) for sname in P.counts]
    P.op("gpsimd", lambda e: e.nop(), wait=final)
    P.build(nc, stack)
    stack.close()
    return nc


_NC_CACHE = {}


def _get_nc():
    if "nc" not in _NC_CACHE:
        _NC_CACHE["nc"] = build_nc(phases=("full",))
    return _NC_CACHE["nc"]


def kernel(**inputs):
    in_maps = host_layout(inputs)
    nc = _get_nc()
    res = run_bass_kernel_spmd(nc, in_maps, core_ids=list(range(NCORES)))
    return gather_out(res.results)


def host_layout(inputs):
    f = np.ascontiguousarray
    x = np.asarray(inputs["L0_attn_a_x"], np.float32)
    key = np.asarray(inputs["L0_attn_b_key"], np.float32)
    val = np.asarray(inputs["L0_attn_b_value"], np.float32)
    keyT = f(key.reshape(2 * T, D).T)
    valT = f(val.reshape(2 * T, D).T)
    wsgT = f(np.asarray(inputs["L0_ffn_shared_gate_w"], np.float32).T
             .reshape(D, SH // 128, 128).transpose(1, 0, 2))
    wsuT = f(np.asarray(inputs["L0_ffn_shared_up_w"], np.float32).T
             .reshape(D, SH // 128, 128).transpose(1, 0, 2))
    wsdT = f(np.asarray(inputs["L0_ffn_shared_down_w"], np.float32).T
             .reshape(SH, NCH, 128).transpose(1, 0, 2))
    common = {
        "wa_q": f(np.asarray(inputs["L0_attn_a_wq"], np.float32)),
        "wa_k": f(np.asarray(inputs["L0_attn_a_wk"], np.float32)),
        "wa_v": f(np.asarray(inputs["L0_attn_a_wv"], np.float32)),
        "wa_o": f(np.asarray(inputs["L0_attn_a_wo"], np.float32)),
        "ba_q": np.asarray(inputs["L0_attn_a_bq"], np.float32).reshape(1, D),
        "ba_k": np.asarray(inputs["L0_attn_a_bk"], np.float32).reshape(1, D),
        "ba_v": np.asarray(inputs["L0_attn_a_bv"], np.float32).reshape(1, D),
        "ba_o": np.asarray(inputs["L0_attn_a_bo"], np.float32).reshape(1, D),
        "keyT": keyT, "valT": valT,
        "wb_q": f(np.asarray(inputs["L0_attn_b_Wq"], np.float32).T),
        "wb_k": f(np.asarray(inputs["L0_attn_b_Wk"], np.float32).T),
        "wb_v": f(np.asarray(inputs["L0_attn_b_Wv"], np.float32).T),
        "wb_o": f(np.asarray(inputs["L0_attn_b_Wo"], np.float32).T),
        "bb_q": np.asarray(inputs["L0_attn_b_bq"], np.float32).reshape(1, D),
        "bb_k": np.asarray(inputs["L0_attn_b_bk"], np.float32).reshape(1, D),
        "bb_v": np.asarray(inputs["L0_attn_b_bv"], np.float32).reshape(1, D),
        "bb_o": np.asarray(inputs["L0_attn_b_bo"], np.float32).reshape(1, D),
        "nw": np.asarray(inputs["L0_attn_b_norm_weight"], np.float32).reshape(1, D),
        "ng": np.asarray(inputs["L0_attn_b_norm_gate"], np.float32).reshape(1, D),
        "routerT": f(np.asarray(inputs["L0_ffn_router_w"], np.float32).T),
        "cbias": np.asarray(inputs["L0_ffn_correction_bias"], np.float32).reshape(1, E),
        "wsgT": wsgT, "wsuT": wsuT, "wsdT": wsdT,
        "gw": f(np.asarray(inputs["L0_ffn_gate_w"], np.float32)
                .reshape(E, D, 2, 128).transpose(0, 2, 1, 3)),
        "uw": f(np.asarray(inputs["L0_ffn_up_w"], np.float32)
                .reshape(E, D, 2, 128).transpose(0, 2, 1, 3)),
        "dw": f(np.asarray(inputs["L0_ffn_down_w"], np.float32)),
    }
    in_maps = []
    for core in range(NCORES):
        xc = x[core * B_LOC:(core + 1) * B_LOC]
        xTc = f(xc.reshape(NTOK, D).T)
        m = dict(common)
        m["xT"] = xTc
        in_maps.append(m)
    return in_maps


def gather_out(results):
    outs = []
    for core in range(NCORES):
        o = results[core]["out"]
        outs.append(np.ascontiguousarray(o.T).reshape(B_LOC, T, D))
    return np.concatenate(outs, axis=0).astype(np.float32)


# version 09 (final)
# speedup vs baseline: 1.0168x; 1.0168x over previous
import contextlib
import numpy as np

import concourse.bass as bass
import concourse.mybir as mybir
from concourse.bass_utils import run_bass_kernel_spmd

F32 = mybir.dt.float32
BF16 = mybir.dt.bfloat16
AF = mybir.ActivationFunctionType
ALU = mybir.AluOpType
AX = mybir.AxisListType

B_FULL = 16
NCORES = 8
B_LOC = B_FULL // NCORES
T = 2048
D = 1024
NH = 16
HD = 64
NTOK = B_LOC * T
NCH = D // 128
TT = 512
NTILES = NTOK // TT
E = 16
EH = 256
SH = 2048
SOFT_C = 12.0
NEGBIG = -1.0e30

ENGINES = ("sync", "gpsimd", "vector", "scalar", "tensor")


class Prog:

    def __init__(self):
        self.q = {e: [] for e in ENGINES}
        self.counts = {}

    def val(self, s):
        return self.counts.get(s, 0)

    def op(self, eng, fn, wait=(), inc=()):
        wait = [(s, v) for (s, v) in wait if v and v > 0]
        inc = list(inc)
        self.q[eng].append((wait, fn, inc))
        out = []
        for s, d in inc:
            self.counts[s] = self.counts.get(s, 0) + d
            out.append(self.counts[s])
        return out[0] if len(out) == 1 else out

    def build(self, nc, stack):
        sems = {s: stack.enter_context(nc.semaphore(s)) for s in self.counts}

        def mk(ename):
            def emit(eng):
                for wait, fn, inc in self.q[ename]:
                    for s, v in wait:
                        eng.wait_ge(sems[s], v)
                    inst = fn(eng)
                    for s, d in inc:
                        inst.then_inc(sems[s], d)
            return emit

        block = stack.enter_context(nc.Block())
        for ename in ENGINES:
            if self.q[ename]:
                getattr(block, ename)(mk(ename))
        return sems


class Env:
    pass


def _sb(nc, stack, name, shape, dtype=F32):
    return stack.enter_context(nc.sbuf_tensor(name, shape, dtype))


def _psum(nc, stack, name):
    return stack.enter_context(nc.psum_tensor(name, [128, 512], F32))


def strided(ap, extra_off, dims):
    base = ap[:, :] if not isinstance(ap, bass.AP) else ap
    return bass.AP(base.tensor, base.offset + extra_off, [list(base.ap[0])] + dims)



def emit_setup(P, nc, stack, env):
    dram = lambda n, s, d=F32, k="ExternalInput": nc.dram_tensor(n, s, d, kind=k).ap()

    env.xT = dram("xT", [D, NTOK])
    env.wa = {k: dram("wa_" + k, [D, D]) for k in ("q", "k", "v", "o")}
    env.ba = {k: dram("ba_" + k, [1, D]) for k in ("q", "k", "v", "o")}
    env.keyT = dram("keyT", [D, 2 * T])
    env.valT = dram("valT", [D, 2 * T])
    env.wb = {k: dram("wb_" + k, [D, D]) for k in ("q", "k", "v", "o")}
    env.bb = {k: dram("bb_" + k, [1, D]) for k in ("q", "k", "v", "o")}
    env.nw = dram("nw", [1, D])
    env.ng = dram("ng", [1, D])
    env.routerT = dram("routerT", [D, E])
    env.cbias = dram("cbias", [1, E])
    env.wsgT = dram("wsgT", [SH // 128, D, 128])
    env.wsuT = dram("wsuT", [SH // 128, D, 128])
    env.wsdT = dram("wsdT", [NCH, SH, 128])
    env.gw = dram("gw", [E, 2, D, 128])
    env.uw = dram("uw", [E, 2, D, 128])
    env.dw = dram("dw", [E, EH, D])

    dint = lambda n, s, d: nc.dram_tensor(n, s, d, kind="Internal").ap()
    env.wab = {k: dint("wab_" + k, [D, D], BF16) for k in env.wa}
    env.wbb = {k: dint("wbb_" + k, [D, D], BF16) for k in env.wb}
    env.keyTb = dint("keyTb", [D, 2 * T], BF16)
    env.valTb = dint("valTb", [D, 2 * T], BF16)
    env.routerTb = dint("routerTb", [D, E], BF16)
    env.wsgTb = dint("wsgTb", [SH // 128, D, 128], BF16)
    env.wsuTb = dint("wsuTb", [SH // 128, D, 128], BF16)
    env.wsdTb = dint("wsdTb", [NCH, SH, 128], BF16)
    env.gwb = dint("gwb", [E, 2, D, 128], BF16)
    env.uwb = dint("uwb", [E, 2, D, 128], BF16)
    env.dwb = dint("dwb", [E, EH, D], BF16)
    env.qscr = dint("qscr", [D, T], BF16)
    env.kbs = dint("kbs", [D, 2 * T], BF16)
    env.vbs = dint("vbs", [D, 2 * T], BF16)
    env.o3 = dint("o3", [D, NTOK], BF16)
    env.xs = dint("xs", [D, NTOK], F32)
    env.srows = dint("srows", [2, NTOK], F32)
    env.ssrow = dint("ssrow", [1, NTOK], F32)
    env.rrow = dint("rrow", [1, NTOK], F32)
    env.out = dram("out", [D, NTOK], F32, k="ExternalOutput")

    groups = {
        "caA": [("wa_q", env.wa["q"], env.wab["q"]), ("wa_k", env.wa["k"], env.wab["k"]),
                ("wa_v", env.wa["v"], env.wab["v"]), ("wa_o", env.wa["o"], env.wab["o"])],
        "caB": [("wb_q", env.wb["q"], env.wbb["q"]), ("wb_k", env.wb["k"], env.wbb["k"]),
                ("wb_v", env.wb["v"], env.wbb["v"]), ("wb_o", env.wb["o"], env.wbb["o"]),
                ("keyT", env.keyT, env.keyTb), ("valT", env.valT, env.valTb)],
        "caF": [("routerT", env.routerT, env.routerTb),
                ("wsgT", env.wsgT, env.wsgTb), ("wsuT", env.wsuT, env.wsuTb),
                ("wsdT", env.wsdT, env.wsdTb),
                ("gw", env.gw, env.gwb), ("uw", env.uw, env.uwb),
                ("dw", env.dw, env.dwb)],
    }
    env.cast_grp = {}
    for grp, casts in groups.items():
        for name, src, dst in casts:
            if len(src.shape) >= 3:
                for i in range(src.shape[0]):
                    P.op("gpsimd", lambda g, s=src, d=dst, i=i:
                         g.dma_start(out=d[i], in_=s[i]), inc=[(grp, 16)])
            else:
                P.op("gpsimd", lambda g, s=src, d=dst:
                     g.dma_start(out=d[:, :], in_=s[:, :]), inc=[(grp, 16)])
        env.cast_grp[grp] = P.val(grp)
    env.cw = lambda grp: (grp, env.cast_grp[grp])

    env.identf = _sb(nc, stack, "identf", [128, 128])
    env.ones_col_b = _sb(nc, stack, "ones_col_b", [128, 1], BF16)
    env.ones_row_f = _sb(nc, stack, "ones_row_f", [1, 128])
    env.ones_row_b = _sb(nc, stack, "ones_row_b", [1, 128], BF16)
    env.hsum_b = _sb(nc, stack, "hsum_b", [128, 2], BF16)
    env.eps1 = _sb(nc, stack, "eps1", [1, 1])
    env.negC = _sb(nc, stack, "negC", [128, 1])
    env.bcol = {}
    for nm in ("ba_q", "ba_k", "ba_o", "bb_q", "bb_k", "bb_v", "bb_o", "nw", "ng"):
        env.bcol[nm] = _sb(nc, stack, "bcol_" + nm, [128, NCH])
    env.bqb_s = _sb(nc, stack, "bqb_s", [128, NCH])
    env.cbrow = _sb(nc, stack, "cbrow", [1, E])
    env.cb128 = _sb(nc, stack, "cb128", [128, E])

    vv = lambda fn, **kw: P.op("vector", fn, **kw)
    g = lambda fn, **kw: P.op("gpsimd", fn, **kw)
    vv(lambda e: e.memset(env.identf[:, :], 0.0))
    vv(lambda e: e.memset(env.ones_col_b[:, :], 1.0))
    vv(lambda e: e.memset(env.ones_row_f[:, :], 1.0))
    vv(lambda e: e.memset(env.ones_row_b[:, :], 1.0))
    vv(lambda e: e.memset(env.eps1[:, :], 1e-5))
    vv(lambda e: e.memset(env.negC[:, :], -SOFT_C))
    vv(lambda e: e.memset(env.hsum_b[:, 0:1], 0.0))
    tok_ms = vv(lambda e: e.memset(env.hsum_b[:, 1:2], 1.0), inc=[("v", 1)])
    g(lambda e: e.affine_select(out=env.identf[:, :], in_=env.identf[:, :],
                                compare_op=ALU.not_equal, fill=1.0, base=0,
                                pattern=[[-1, 128]], channel_multiplier=1),
      wait=[("v", tok_ms)])
    g(lambda e: e.affine_select(out=env.hsum_b[:, 0:1], in_=env.hsum_b[:, 0:1],
                                compare_op=ALU.is_ge, fill=1.0, base=-64,
                                pattern=[[0, 1]], channel_multiplier=1))
    g(lambda e: e.affine_select(out=env.hsum_b[:, 1:2], in_=env.hsum_b[:, 1:2],
                                compare_op=ALU.is_ge, fill=0.0, base=-64,
                                pattern=[[0, 1]], channel_multiplier=1))
    env.tok_consts = g(lambda e: e.nop(), inc=[("g", 1)])

    for nm, srcap in [("ba_q", env.ba["q"]), ("ba_k", env.ba["k"]), ("ba_o", env.ba["o"]),
                      ("bb_q", env.bb["q"]), ("bb_k", env.bb["k"]), ("bb_v", env.bb["v"]),
                      ("bb_o", env.bb["o"]), ("nw", env.nw), ("ng", env.ng)]:
        dst = env.bcol[nm]
        src = bass.AP(srcap.tensor, 0, [[1, 128], [128, NCH]])
        P.op("sync", lambda e, d=dst, s=src: e.dma_start(
            out=d[:, :], in_=s, allow_slow_non_contiguous=True), inc=[("bia", 16)])
    P.op("sync", lambda e: e.dma_start(out=env.cbrow[:, :], in_=env.cbias[:, :]),
         inc=[("bia", 16)])
    tok_bld = P.val("bia")
    tok_bqs = P.op("vector", lambda v: v.tensor_scalar(
        env.bqb_s[:, :], env.bcol["bb_q"][:, :], 1.0 / np.sqrt(HD), None, ALU.mult),
        wait=[("bia", tok_bld)], inc=[("v", 1)])
    with contextlib.ExitStack() as setup_st:
        pst = setup_st.enter_context(nc.psum_tensor("ps_setup", [128, 512], F32))
        ttok = P.op("tensor", lambda t: t.matmul(
            pst[:, 0:E], env.ones_row_f[:, :], env.cbrow[0:1, :], start=True, stop=True),
            wait=[("bia", tok_bld), ("g", env.tok_consts), ("v", tok_bqs)], inc=[("t", 1)])
        env.tok_bcasts = P.op("vector", lambda v: v.tensor_copy(env.cb128[:, :], pst[:, 0:E]),
                              wait=[("t", ttok)], inc=[("v", 1)])



def emit_stats(P, nc, stack, env, src, src_ready, tag=""):
    with contextlib.ExitStack() as st:
        xt = [_sb(nc, st, f"sa{tag}_x{i}", [128, NCH * TT]) for i in range(2)]
        xb = _sb(nc, st, "sa" + tag + "_xb", [128, NCH * TT], BF16)
        sq = _sb(nc, st, "sa" + tag + "_sq", [128, NCH * TT], BF16)
        row = [_sb(nc, st, f"sa{tag}_row{i}", [1, 2 * TT]) for i in range(2)]
        tmp = _sb(nc, st, "sa" + tag + "_tmp", [1, TT])
        p_mu = _psum(nc, st, "sa" + tag + "_pmu")
        p_is = _psum(nc, st, "sa" + tag + "_pis")
        free_x = [0, 0]
        free_row = [0, 0]
        tok_is_mm_prev = 0
        tok_var_prev = 0
        for i in range(NTILES):
            sl = i % 2
            xsem = "xt%d" % sl
            rsem = "ss%d" % sl
            t0 = i * TT
            tok_ld = P.op("sync", lambda e, sl=sl, t0=t0: e.dma_start(
                out=xt[sl].rearrange("p (c t) -> p c t", t=TT),
                in_=src[:, t0:t0 + TT].rearrange("(c p) t -> p c t", p=128)),
                wait=src_ready(i) + [("v", free_x[sl])], inc=[(xsem, 16)])
            tok_cast = P.op("vector", lambda v, sl=sl: v.tensor_copy(xb[:, :], xt[sl][:, :]),
                            wait=[(xsem, tok_ld), ("t", tok_is_mm_prev)], inc=[("v", 1)])
            free_x[sl] = tok_cast
            tok_sq = P.op("scalar", lambda s: s.activation(sq[:, :], xb[:, :], AF.Square),
                          wait=[("v", tok_cast)], inc=[("s", 1)])
            tok_mu_mm = 0
            for c in range(NCH):
                tok_mu_mm = P.op("tensor", lambda t, c=c: t.matmul(
                    p_mu[0:1, 0:TT], env.ones_col_b[:, :], xb[:, c * TT:(c + 1) * TT],
                    start=(c == 0), stop=(c == NCH - 1)),
                    wait=[("v", tok_cast), ("v", tok_var_prev)] if c == 0 else [],
                    inc=[("t", 1)] if c == NCH - 1 else [])
            tok_is_mm = 0
            for c in range(NCH):
                tok_is_mm = P.op("tensor", lambda t, c=c: t.matmul(
                    p_is[0:1, 0:TT], env.ones_col_b[:, :], sq[:, c * TT:(c + 1) * TT],
                    start=(c == 0), stop=(c == NCH - 1)),
                    wait=[("s", tok_sq)] if c == 0 else [],
                    inc=[("t", 1)] if c == NCH - 1 else [])
            tok_is_mm_prev = tok_is_mm
            tv = P.op("vector", lambda v, sl=sl: v.tensor_scalar(
                row[sl][0:1, 0:TT], p_mu[0:1, 0:TT], 1.0 / D, None, ALU.mult),
                wait=[("t", tok_mu_mm), (rsem, free_row[sl])], inc=[("v", 1)])
            tv = P.op("vector", lambda v, sl=sl: v.tensor_mul(
                tmp[0:1, 0:TT], row[sl][0:1, 0:TT], row[sl][0:1, 0:TT]),
                wait=[("v", tv)], inc=[("v", 1)])
            tok_var = tok_var_prev = P.op("vector", lambda v, sl=sl: v.scalar_tensor_tensor(
                row[sl][0:1, TT:2 * TT], p_is[0:1, 0:TT], 1.0 / D, tmp[0:1, 0:TT],
                ALU.mult, ALU.subtract), wait=[("t", tok_is_mm), ("v", tv)], inc=[("v", 1)])
            ts_ = P.op("scalar", lambda s, sl=sl: s.activation(
                row[sl][0:1, TT:2 * TT], row[sl][0:1, TT:2 * TT], AF.Ln,
                bias=env.eps1[0:1, 0:1]), wait=[("v", tok_var)], inc=[("s", 1)])
            tok_inv = P.op("scalar", lambda s, sl=sl: s.activation(
                row[sl][0:1, TT:2 * TT], row[sl][0:1, TT:2 * TT], AF.Exp, scale=-0.5),
                wait=[("s", ts_)], inc=[("s", 1)])
            P.op("gpsimd", lambda e, sl=sl, t0=t0: e.dma_start(
                out=env.srows[0:1, t0:t0 + TT], in_=row[sl][0:1, 0:TT]),
                wait=[("s", tok_inv)], inc=[(rsem, 16)])
            free_row[sl] = P.op("gpsimd", lambda e, sl=sl, t0=t0: e.dma_start(
                out=env.srows[1:2, t0:t0 + TT], in_=row[sl][0:1, TT:2 * TT]),
                inc=[(rsem, 16)])
    return [("ss0", P.val("ss0")), ("ss1", P.val("ss1"))]



def emit_ffn(P, nc, stack, env, src, dst, src_ready, stats_done, phase, free_dws=0):
    store_toks = {}
    with contextlib.ExitStack() as st:
        xt = [_sb(nc, st, f"f{phase}_x{i}", [128, NCH * TT]) for i in range(2)]
        xn = _sb(nc, st, f"f{phase}_xn", [128, NCH * TT], BF16)
        hsh = _sb(nc, st, f"f{phase}_hsh", [128, 16 * TT], BF16)
        hall = _sb(nc, st, f"f{phase}_hall", [128, 32 * TT], BF16)
        dws = _sb(nc, st, f"f{phase}_dws", [128, E * 2048], BF16)
        geb = [_sb(nc, st, f"f{phase}_ge{i}", [128, 2048], BF16) for i in range(2)]
        ueb = [_sb(nc, st, f"f{phase}_ue{i}", [128, 2048], BF16) for i in range(2)]
        wsg = [_sb(nc, st, f"f{phase}_wsg{i}", [128, D], BF16) for i in range(2)]
        wsu = [_sb(nc, st, f"f{phase}_wsu{i}", [128, D], BF16) for i in range(2)]
        wsd = [_sb(nc, st, f"f{phase}_wsd{i}", [128, SH], BF16) for i in range(2)]
        srow = [_sb(nc, st, f"f{phase}_sr0", [1, 2 * TT])]
        tmpf = _sb(nc, st, f"f{phase}_tmpf", [128, TT])
        slact = _sb(nc, st, f"f{phase}_slact", [128, TT], BF16)
        sgact = slact
        thh = _sb(nc, st, f"f{phase}_thh", [128, TT], BF16)
        cbsb = _sb(nc, st, f"f{phase}_cbsb", [128, TT], BF16)
        xout = [_sb(nc, st, f"f{phase}_xo{i}", [128, TT]) for i in range(2)]
        rtball = _sb(nc, st, f"f{phase}_rtball", [128, NCH * E], BF16)
        rlg = _sb(nc, st, f"f{phase}_rlg", [128, 4 * E])
        rsc = _sb(nc, st, f"f{phase}_rsc", [128, 4 * E])
        rsb = _sb(nc, st, f"f{phase}_rsb", [128, 4 * E])
        rsbm = _sb(nc, st, f"f{phase}_rsbm", [128, 4 * E])
        gm16 = _sb(nc, st, f"f{phase}_gm16", [128, 4 * E])
        rw = rlg
        combine = _sb(nc, st, f"f{phase}_comb", [128, 4 * E])
        rgm = _sb(nc, st, f"f{phase}_rgm", [128, 16])
        rt1 = _sb(nc, st, f"f{phase}_rt1", [128, 16])
        rt2 = _sb(nc, st, f"f{phase}_rt2", [128, 16])
        rt3 = _sb(nc, st, f"f{phase}_rt3", [128, 16])
        rt4 = _sb(nc, st, f"f{phase}_rt4", [128, 16])
        g4a = _sb(nc, st, f"f{phase}_g4a", [128, 4])
        g4b = _sb(nc, st, f"f{phase}_g4b", [128, 4])
        g4c = _sb(nc, st, f"f{phase}_g4c", [128, 4])
        g4d = _sb(nc, st, f"f{phase}_g4d", [128, 4])
        gthr = _sb(nc, st, f"f{phase}_gthr", [128, 4])
        thr = _sb(nc, st, f"f{phase}_thr", [128, 4])
        msk = _sb(nc, st, f"f{phase}_msk", [128, E])
        dnm = _sb(nc, st, f"f{phase}_dnm", [128, 4])
        cT = _sb(nc, st, f"f{phase}_cT", [16, TT], BF16)
        selcb = _sb(nc, st, f"f{phase}_selcb", [16, E * 128], BF16)

        p_g = _psum(nc, st, f"f{phase}_pg")
        p_u = _psum(nc, st, f"f{phase}_pu")
        p_cb = _psum(nc, st, f"f{phase}_pcb")
        p_mu = _psum(nc, st, f"f{phase}_pmu")
        p_is = _psum(nc, st, f"f{phase}_pis")
        p_d0 = _psum(nc, st, f"f{phase}_pd0")
        p_d1 = _psum(nc, st, f"f{phase}_pd1")
        p_r = _psum(nc, st, f"f{phase}_pr")

        tok_sel_ms = P.op("vector", lambda e: e.memset(selcb[:, :], 1.0),
                          inc=[("v", 1)])
        for e_i in range(E):
            P.op("gpsimd", lambda e, e_i=e_i: e.affine_select(
                out=selcb[:, e_i * 128:(e_i + 1) * 128],
                in_=selcb[:, e_i * 128:(e_i + 1) * 128],
                compare_op=ALU.is_equal, fill=0.0, base=-e_i,
                pattern=[[0, 128]], channel_multiplier=1),
                wait=[("v", tok_sel_ms)] if e_i == 0 else [])
        tok_selcb = P.op("gpsimd", lambda e: e.nop(), inc=[("g", 1)])
        tok_rt = P.op("sync", lambda e: e.dma_start(
            out=rtball.rearrange("p (c e) -> p c e", e=E),
            in_=env.routerTb[:, :].rearrange("(c p) e -> p c e", p=128)),
            wait=[env.cw("caF")], inc=[("rt", 16)])
        for e_i in range(E):
            P.op("sync", lambda e, e_i=e_i: e.dma_start(
                out=dws[:, e_i * 2048:(e_i + 1) * 2048].rearrange("p (k c) -> p k c", k=2),
                in_=env.dwb[e_i].rearrange("(k p) c -> p k c", p=128)),
                wait=[env.cw("caF"), ("t", free_dws)], inc=[("dwl", 16)])
        tok_dwl = P.val("dwl")

        free_x = [0, 0]
        free_sr = [0]
        free_ge = [0, 0]; free_ue = [0, 0]
        free_wsg = [0, 0]; free_wsu = [0, 0]; free_wsd = [0, 0]
        free_xo = [0, 0]
        c_rows = lambda mc: slice(mc * 128, (mc + 1) * 128)
        bz = env.tok_bcasts
        rd_pg = ("v", bz); rd_pu = ("v", bz); rd_pcb = ("v", bz)
        rd_pmuis = ("v", bz); rd_pr = ("v", bz)
        rd_pd = {0: ("v", bz), 1: ("v", bz)}
        tok_store_prev = 0
        tok_tensor_tile_prev = 0

        for i in range(NTILES):
            sl = i % 2
            rl = 0
            t0 = i * TT
            xsem = "xt%d" % sl
            tok_xld = P.op("sync", lambda e, sl=sl, t0=t0: e.dma_start(
                out=xt[sl].rearrange("p (c t) -> p c t", t=TT),
                in_=src[:, t0:t0 + TT].rearrange("(c p) t -> p c t", p=128)),
                wait=src_ready(i) + [("v", free_x[sl])], inc=[(xsem, 16)])
            P.op("sync", lambda e, rl=rl, t0=t0: e.dma_start(
                out=srow[rl][0:1, 0:TT], in_=env.srows[0:1, t0:t0 + TT]),
                wait=stats_done(i) + [("t", free_sr[rl])], inc=[("sr", 16)])
            tok_srld = P.op("sync", lambda e, rl=rl, t0=t0: e.dma_start(
                out=srow[rl][0:1, TT:2 * TT], in_=env.srows[1:2, t0:t0 + TT]),
                inc=[("sr", 16)])
            P.op("tensor", lambda t, rl=rl: t.matmul(
                p_mu[:, 0:TT], env.ones_row_f[:, :], srow[rl][0:1, 0:TT],
                start=True, stop=True),
                wait=[("sr", tok_srld), (rd_pmuis[0], rd_pmuis[1])])
            tok_bc = P.op("tensor", lambda t, rl=rl: t.matmul(
                p_is[:, 0:TT], env.ones_row_f[:, :], srow[rl][0:1, TT:2 * TT],
                start=True, stop=True), inc=[("t", 1)])
            free_sr[rl] = tok_bc
            for c in range(NCH):
                P.op("vector", lambda v, sl=sl, c=c: v.tensor_sub(
                    tmpf[:, :], xt[sl][:, c * TT:(c + 1) * TT], p_mu[:, 0:TT]),
                    wait=([("t", tok_bc), (xsem, tok_xld), ("t", tok_tensor_tile_prev)]
                          if c == 0 else []))
                P.op("vector", lambda v, c=c: v.tensor_mul(
                    xn[:, c * TT:(c + 1) * TT], tmpf[:, :], p_is[:, 0:TT]))
            tok_xn = P.op("vector", lambda v: v.engine_nop(), inc=[("v", 1)])
            rd_pmuis = ("v", tok_xn)

            tok_rl = 0
            for gtok in range(4):
                for c in range(NCH):
                    tok_rl = P.op("tensor", lambda t, gtok=gtok, c=c: t.matmul(
                        p_r[:, gtok * E:(gtok + 1) * E],
                        xn[:, c * TT + gtok * 128: c * TT + gtok * 128 + 128],
                        rtball[:, c * E:(c + 1) * E],
                        start=(c == 0), stop=(c == NCH - 1)),
                        wait=([("v", tok_xn), ("rt", tok_rt), (rd_pr[0], rd_pr[1])]
                              if (gtok == 0 and c == 0) else []),
                        inc=[("t", 1)] if (gtok == 3 and c == NCH - 1) else [])
            tok_tanh = P.op("scalar", lambda s: s.activation(
                rlg[:, :], p_r[:, 0:4 * E], AF.Tanh, scale=0.5),
                wait=[("t", tok_rl)], inc=[("s", 1)])
            rd_pr = ("s", tok_tanh)
            _lastv = [P.val("v")]
            def vv(fn, wait=(), **kw):
                tok = P.op("vector", fn, wait=list(wait) + [("v", _lastv[0])],
                           inc=[("v", 1)])
                _lastv[0] = tok
                return tok
            vv(lambda v: v.tensor_scalar(rsc[:, :], rlg[:, :], 0.5, 0.5, ALU.mult, ALU.add),
               wait=[("s", tok_tanh)])
            for gtok in range(4):
                vv(lambda v, gtok=gtok: v.tensor_add(
                    rsb[:, gtok * E:(gtok + 1) * E], rsc[:, gtok * E:(gtok + 1) * E],
                    env.cb128[:, :]))
            s3 = lambda ap, j: strided(ap[:, :], j, [[4, 16]])
            s4 = lambda ap, j: strided(ap[:, :], j, [[4, 4]])
            vv(lambda v: v.tensor_tensor(rt1[:, :], s3(rsb, 0), s3(rsb, 1), ALU.max))
            vv(lambda v: v.tensor_tensor(rt2[:, :], s3(rsb, 0), s3(rsb, 1), ALU.min))
            vv(lambda v: v.tensor_tensor(rt3[:, :], s3(rsb, 2), s3(rsb, 3), ALU.max))
            vv(lambda v: v.tensor_tensor(rt4[:, :], s3(rsb, 2), s3(rsb, 3), ALU.min))
            vv(lambda v: v.tensor_tensor(rgm[:, :], rt1[:, :], rt3[:, :], ALU.min))
            vv(lambda v: v.tensor_tensor(rt2[:, :], rt2[:, :], rt4[:, :], ALU.max))
            vv(lambda v: v.tensor_tensor(rgm[:, :], rgm[:, :], rt2[:, :], ALU.max))
            vv(lambda v: v.tensor_tensor(rt1[:, :], rt1[:, :], rt3[:, :], ALU.max))
            vv(lambda v: v.tensor_add(rgm[:, :], rt1[:, :], rgm[:, :]))
            vv(lambda v: v.tensor_tensor(g4a[:, :], s4(rgm, 0), s4(rgm, 1), ALU.max))
            vv(lambda v: v.tensor_tensor(g4b[:, :], s4(rgm, 0), s4(rgm, 1), ALU.min))
            vv(lambda v: v.tensor_tensor(g4c[:, :], s4(rgm, 2), s4(rgm, 3), ALU.max))
            vv(lambda v: v.tensor_tensor(g4d[:, :], s4(rgm, 2), s4(rgm, 3), ALU.min))
            vv(lambda v: v.tensor_tensor(gthr[:, :], g4a[:, :], g4c[:, :], ALU.min))
            vv(lambda v: v.tensor_tensor(g4b[:, :], g4b[:, :], g4d[:, :], ALU.max))
            vv(lambda v: v.tensor_tensor(gthr[:, :], gthr[:, :], g4b[:, :], ALU.max))
            for b in range(4):
                for j in range(4):
                    dstv = strided(gm16[:, :], b * 16 + j, [[4, 4]])
                    vv(lambda v, b=b, dstv=dstv: v.tensor_scalar(
                        dstv, rgm[:, b * 4:(b + 1) * 4], gthr[:, b:b + 1], None, ALU.is_ge))
            vv(lambda v: v.tensor_mul(rsbm[:, :], rsb[:, :], gm16[:, :]))
            vv(lambda v: v.tensor_scalar(gm16[:, :], gm16[:, :], 1.0e30, -1.0e30,
                                         ALU.mult, ALU.add))
            vv(lambda v: v.tensor_add(rsbm[:, :], rsbm[:, :], gm16[:, :]))
            vv(lambda v: v.tensor_copy(rw[:, :], rsbm[:, :]))
            for it in range(4):
                vv(lambda v: v.reduce_max(
                    thr.rearrange("p (b one) -> p b one", one=1),
                    rw.rearrange("p (b e) -> p b e", e=E), axis=AX.X))
                if it < 3:
                    for b in range(4):
                        vv(lambda v, b=b: v.tensor_scalar(
                            msk[:, :], rw[:, b * E:(b + 1) * E], thr[:, b:b + 1],
                            None, ALU.is_ge))
                        vv(lambda v, b=b: v.scalar_tensor_tensor(
                            rw[:, b * E:(b + 1) * E], msk[:, :], NEGBIG,
                            rw[:, b * E:(b + 1) * E], ALU.mult, ALU.add))
            for b in range(4):
                vv(lambda v, b=b: v.tensor_scalar(
                    combine[:, b * E:(b + 1) * E], rsbm[:, b * E:(b + 1) * E],
                    thr[:, b:b + 1], None, ALU.is_ge))
            vv(lambda v: v.tensor_mul(combine[:, :], combine[:, :], rsc[:, :]))
            vv(lambda v: v.reduce_sum(
                dnm.rearrange("p (b one) -> p b one", one=1),
                combine.rearrange("p (b e) -> p b e", e=E), axis=AX.X))
            vv(lambda v: v.tensor_scalar(dnm[:, :], dnm[:, :], 1e-20, None, ALU.add))
            vv(lambda v: v.reciprocal(dnm[:, :], dnm[:, :]))
            for b in range(4):
                vv(lambda v, b=b: v.tensor_scalar(
                    combine[:, b * E:(b + 1) * E], combine[:, b * E:(b + 1) * E],
                    dnm[:, b:b + 1], None, ALU.mult))
            tok_comb = P.op("vector", lambda v: v.engine_nop(), inc=[("v", 1)])
            for b in range(4):
                ttok = P.op("tensor", lambda t, b=b: t.transpose(
                    p_cb[0:E, 0:128], combine[:, b * E:(b + 1) * E], env.identf[:, :]),
                    wait=[("v", tok_comb), (rd_pcb[0], rd_pcb[1])] if b == 0
                    else [(rd_pcb[0], rd_pcb[1])], inc=[("t", 1)])
                stok = P.op("scalar", lambda s, b=b: s.copy(
                    cT[:, b * 128:(b + 1) * 128], p_cb[0:E, 0:128]),
                    wait=[("t", ttok)], inc=[("s", 1)])
                rd_pcb = ("s", stok)
            tok_cT = P.val("s")
            tok_hsh_parts = 0
            for mc in range(SH // 128):
                wslot = mc % 2
                tok_wg = P.op("sync", lambda e, mc=mc, wslot=wslot: e.dma_start(
                    out=wsg[wslot].rearrange("p (c x) -> p c x", x=128),
                    in_=env.wsgTb[mc].rearrange("(c p) x -> p c x", p=128)),
                    wait=[env.cw("caF"), ("t", free_wsg[wslot])],
                    inc=[("wg%d" % wslot, 16)])
                tok_wu = P.op("sync", lambda e, mc=mc, wslot=wslot: e.dma_start(
                    out=wsu[wslot].rearrange("p (c x) -> p c x", x=128),
                    in_=env.wsuTb[mc].rearrange("(c p) x -> p c x", p=128)),
                    wait=[env.cw("caF"), ("t", free_wsu[wslot])],
                    inc=[("wu%d" % wslot, 16)])
                tok_gmm = 0
                for c in range(NCH):
                    tok_gmm = P.op("tensor", lambda t, c=c, wslot=wslot: t.matmul(
                        p_g[:, 0:TT], wsg[wslot][:, c * 128:(c + 1) * 128],
                        xn[:, c * TT:(c + 1) * TT],
                        start=(c == 0), stop=(c == NCH - 1)),
                        wait=([("wg%d" % wslot, tok_wg), ("v", tok_xn),
                               (rd_pg[0], rd_pg[1])] if c == 0 else []),
                        inc=[("t", 1)] if c == NCH - 1 else [])
                free_wsg[wslot] = tok_gmm
                tok_sg = P.op("scalar", lambda s: s.activation(
                    sgact[:, :], p_g[:, 0:TT], AF.Silu),
                    wait=[("t", tok_gmm), ("v", tok_hsh_parts)], inc=[("s", 1)])
                rd_pg = ("s", tok_sg)
                tok_umm = 0
                for c in range(NCH):
                    tok_umm = P.op("tensor", lambda t, c=c, wslot=wslot: t.matmul(
                        p_u[:, 0:TT], wsu[wslot][:, c * 128:(c + 1) * 128],
                        xn[:, c * TT:(c + 1) * TT],
                        start=(c == 0), stop=(c == NCH - 1)),
                        wait=([("wu%d" % wslot, tok_wu), (rd_pu[0], rd_pu[1])]
                              if c == 0 else []),
                        inc=[("t", 1)] if c == NCH - 1 else [])
                free_wsu[wslot] = tok_umm
                tok_hsh_parts = P.op("vector", lambda v, mc=mc: v.scalar_tensor_tensor(
                    hsh[:, mc * TT:(mc + 1) * TT], p_u[:, 0:TT], 1.0, sgact[:, :],
                    ALU.mult, ALU.mult),
                    wait=[("t", tok_umm), ("s", tok_sg)], inc=[("v", 1)])
                rd_pu = ("v", tok_hsh_parts)
            tok_hsh = tok_hsh_parts

            tok_h_parts = 0
            for e_i in range(E):
                wslot = e_i % 2
                tok_ge = P.op("sync", lambda e, e_i=e_i, wslot=wslot: e.dma_start(
                    out=geb[wslot].rearrange("p (h c x) -> p h c x", h=2, x=128),
                    in_=env.gwb[e_i].rearrange("h (c p) x -> p h c x", p=128)),
                    wait=[env.cw("caF"), ("t", free_ge[wslot])],
                    inc=[("ge%d" % wslot, 16)])
                tok_ue = P.op("sync", lambda e, e_i=e_i, wslot=wslot: e.dma_start(
                    out=ueb[wslot].rearrange("p (h c x) -> p h c x", h=2, x=128),
                    in_=env.uwb[e_i].rearrange("h (c p) x -> p h c x", p=128)),
                    wait=[env.cw("caF"), ("t", free_ue[wslot])],
                    inc=[("ue%d" % wslot, 16)])
                tok_cbmm = P.op("tensor", lambda t, e_i=e_i: t.matmul(
                    p_cb[:, 0:TT], selcb[:, e_i * 128:(e_i + 1) * 128], cT[0:16, 0:TT],
                    start=True, stop=True),
                    wait=[("s", tok_cT), ("g", tok_selcb), (rd_pcb[0], rd_pcb[1])],
                    inc=[("t", 1)])
                tok_cbev = P.op("scalar", lambda s: s.copy(cbsb[:, :], p_cb[:, 0:TT]),
                                wait=[("t", tok_cbmm), ("v", tok_h_parts)], inc=[("s", 1)])
                rd_pcb = ("s", tok_cbev)
                for hc in range(2):
                    tok_gmm = 0
                    for c in range(NCH):
                        tok_gmm = P.op("tensor", lambda t, c=c, hc=hc, wslot=wslot: t.matmul(
                            p_g[:, 0:TT],
                            geb[wslot][:, hc * 1024 + c * 128: hc * 1024 + c * 128 + 128],
                            xn[:, c * TT:(c + 1) * TT],
                            start=(c == 0), stop=(c == NCH - 1)),
                            wait=([("ge%d" % wslot, tok_ge), (rd_pg[0], rd_pg[1])]
                                  if c == 0 else []),
                            inc=[("t", 1)] if c == NCH - 1 else [])
                    if hc == 1:
                        free_ge[wslot] = tok_gmm
                    tok_sl = P.op("scalar", lambda s, hc=hc: s.activation(
                        slact[:, :], p_g[:, 0:TT], AF.Silu),
                        wait=[("t", tok_gmm), ("v", tok_h_parts)], inc=[("s", 1)])
                    rd_pg = ("s", tok_sl)
                    tok_umm = 0
                    for c in range(NCH):
                        tok_umm = P.op("tensor", lambda t, c=c, hc=hc, wslot=wslot: t.matmul(
                            p_u[:, 0:TT],
                            ueb[wslot][:, hc * 1024 + c * 128: hc * 1024 + c * 128 + 128],
                            xn[:, c * TT:(c + 1) * TT],
                            start=(c == 0), stop=(c == NCH - 1)),
                            wait=([("ue%d" % wslot, tok_ue), (rd_pu[0], rd_pu[1])]
                                  if c == 0 else []),
                            inc=[("t", 1)] if c == NCH - 1 else [])
                    if hc == 1:
                        free_ue[wslot] = tok_umm
                    tok_last_xn_rd = tok_umm
                    P.op("vector", lambda v, hc=hc: v.scalar_tensor_tensor(
                        thh[:, :], p_u[:, 0:TT], 1.0, slact[:, :],
                        ALU.mult, ALU.mult),
                        wait=[("t", tok_umm), ("s", max(tok_sl, tok_cbev))])
                    tok_h_parts = P.op("vector", lambda v, e_i=e_i, hc=hc: v.tensor_mul(
                        hall[:, (e_i * 2 + hc) * TT:(e_i * 2 + hc + 1) * TT],
                        thh[:, :], cbsb[:, :]), inc=[("v", 1)])
                    rd_pu = ("v", tok_h_parts)
            tok_hall = tok_h_parts

            tok_last_dmm = 0
            for mc in range(NCH):
                dslot = mc % 2
                pd = p_d0 if dslot == 0 else p_d1
                tok_wd = P.op("sync", lambda e, mc=mc, dslot=dslot: e.dma_start(
                    out=wsd[dslot].rearrange("p (c x) -> p c x", x=128),
                    in_=env.wsdTb[mc].rearrange("(c p) x -> p c x", p=128)),
                    wait=[env.cw("caF"), ("t", free_wsd[dslot])],
                    inc=[("wd%d" % dslot, 16)])
                for kc in range(SH // 128):
                    P.op("tensor", lambda t, kc=kc, dslot=dslot, pd=pd: t.matmul(
                        pd[:, 0:TT], wsd[dslot][:, kc * 128:(kc + 1) * 128],
                        hsh[:, kc * TT:(kc + 1) * TT],
                        start=(kc == 0), stop=False),
                        wait=([("wd%d" % dslot, tok_wd), ("v", tok_hsh),
                               (rd_pd[dslot][0], rd_pd[dslot][1])] if kc == 0 else []))
                tok_dmm = 0
                for e_i in range(E):
                    for kc in range(2):
                        last = (e_i == E - 1 and kc == 1)
                        tok_dmm = P.op("tensor", lambda t, e_i=e_i, kc=kc, mc=mc, last=last,
                                       pd=pd: t.matmul(
                                           pd[:, 0:TT],
                                           dws[:, e_i * 2048 + kc * 1024 + mc * 128:
                                                e_i * 2048 + kc * 1024 + mc * 128 + 128],
                                           hall[:, (e_i * 2 + kc) * TT:(e_i * 2 + kc + 1) * TT],
                                           start=False, stop=last),
                                       wait=([("v", tok_hall), ("dwl", tok_dwl)]
                                             if (e_i == 0 and kc == 0 and mc == 0) else []),
                                       inc=[("t", 1)] if last else [])
                free_wsd[dslot] = tok_dmm
                tok_last_dmm = tok_dmm
                tok_ev = P.op("vector", lambda v, mc=mc, sl=sl, pd=pd: v.scalar_tensor_tensor(
                    xout[mc % 2][:, :], pd[:, 0:TT], 1.0,
                    xt[sl][:, mc * TT:(mc + 1) * TT], ALU.mult, ALU.add),
                    wait=[("t", tok_dmm), ("xo%d" % (mc % 2), free_xo[mc % 2])],
                    inc=[("v", 1)])
                rd_pd[dslot] = ("v", tok_ev)
                free_xo[mc % 2] = P.op("gpsimd", lambda e, mc=mc, t0=t0: e.dma_start(
                    out=dst[c_rows(mc), t0:t0 + TT], in_=xout[mc % 2][:, :]),
                    wait=[("v", tok_ev)], inc=[("xo%d" % (mc % 2), 16)])
            free_x[sl] = P.val("v")
            tok_tensor_tile_prev = tok_last_dmm
            store_toks[i] = [("xo0", free_xo[0]), ("xo1", free_xo[1])]
    return store_toks, P.val("t")


def emit_barrier(P):
    snap = [(s, P.val(s)) for s in P.counts]
    for e in ENGINES:
        P.op(e, lambda x: x.nop(), wait=snap)





def emit_attn_a(P, nc, stack, env, src, dst, src_ready, stats_done):
    store_toks = {}
    NKC = T // 128
    NQG = T // TT
    with contextlib.ExitStack() as st:
        masksM = _sb(nc, st, "aa_masksM", [128, 4 * TT], BF16)
        selz = _sb(nc, st, "aa_selz", [128, NCH * 128])
        kts = _sb(nc, st, "aa_kts", [128, NCH * T], BF16)
        vts = _sb(nc, st, "aa_vts", [128, NKC * 16 * 80], BF16)
        warena = [_sb(nc, st, f"aa_w{i}", [128, NCH * 1024], BF16) for i in range(3)]
        xt = _sb(nc, st, "aa_xt", [128, NCH * TT])
        xn = _sb(nc, st, "aa_xn", [128, NCH * TT], BF16)
        qstage = _sb(nc, st, "aa_qst", [128, TT], BF16)
        qsl = _sb(nc, st, "aa_qsl", [128, NCH * TT], BF16)
        pt = [_sb(nc, st, f"aa_pt{i}", [128, 2 * TT], BF16) for i in range(2)]
        oraw = _sb(nc, st, "aa_oraw", [128, NCH * TT], BF16)
        onrm = _sb(nc, st, "aa_onrm", [128, NCH * TT], BF16)
        zbuf = _sb(nc, st, "aa_zbuf", [128, TT])
        srow = _sb(nc, st, "aa_srow", [1, 2 * TT])
        xo = [_sb(nc, st, f"aa_xo{i}", [128, TT]) for i in range(2)]
        bva_row = _sb(nc, st, "aa_bvar", [1, D])
        bva128 = _sb(nc, st, "aa_bva", [128, D])
        pd = [st.enter_context(nc.psum_tensor(f"aa_pd{i}", [128, 1024], F32))
              for i in range(2)]
        ps4 = [_psum(nc, st, f"aa_p{i}") for i in range(4)]
        ps = [pd[0][:, 0:512], pd[0][:, 512:1024],
              pd[1][:, 0:512], pd[1][:, 512:1024],
              ps4[0][:, :], ps4[1][:, :], ps4[2][:, :], ps4[3][:, :]]

        tok_bv = P.op("sync", lambda e: e.dma_start(out=bva_row[:, :], in_=env.ba["v"][:, :]),
                      inc=[("bv", 16)])
        tok_bvv = env.tok_bcasts
        for half in range(2):
            tb = P.op("tensor", lambda t, half=half: t.matmul(
                ps[7][:, 0:TT], env.ones_row_f[:, :],
                bva_row[0:1, half * TT:(half + 1) * TT], start=True, stop=True),
                wait=[("bv", tok_bv), ("g", env.tok_consts), ("v", tok_bvv)],
                inc=[("t", 1)])
            tok_bvv = P.op("vector", lambda v, half=half: v.tensor_copy(
                bva128[:, half * TT:(half + 1) * TT], ps[7][:, 0:TT]),
                wait=[("t", tb)], inc=[("v", 1)])
        rd_p7_init = ("v", tok_bvv)

        tok_ms = P.op("vector", lambda v: v.memset(masksM[:, :], 0.0), inc=[("v", 1)])
        for j in range(4):
            off = j * 128
            P.op("gpsimd", lambda e, j=j, off=off: e.affine_select(
                out=masksM[:, j * TT:(j + 1) * TT], in_=masksM[:, j * TT:(j + 1) * TT],
                compare_op=ALU.is_ge, fill=1.0, base=off - 1,
                pattern=[[-1, TT]], channel_multiplier=1),
                wait=[("v", tok_ms)] if j == 0 else [])
        tok_sz = P.op("vector", lambda v: v.memset(selz[:, :], 1.0), inc=[("v", 1)])
        first_sel = True
        for c in range(NCH):
            for half in range(2):
                h = 2 * c + half
                P.op("gpsimd", lambda e, c=c, half=half, h=h: e.affine_select(
                    out=selz[:, c * 128 + half * 64: c * 128 + half * 64 + 64],
                    in_=selz[:, c * 128 + half * 64: c * 128 + half * 64 + 64],
                    compare_op=ALU.is_equal, fill=0.0, base=-(64 + h),
                    pattern=[[0, 64]], channel_multiplier=1),
                    wait=[("v", tok_sz)] if first_sel else [])
                first_sel = False
        tok_consts = P.op("gpsimd", lambda e: e.nop(), inc=[("g", 1)])

        P.op("vector", lambda v: v.memset(vts[:, :], 0.0))
        for h in range(NH):
            base0 = vts[:, :]
            ap = bass.AP(base0.tensor, base0.offset + h * 80 + 64 + h,
                         [list(base0.ap[0]), [1280, NKC], [1, 1]])
            P.op("vector", lambda v, ap=ap: v.memset(ap, 1.0))
        tok_vts_ones = P.op("vector", lambda v: v.engine_nop(), inc=[("v", 1)])

        kstage = [_sb(nc, st, f"aa_kst{i}", [128, TT], BF16) for i in range(2)]

        def kv_gen():
            free_kst = [0, 0]
            rd_kv = ("v", env.tok_bcasts)
            for which, wsrc, kvsrc, kvdst, bname in (
                    (0, env.wbb["k"], env.keyTb, env.kbs, "bb_k"),
                    (1, env.wbb["v"], env.valTb, env.vbs, "bb_v")):
                tok_wl = P.op("sync", lambda e, wsrc=wsrc: e.dma_start(
                    out=warena[1].rearrange("p (c x) -> p c x", x=1024),
                    in_=wsrc.rearrange("(c p) x -> p c x", p=128)),
                    wait=[env.cw("caB"), ("t", P.val("t"))], inc=[("kw1", 16)])
                for rt in range(2 * T // TT):
                    tok_kv = P.op("sync", lambda e, kvsrc=kvsrc, rt=rt: e.dma_start(
                        out=warena[2][:, 0:NCH * TT].rearrange("p (c t) -> p c t", t=TT),
                        in_=kvsrc[:, rt * TT:(rt + 1) * TT]
                        .rearrange("(c p) t -> p c t", p=128)),
                        wait=[env.cw("caB"), ("t", P.val("t"))], inc=[("kv", 16)])
                    for mc in range(NCH):
                        tok_mm = 0
                        for c in range(NCH):
                            tok_mm = P.op("tensor", lambda t, mc=mc, c=c: t.matmul(
                                ps[6][:, 0:TT],
                                warena[1][:, c * 1024 + mc * 128: c * 1024 + mc * 128 + 128],
                                warena[2][:, c * TT:(c + 1) * TT],
                                start=(c == 0), stop=(c == NCH - 1)),
                                wait=([("kw1", tok_wl), ("kv", tok_kv),
                                       (rd_kv[0], rd_kv[1])] if c == 0 else []),
                                inc=[("t", 1)] if c == NCH - 1 else [])
                        ksl2 = mc % 2
                        tok_ev = P.op("scalar", lambda sc, mc=mc, bname=bname,
                                      ksl2=ksl2: sc.activation(
                            kstage[ksl2][:, :], ps[6][:, 0:TT], AF.Identity,
                            bias=env.bcol[bname][:, mc:mc + 1]),
                            wait=[("t", tok_mm), ("kvs", free_kst[ksl2])], inc=[("s", 1)])
                        rd_kv = ("s", tok_ev)
                        free_kst[ksl2] = P.op(
                            "gpsimd", lambda e, kvdst=kvdst, mc=mc, rt=rt, ksl2=ksl2:
                            e.dma_start(
                                out=kvdst[mc * 128:(mc + 1) * 128, rt * TT:(rt + 1) * TT],
                                in_=kstage[ksl2][:, :]),
                            wait=[("s", P.val("s"))], inc=[("kvs", 16)])
                        yield
            while True:
                yield

        kvg = kv_gen()
        free_w = [0, 0, 0]
        free_qst = 0
        free_xo = [0, 0]
        free_xt = 0
        free_qsl = 0
        rd_ps = [("v", env.tok_bcasts)] * 8
        rd_ps[7] = rd_p7_init
        rd_pd = [("v", env.tok_bcasts), ("v", env.tok_bcasts)]
        rd_ps_av = list(rd_ps)
        rd_pt = [0, 0]
        tok_kts_rd = 0
        tok_vts_rd = 0
        tok_xn_rd = 0
        tok_srow_rd = 0
        for s_i in range(B_LOC):
            base = s_i * T
            tok_w = [0, 0, 0]
            for wi, wname in enumerate(("q", "k", "v")):
                tok_w[wi] = P.op("sync", lambda e, wi=wi, wname=wname: e.dma_start(
                    out=warena[wi].rearrange("p (c x) -> p c x", x=1024),
                    in_=env.wab[wname].rearrange("(c p) x -> p c x", p=128)),
                    wait=[env.cw("caA"), ("t", free_w[wi])], inc=[("kw%d" % wi, 16)])
            tok_kts_wr = 0
            tok_vts_wr = 0
            for j in range(NQG):
                t0 = base + j * TT
                tok_xld = P.op("sync", lambda e, t0=t0: e.dma_start(
                    out=xt.rearrange("p (c t) -> p c t", t=TT),
                    in_=src[:, t0:t0 + TT].rearrange("(c p) t -> p c t", p=128)),
                    wait=src_ready(t0 // TT) + [("v", free_xt)], inc=[("ax", 16)])
                P.op("sync", lambda e, t0=t0: e.dma_start(
                    out=srow[0:1, 0:TT], in_=env.srows[0:1, t0:t0 + TT]),
                    wait=stats_done(t0 // TT) + [("t", tok_srow_rd)], inc=[("sr", 16)])
                tok_sr = P.op("sync", lambda e, t0=t0: e.dma_start(
                    out=srow[0:1, TT:2 * TT], in_=env.srows[1:2, t0:t0 + TT]),
                    inc=[("sr", 16)])
                P.op("tensor", lambda t: t.matmul(
                    ps[0][:, 0:TT], env.ones_row_f[:, :], srow[0:1, 0:TT],
                    start=True, stop=True),
                    wait=[("sr", tok_sr), (rd_ps[0][0], rd_ps[0][1])])
                tok_bc = P.op("tensor", lambda t: t.matmul(
                    ps[1][:, 0:TT], env.ones_row_f[:, :], srow[0:1, TT:2 * TT],
                    start=True, stop=True),
                    wait=[(rd_ps[1][0], rd_ps[1][1])], inc=[("t", 1)])
                tok_srow_rd = tok_bc
                for c in range(NCH):
                    P.op("vector", lambda v, c=c: v.tensor_sub(
                        xo[0][:, :], xt[:, c * TT:(c + 1) * TT], ps[0][:, 0:TT]),
                        wait=([("t", tok_bc), ("ax", tok_xld), ("t", tok_xn_rd),
                               ("xo0", free_xo[0])] if c == 0 else []))
                    P.op("vector", lambda v, c=c: v.tensor_mul(
                        xn[:, c * TT:(c + 1) * TT], xo[0][:, :], ps[1][:, 0:TT]))
                tok_xn = P.op("vector", lambda v: v.engine_nop(), inc=[("v", 1)])
                free_xt = tok_xn
                rd_ps[0] = ("v", tok_xn)
                rd_ps[1] = ("v", tok_xn)
                for wi in range(2):
                    for mc in range(NCH):
                        pb = 2 + (mc % 2)
                        tok_mm = 0
                        for c in range(NCH):
                            tok_mm = P.op("tensor", lambda t, wi=wi, mc=mc, c=c, pb=pb: t.matmul(
                                ps[pb][:, 0:TT],
                                warena[wi][:, c * 1024 + mc * 128: c * 1024 + mc * 128 + 128],
                                xn[:, c * TT:(c + 1) * TT],
                                start=(c == 0), stop=(c == NCH - 1)),
                                wait=([("kw%d" % wi, tok_w[wi]), ("v", tok_xn),
                                       (rd_ps[pb][0], rd_ps[pb][1])] if c == 0 else []),
                                inc=[("t", 1)] if c == NCH - 1 else [])
                        if wi == 0:
                            tok_ev = P.op("scalar", lambda sc, mc=mc, pb=pb: sc.activation(
                                qstage[:, :], ps[pb][:, 0:TT], AF.Identity,
                                bias=env.bcol["ba_q"][:, mc:mc + 1]),
                                wait=[("t", tok_mm), ("qst", free_qst)], inc=[("s", 1)])
                            rd_ps[pb] = ("s", tok_ev)
                            free_qst = P.op("gpsimd", lambda e, mc=mc, j=j: e.dma_start(
                                out=env.qscr[mc * 128:(mc + 1) * 128, j * TT:(j + 1) * TT],
                                in_=qstage[:, :]),
                                wait=[("s", tok_ev)] + ([("qsl", P.val("qsl"))]
                                                        if (j == 0 and mc == 0) else []),
                                inc=[("qst", 16)])
                        else:
                            tok_ev = P.op("scalar", lambda sc, mc=mc, pb=pb, j=j: sc.activation(
                                kts[:, mc * T + j * TT: mc * T + (j + 1) * TT],
                                ps[pb][:, 0:TT], AF.Identity,
                                bias=env.bcol["ba_k"][:, mc:mc + 1]),
                                wait=[("t", tok_mm)] + ([("t", tok_kts_rd)]
                                                        if (mc == 0 and j == 0) else []),
                                inc=[("s", 1)])
                            rd_ps[pb] = ("s", tok_ev)
                tok_kts_wr = P.val("s")
                for tsl in range(4):
                    kcv = j * 4 + tsl
                    for half in range(2):
                        pb = 4 + half
                        tok_mm = 0
                        for c in range(NCH):
                            tok_mm = P.op("tensor", lambda t, tsl=tsl, half=half, c=c, pb=pb: t.matmul(
                                ps[pb][:, 0:TT],
                                xn[:, c * TT + tsl * 128: c * TT + tsl * 128 + 128],
                                warena[2][:, c * 1024 + half * 512: c * 1024 + (half + 1) * 512],
                                start=(c == 0), stop=(c == NCH - 1)),
                                wait=([("kw2", tok_w[2]), (rd_ps[pb][0], rd_ps[pb][1]),
                                       ("v", tok_vts_ones)] if c == 0 else []),
                                inc=[("t", 1)] if c == NCH - 1 else [])
                        dstb = vts[:, :]
                        dap = bass.AP(dstb.tensor, dstb.offset + kcv * 1280 + half * 8 * 80,
                                      [list(dstb.ap[0]), [80, 8], [1, 64]])
                        tok_ev = P.op("vector", lambda v, pb=pb, dap=dap, half=half: v.tensor_add(
                            dap, ps[pb][:, 0:TT].rearrange("p (h d) -> p h d", d=64),
                            bva128[:, half * 512:(half + 1) * 512]
                            .rearrange("p (h d) -> p h d", d=64)),
                            wait=[("t", tok_mm)] + ([("t", tok_vts_rd)]
                                                    if (j == 0 and tsl == 0 and half == 0)
                                                    else []),
                            inc=[("v", 1)])
                        rd_ps[pb] = ("v", tok_ev)
                tok_vts_wr = P.val("v")
                tok_xn_rd = P.val("t")
            rd_pd[0] = rd_ps[0]
            rd_pd[1] = (rd_ps[2][0], max(rd_ps[2][1], rd_ps[3][1]))
            rd_ps_av[4] = rd_ps[4]
            rd_ps_av[5] = rd_ps[5]
            tok_w3 = P.op("sync", lambda e: e.dma_start(
                out=warena[0].rearrange("p (c x) -> p c x", x=1024),
                in_=env.wab["o"].rearrange("(c p) x -> p c x", p=128)),
                wait=[env.cw("caA"), ("t", P.val("t"))], inc=[("kw0", 16)])
            tok_zb_rd = 0
            for qg in range(NQG):
                n_kc = 4 * (qg + 1)
                tok_q = 0
                for h in range(NH):
                    tok_q = P.op("sync", lambda e, h=h, qg=qg: e.dma_start(
                        out=qsl[(h % 2) * 64:(h % 2) * 64 + 64,
                                (h // 2) * TT:(h // 2 + 1) * TT],
                        in_=env.qscr[h * 64:(h + 1) * 64, qg * TT:(qg + 1) * TT]),
                        wait=([("qst", free_qst), ("t", free_qsl)] if h == 0 else []),
                        inc=[("qsl", 16)])
                tok_xld = P.op("sync", lambda e, qg=qg, base=base: e.dma_start(
                    out=xt.rearrange("p (c t) -> p c t", t=TT),
                    in_=src[:, base + qg * TT: base + (qg + 1) * TT]
                    .rearrange("(c p) t -> p c t", p=128)),
                    wait=[("v", free_xt)], inc=[("ax", 16)])
                tok_zz = P.op("vector", lambda v: v.memset(zbuf[64:80, 0:TT], 0.0),
                              wait=[("t", tok_zb_rd)], inc=[("v", 1)])
                for h in range(NH):
                    pb_av = 4 + (h % 2)
                    n_pair = n_kc // 2
                    tok_exp = [0] * n_pair
                    tok_av = 0

                    def emit_sc_pair(pr):
                        X = pr % 2
                        tok_sc = 0
                        for sub in range(2):
                            kcv = 2 * pr + sub
                            tok_sc = P.op("tensor", lambda t, h=h, kcv=kcv, X=X, sub=sub: t.matmul(
                                pd[X][:, sub * TT:(sub + 1) * TT],
                                kts[(h % 2) * 64:(h % 2) * 64 + 64,
                                    (h // 2) * T + kcv * 128: (h // 2) * T + kcv * 128 + 128],
                                qsl[(h % 2) * 64:(h % 2) * 64 + 64,
                                    (h // 2) * TT:(h // 2 + 1) * TT],
                                start=True, stop=True),
                                wait=([("qsl", tok_q), ("s", tok_kts_wr)]
                                      if (kcv == 0 and h == 0) else [])
                                + ([(rd_pd[X][0], rd_pd[X][1])] if sub == 0 else []),
                                inc=[("t", 1)])
                        tok_exp[pr] = P.op("scalar", lambda sc, X=X: sc.activation(
                            pt[X][:, :], pd[X][:, 0:2 * TT], AF.Exp,
                            bias=env.negC[:, 0:1], scale=0.125),
                            wait=[("t", tok_sc), ("t", rd_pt[X])], inc=[("s", 1)])
                        rd_pd[X] = ("s", tok_exp[pr])
                        if 2 * pr >= qg * 4:
                            off0 = 2 * pr - qg * 4
                            tok_exp[pr] = ("m", P.op("vector",
                                lambda v, X=X, off0=off0: v.tensor_mul(
                                    pt[X][:, 0:2 * TT], pt[X][:, 0:2 * TT],
                                    masksM[:, off0 * TT:(off0 + 2) * TT]),
                                wait=[("s", tok_exp[pr]), ("g", tok_consts)],
                                inc=[("v", 1)]))

                    def emit_av_pair(pr):
                        nonlocal tok_av
                        X = pr % 2
                        for sub in range(2):
                            kcv = 2 * pr + sub
                            te = tok_exp[pr]
                            wexp = [("v", te[1])] if isinstance(te, tuple) else [("s", te)]
                            tok_av = P.op("tensor", lambda t, h=h, kcv=kcv, X=X, sub=sub,
                                          pb_av=pb_av,
                                          first=(kcv == 0), last=(kcv == n_kc - 1): t.matmul(
                                              ps[pb_av][0:80, 0:TT],
                                              vts[:, kcv * 1280 + h * 80: kcv * 1280 + h * 80 + 80],
                                              pt[X][:, sub * TT:(sub + 1) * TT],
                                              start=first, stop=last),
                                          wait=wexp
                                          + ([(rd_ps_av[pb_av][0], rd_ps_av[pb_av][1])]
                                             if kcv == 0 else [])
                                          + ([("v", tok_vts_wr)] if (kcv == 0 and h == 0) else []),
                                          inc=[("t", 1)])
                        rd_pt[X] = tok_av

                    skew = 2 if n_pair >= 3 else 1
                    for pr in range(skew):
                        emit_sc_pair(pr)
                    for pr in range(skew, n_pair):
                        emit_sc_pair(pr)
                        emit_av_pair(pr - skew)
                    for pr in range(n_pair - skew, n_pair):
                        emit_av_pair(pr)
                    tok_z = P.op("vector", lambda v, pb_av=pb_av: v.tensor_add(
                        zbuf[64:80, 0:TT], zbuf[64:80, 0:TT], ps[pb_av][64:80, 0:TT]),
                        wait=[("t", tok_av), ("v", tok_zz)], inc=[("v", 1)])
                    tok_oraw = P.op("vector", lambda v, h=h, pb_av=pb_av: v.tensor_copy(
                        oraw[(h % 2) * 64:(h % 2) * 64 + 64, (h // 2) * TT:(h // 2 + 1) * TT],
                        ps[pb_av][0:64, 0:TT]),
                        wait=[("v", tok_z)], inc=[("v", 1)])
                    rd_ps_av[pb_av] = ("v", tok_oraw)
                    next(kvg)
                free_qsl = P.val("t")
                tz1 = P.op("scalar", lambda sc: sc.activation(
                    zbuf[64:80, 0:TT], zbuf[64:80, 0:TT], AF.Ln),
                    wait=[("v", P.val("v"))], inc=[("s", 1)])
                tz2 = P.op("scalar", lambda sc: sc.activation(
                    zbuf[64:80, 0:TT], zbuf[64:80, 0:TT], AF.Exp, scale=-1.0),
                    wait=[("s", tz1)], inc=[("s", 1)])
                for c in range(NCH):
                    pb = c % 2
                    tok_zb = P.op("tensor", lambda t, c=c, pb=pb: t.matmul(
                        ps[pb][:, 0:TT], selz[64:80, c * 128:(c + 1) * 128],
                        zbuf[64:80, 0:TT], start=True, stop=True),
                        wait=[("s", tz2), ("g", tok_consts),
                              (rd_pd[pb][0], rd_pd[pb][1])], inc=[("t", 1)])
                    tok_on = P.op("vector", lambda v, c=c, pb=pb: v.scalar_tensor_tensor(
                        onrm[:, c * TT:(c + 1) * TT], ps[pb][:, 0:TT], 1.0,
                        oraw[:, c * TT:(c + 1) * TT], ALU.mult, ALU.mult),
                        wait=[("t", tok_zb)], inc=[("v", 1)])
                    rd_pd[pb] = ("v", tok_on)
                tok_on = P.val("v")
                tok_zb_rd = P.val("t")
                for mc in range(NCH):
                    pb = 7 if mc % 2 == 0 else 4
                    tok_mm = 0
                    for c in range(NCH):
                        tok_mm = P.op("tensor", lambda t, mc=mc, c=c, pb=pb: t.matmul(
                            ps[pb][:, 0:TT],
                            warena[0][:, c * 1024 + mc * 128: c * 1024 + mc * 128 + 128],
                            onrm[:, c * TT:(c + 1) * TT],
                            start=(c == 0), stop=(c == NCH - 1)),
                            wait=([("kw0", tok_w3), ("v", tok_on),
                                   (rd_ps_av[pb][0], rd_ps_av[pb][1])] if c == 0 else []),
                            inc=[("t", 1)] if c == NCH - 1 else [])
                    tok_ev = P.op("vector", lambda v, mc=mc, pb=pb: v.scalar_tensor_tensor(
                        xo[mc % 2][:, :], ps[pb][:, 0:TT], env.bcol["ba_o"][:, mc:mc + 1],
                        xt[:, mc * TT:(mc + 1) * TT], ALU.add, ALU.add),
                        wait=[("t", tok_mm), ("ax", tok_xld),
                              ("xo%d" % (mc % 2), free_xo[mc % 2])], inc=[("v", 1)])
                    rd_ps_av[pb] = ("v", tok_ev)
                    free_xo[mc % 2] = P.op("gpsimd", lambda e, mc=mc, qg=qg, base=base: e.dma_start(
                        out=dst[mc * 128:(mc + 1) * 128, base + qg * TT: base + (qg + 1) * TT],
                        in_=xo[mc % 2][:, :]),
                        wait=[("v", tok_ev)], inc=[("xo%d" % (mc % 2), 16)])
                free_xt = P.val("v")
                store_toks[4 * s_i + qg] = [("xo0", free_xo[0]), ("xo1", free_xo[1])]
            tok_kts_rd = P.val("t")
            tok_vts_rd = P.val("t")
            tok_xn_rd = P.val("t")
            free_w = [P.val("t")] * 3
            rd_ps[0] = rd_pd[0]
            rd_ps[1] = rd_pd[0]
            rd_ps[2] = rd_pd[1]
            rd_ps[3] = rd_pd[1]
            for _b in range(4, 8):
                rd_ps[_b] = rd_ps_av[_b]
        env.kv_done = ("kvs", P.val("kvs"))
    return store_toks





def emit_attn_b(P, nc, stack, env, src, dst, src_ready, stats_done):
    store_toks = {}
    with contextlib.ExitStack() as st:
        warena = _sb(nc, st, "ab_w", [128, NCH * 1024], BF16)
        xn = _sb(nc, st, "ab_xn", [128, NCH * TT], BF16)
        qb = _sb(nc, st, "ab_qb", [128, NCH * TT], BF16)
        kv0 = _sb(nc, st, "ab_kv0", [128, NCH * TT], BF16)
        kv1 = _sb(nc, st, "ab_kv1", [128, NCH * TT], BF16)
        vv0 = _sb(nc, st, "ab_vv0", [128, NCH * TT], BF16)
        vv1 = _sb(nc, st, "ab_vv1", [128, NCH * TT], BF16)
        tmp = _sb(nc, st, "ab_tmp", [128, NCH * TT], BF16)
        tmp2 = _sb(nc, st, "ab_tmp2", [128, NCH * TT], BF16)
        pb2 = _sb(nc, st, "ab_pb", [128, 4 * TT], BF16)
        o3s = _sb(nc, st, "ab_o3", [128, NCH * TT], BF16)
        sq = _sb(nc, st, "ab_sq", [128, NCH * TT], BF16)
        hblk = _sb(nc, st, "ab_hblk", [128, 128], BF16)
        srow = [_sb(nc, st, f"ab_srow{i}", [1, 2 * TT]) for i in range(2)]
        rrst = _sb(nc, st, "ab_rrst", [1, TT])
        t1b = _sb(nc, st, "ab_t1", [128, TT], BF16)
        t2b = _sb(nc, st, "ab_t2", [128, TT], BF16)
        sig = [_sb(nc, st, f"ab_sig{i}", [128, TT], BF16) for i in range(2)]
        xo = [_sb(nc, st, f"ab_xo{i}", [128, TT]) for i in range(2)]
        ps = [_psum(nc, st, f"ab_p{i}") for i in range(8)]

        P.op("vector", lambda v: v.memset(hblk[:, 0:64], 0.0))
        tok_hb = P.op("vector", lambda v: v.memset(hblk[:, 64:128], 1.0), inc=[("v", 1)])
        P.op("gpsimd", lambda e: e.affine_select(
            out=hblk[:, 0:64], in_=hblk[:, 0:64], compare_op=ALU.is_ge,
            fill=1.0, base=-64, pattern=[[0, 64]], channel_multiplier=1),
            wait=[("v", tok_hb)])
        P.op("gpsimd", lambda e: e.affine_select(
            out=hblk[:, 64:128], in_=hblk[:, 64:128], compare_op=ALU.is_ge,
            fill=0.0, base=-64, pattern=[[0, 64]], channel_multiplier=1))
        tok_hblk = P.op("gpsimd", lambda e: e.nop(), inc=[("g", 1)])

        tok_kvdone = env.kv_done[1]
        rd_pre_last = {2: ("v", env.tok_bcasts), 3: ("v", env.tok_bcasts)}
        tok_wq = P.op("sync", lambda e: e.dma_start(
            out=warena.rearrange("p (c x) -> p c x", x=1024),
            in_=env.wbb["q"].rearrange("(c p) x -> p c x", p=128)),
            wait=[env.cw("caB"), ("t", P.val("t"))], inc=[("kw0", 16)])

        o3d = [_sb(nc, st, f"ab_o3d{i}", [128, NCH * TT], BF16) for i in range(2)]
        xtd = [_sb(nc, st, f"ab_xtd{i}", [128, NCH * TT]) for i in range(2)]
        qind = [_sb(nc, st, f"ab_qin{i}", [128, NCH * TT], BF16) for i in range(2)]
        free_xtd = [0, 0]
        tok_ss_prev = 0
        tok_o3st_prev = 0
        tok_sq_prev = 0
        free_rrst = 0
        rd_ps = [("v", env.tok_bcasts)] * 8
        rd_ps[2] = rd_pre_last[2]
        rd_ps[3] = rd_pre_last[3]
        tok_srow_rd = [0, 0]
        tok_kvld = 0
        free_kvld = 0
        o3_toks = {}
        tok_wq = P.op("sync", lambda e: e.dma_start(
            out=warena.rearrange("p (c x) -> p c x", x=1024),
            in_=env.wbb["q"].rearrange("(c p) x -> p c x", p=128)),
            wait=[env.cw("caB"), ("t", P.val("t"))], inc=[("kw0", 16)])
        tile_list = []
        for j in range(NTILES // 2):
            for l in range(B_LOC):
                tile_list.append((j, l))
        fronts = {}

        def emit_d1_loads(idx):
            j, l = tile_list[idx]
            i = l * (NTILES // 2) + j
            t0 = i * TT
            sl = idx % 2
            tok_xld = P.op("sync", lambda e, t0=t0, sl=sl: e.dma_start(
                out=xtd[sl].rearrange("p (c t) -> p c t", t=TT),
                in_=src[:, t0:t0 + TT].rearrange("(c p) t -> p c t", p=128)),
                wait=src_ready(i) + [("v", free_xtd[sl])], inc=[("ax", 16)])
            P.op("sync", lambda e, t0=t0, sl=sl: e.dma_start(
                out=srow[sl][0:1, 0:TT], in_=env.srows[0:1, t0:t0 + TT]),
                wait=stats_done(i) + [("t", tok_srow_rd[sl])], inc=[("sr", 16)])
            tok_sr = P.op("sync", lambda e, t0=t0, sl=sl: e.dma_start(
                out=srow[sl][0:1, TT:2 * TT], in_=env.srows[1:2, t0:t0 + TT]),
                inc=[("sr", 16)])
            return tok_xld, tok_sr

        def emit_d1_front(idx, loads, gate_t=0):
            nonlocal tok_kvld, free_kvld
            j, l = tile_list[idx]
            i = l * (NTILES // 2) + j
            t0 = i * TT
            sl = idx % 2
            tok_xld, tok_sr = loads
            if l == 0:
                tok_kvld = P.op("sync", lambda e, j=j: e.dma_start(
                    out=kv0.rearrange("p (c t) -> p c t", t=TT),
                    in_=env.kbs[:, j * TT:(j + 1) * TT].rearrange("(c p) t -> p c t", p=128)),
                    wait=[("kvs", tok_kvdone), ("t", free_kvld)], inc=[("kv", 16)])
                P.op("sync", lambda e, j=j: e.dma_start(
                    out=kv1.rearrange("p (c t) -> p c t", t=TT),
                    in_=env.kbs[:, T + j * TT: T + (j + 1) * TT]
                    .rearrange("(c p) t -> p c t", p=128)), inc=[("kv", 16)])
                P.op("sync", lambda e, j=j: e.dma_start(
                    out=vv0.rearrange("p (c t) -> p c t", t=TT),
                    in_=env.vbs[:, j * TT:(j + 1) * TT].rearrange("(c p) t -> p c t", p=128)),
                    inc=[("kv", 16)])
                tok_kvld = P.op("sync", lambda e, j=j: e.dma_start(
                    out=vv1.rearrange("p (c t) -> p c t", t=TT),
                    in_=env.vbs[:, T + j * TT: T + (j + 1) * TT]
                    .rearrange("(c p) t -> p c t", p=128)), inc=[("kv", 16)])
            P.op("tensor", lambda t, sl=sl: t.matmul(
                ps[0][:, 0:TT], env.ones_row_f[:, :], srow[sl][0:1, 0:TT],
                start=True, stop=True),
                wait=[("sr", tok_sr), (rd_ps[0][0], rd_ps[0][1])])
            tok_bc = P.op("tensor", lambda t, sl=sl: t.matmul(
                ps[1][:, 0:TT], env.ones_row_f[:, :], srow[sl][0:1, TT:2 * TT],
                start=True, stop=True),
                wait=[(rd_ps[1][0], rd_ps[1][1])], inc=[("t", 1)])
            tok_srow_rd[sl] = tok_bc
            for c in range(NCH):
                P.op("vector", lambda v, c=c, sl=sl: v.tensor_sub(
                    xo[0][:, :], xtd[sl][:, c * TT:(c + 1) * TT], ps[0][:, 0:TT]),
                    wait=([("t", tok_bc), ("ax", tok_xld)] if c == 0 else []))
                P.op("vector", lambda v, c=c: v.tensor_mul(
                    xn[:, c * TT:(c + 1) * TT], xo[0][:, :], ps[1][:, 0:TT]))
            tok_xn = P.op("vector", lambda v: v.engine_nop(), inc=[("v", 1)])
            free_xtd[sl] = tok_xn
            rd_ps[0] = ("v", tok_xn)
            rd_ps[1] = ("v", tok_xn)
            for mc in range(NCH):
                pbk = 2 + (mc % 2)
                tok_mm = 0
                for c in range(NCH):
                    tok_mm = P.op("tensor", lambda t, mc=mc, c=c, pbk=pbk: t.matmul(
                        ps[pbk][:, 0:TT],
                        warena[:, c * 1024 + mc * 128: c * 1024 + mc * 128 + 128],
                        xn[:, c * TT:(c + 1) * TT],
                        start=(c == 0), stop=(c == NCH - 1)),
                        wait=([("kw0", tok_wq), ("v", tok_xn),
                               (rd_ps[pbk][0], rd_ps[pbk][1])] if c == 0 else []),
                        inc=[("t", 1)] if c == NCH - 1 else [])
                tok_ev = P.op("scalar", lambda sc, mc=mc, pbk=pbk: sc.activation(
                    qb[:, mc * TT:(mc + 1) * TT], ps[pbk][:, 0:TT], AF.Identity,
                    bias=env.bqb_s[:, mc:mc + 1], scale=0.125),
                    wait=[("t", tok_mm)], inc=[("s", 1)])
                rd_ps[pbk] = ("s", tok_ev)
            tok_qb = P.val("s")
            tok_tmp = 0
            for c in range(NCH):
                P.op("vector", lambda v, c=c: v.tensor_mul(
                    tmp[:, c * TT:(c + 1) * TT], qb[:, c * TT:(c + 1) * TT],
                    kv0[:, c * TT:(c + 1) * TT]),
                    wait=([("s", tok_qb), ("kv", tok_kvld), ("t", gate_t)]
                          if c == 0 else []))
                tok_tmp = P.op("vector", lambda v, c=c: v.tensor_mul(
                    tmp2[:, c * TT:(c + 1) * TT],
                    qb[:, c * TT:(c + 1) * TT], kv1[:, c * TT:(c + 1) * TT]),
                    inc=[("v", 1)])
            return i, t0, tok_tmp

        def emit_d1_back(idx, front_info, interleave=None):
            nonlocal tok_ss_prev, tok_o3st_prev, tok_sq_prev, free_rrst, free_kvld
            i, t0, tok_tmp = front_info
            tok_o3 = 0
            rd_pb2 = [0, 0]
            for c in range(NCH):
                sl2 = c % 2
                pbk = 4 + (c % 2)
                tok_l0 = P.op("tensor", lambda t, c=c, pbk=pbk: t.matmul(
                    ps[pbk][:, 0:TT], hblk[:, :], tmp[:, c * TT:(c + 1) * TT],
                    start=True, stop=True),
                    wait=([("v", tok_tmp), ("g", tok_hblk)] if c == 0 else [])
                    + [(rd_ps[pbk][0], rd_ps[pbk][1])], inc=[("t", 1)])
                tok_l1 = P.op("tensor", lambda t, c=c: t.matmul(
                    ps[6][:, 0:TT], hblk[:, :], tmp2[:, c * TT:(c + 1) * TT],
                    start=True, stop=True),
                    wait=[(rd_ps[6][0], rd_ps[6][1])], inc=[("t", 1)])
                tok_r0 = P.op("scalar", lambda sc, pbk=pbk, sl2=sl2: sc.activation(
                    pb2[:, sl2 * 2 * TT: sl2 * 2 * TT + TT], ps[pbk][:, 0:TT], AF.Relu),
                    wait=[("t", tok_l0), ("v", rd_pb2[sl2])], inc=[("s", 1)])
                rd_ps[pbk] = ("s", tok_r0)
                tok_r1 = P.op("scalar", lambda sc, sl2=sl2: sc.activation(
                    pb2[:, sl2 * 2 * TT + TT: sl2 * 2 * TT + 2 * TT], ps[6][:, 0:TT],
                    AF.Relu),
                    wait=[("t", tok_l1)], inc=[("s", 1)])
                rd_ps[6] = ("s", tok_r1)
                P.op("vector", lambda v, c=c, sl2=sl2: v.tensor_mul(
                    t1b[:, :], pb2[:, sl2 * 2 * TT: sl2 * 2 * TT + TT],
                    vv0[:, c * TT:(c + 1) * TT]),
                    wait=[("s", tok_r1)])
                P.op("vector", lambda v, c=c, sl2=sl2: v.tensor_mul(
                    t2b[:, :], pb2[:, sl2 * 2 * TT + TT: sl2 * 2 * TT + 2 * TT],
                    vv1[:, c * TT:(c + 1) * TT]))
                tok_o3 = P.op("vector", lambda v, c=c: v.tensor_add(
                    o3s[:, c * TT:(c + 1) * TT], t1b[:, :], t2b[:, :]),
                    inc=[("v", 1)])
                rd_pb2[sl2] = tok_o3
            free_kvld = P.val("t")
            if interleave is not None:
                interleave(P.val("t"))
            tok_o3st = P.op("gpsimd", lambda e, t0=t0: e.dma_start(
                out=env.o3[:, t0:t0 + TT].rearrange("(c p) t -> p c t", p=128),
                in_=o3s.rearrange("p (c t) -> p c t", t=TT)),
                wait=[("v", tok_o3)], inc=[("o3s", 16)])
            tok_o3st_prev = tok_o3st
            o3_toks[i] = tok_o3st
            tok_sq = tok_sq_prev = P.op("scalar", lambda sc: sc.activation(
                sq[:, :], o3s[:, :], AF.Square),
                wait=[("v", tok_o3)], inc=[("s", 1)])
            tok_ss = 0
            for c in range(NCH):
                tok_ss = P.op("tensor", lambda t, c=c: t.matmul(
                    ps[7][0:1, 0:TT], env.ones_col_b[:, :], sq[:, c * TT:(c + 1) * TT],
                    start=(c == 0), stop=(c == NCH - 1)),
                    wait=[("s", tok_sq), (rd_ps[7][0], rd_ps[7][1])] if c == 0 else [],
                    inc=[("t", 1)] if c == NCH - 1 else [])
            tok_ss_prev = tok_ss
            tv = P.op("vector", lambda v: v.tensor_scalar(
                rrst[0:1, 0:TT], ps[7][0:1, 0:TT], 1e-30, None, ALU.add),
                wait=[("t", tok_ss), ("rrs", free_rrst)], inc=[("v", 1)])
            rd_ps[7] = ("v", tv)
            ts1 = P.op("scalar", lambda sc: sc.activation(
                rrst[0:1, 0:TT], rrst[0:1, 0:TT], AF.Ln),
                wait=[("v", tv)], inc=[("s", 1)])
            ts2 = P.op("scalar", lambda sc: sc.activation(
                rrst[0:1, 0:TT], rrst[0:1, 0:TT], AF.Exp, scale=0.5),
                wait=[("s", ts1)], inc=[("s", 1)])
            tv2 = P.op("vector", lambda v: v.tensor_scalar(
                rrst[0:1, 0:TT], rrst[0:1, 0:TT], 1.0 / 32.0, 1e-8,
                ALU.mult, ALU.add), wait=[("s", ts2)], inc=[("v", 1)])
            ts3 = P.op("scalar", lambda sc: sc.activation(
                rrst[0:1, 0:TT], rrst[0:1, 0:TT], AF.Ln),
                wait=[("v", tv2)], inc=[("s", 1)])
            ts4 = P.op("scalar", lambda sc: sc.activation(
                rrst[0:1, 0:TT], rrst[0:1, 0:TT], AF.Exp, scale=-1.0),
                wait=[("s", ts3)], inc=[("s", 1)])
            free_rrst = P.op("gpsimd", lambda e, t0=t0: e.dma_start(
                out=env.rrow[0:1, t0:t0 + TT], in_=rrst[0:1, 0:TT]),
                wait=[("s", ts4)], inc=[("rrs", 16)])

        d1_loads = {0: emit_d1_loads(0)}
        fronts[0] = emit_d1_front(0, d1_loads[0])
        for idx in range(len(tile_list)):
            if idx + 1 < len(tile_list):
                d1_loads[idx + 1] = emit_d1_loads(idx + 1)
                def ilv(gate_t, idx=idx):
                    fronts[idx + 1] = emit_d1_front(idx + 1, d1_loads[idx + 1],
                                                    gate_t=gate_t)
                emit_d1_back(idx, fronts[idx], interleave=ilv)
            else:
                emit_d1_back(idx, fronts[idx])
        tok_d1_done = [(sn, P.val(sn)) for sn in ("o3s", "rrs")]

        tok_wo = P.op("sync", lambda e: e.dma_start(
            out=warena.rearrange("p (c x) -> p c x", x=1024),
            in_=env.wbb["o"].rearrange("(c p) x -> p c x", p=128)),
            wait=[env.cw("caB"), ("t", P.val("t"))], inc=[("kw0", 16)])
        free_xo = [0, 0]
        free_o3d = [0, 0]
        free_qind = [0, 0]
        tok_rr_rd = [tok_srow_rd[0], tok_srow_rd[1]]
        toks_q = {}

        def emit_d2_front(i):
            sl = i % 2
            t0 = i * TT
            tok_xld = P.op("sync", lambda e, t0=t0, sl=sl: e.dma_start(
                out=xtd[sl].rearrange("p (c t) -> p c t", t=TT),
                in_=src[:, t0:t0 + TT].rearrange("(c p) t -> p c t", p=128)),
                wait=[("v", free_xtd[sl])], inc=[("ax", 16)])
            tok_o3ld = P.op("sync", lambda e, t0=t0, sl=sl: e.dma_start(
                out=o3d[sl].rearrange("p (c t) -> p c t", t=TT),
                in_=env.o3[:, t0:t0 + TT].rearrange("(c p) t -> p c t", p=128)),
                wait=tok_d1_done + [("s", free_o3d[sl][0] if False else 0),
                                    ("v", free_o3d[sl])], inc=[("o3l", 16)])
            tok_rr = P.op("sync", lambda e, t0=t0, sl=sl: e.dma_start(
                out=srow[sl][0:1, 0:TT], in_=env.rrow[0:1, t0:t0 + TT]),
                wait=[("t", tok_rr_rd[sl])], inc=[("sr", 16)])
            tok_rb = P.op("tensor", lambda t, sl=sl: t.matmul(
                ps[0][:, 0:TT], env.ones_row_f[:, :], srow[sl][0:1, 0:TT],
                start=True, stop=True),
                wait=[("sr", tok_rr), (rd_ps[0][0], rd_ps[0][1])], inc=[("t", 1)])
            tok_rr_rd[sl] = tok_rb
            free_o3d_guard = free_o3d[sl]
            tok_qin = 0
            for c in range(NCH):
                tok_sg = P.op("scalar", lambda sc, c=c, sl=sl: sc.activation(
                    sig[sl][:, :], o3d[sl][:, c * TT:(c + 1) * TT], AF.Sigmoid,
                    scale=env.bcol["ng"][:, c:c + 1]),
                    wait=([("o3l", tok_o3ld), ("v", free_o3d_guard)] if c == 0
                          else [("v", tok_qin)]),
                    inc=[("s", 1)])
                tv1 = P.op("vector", lambda v, c=c, sl=sl: v.scalar_tensor_tensor(
                    t1b[:, :], ps[0][:, 0:TT], 1.0, o3d[sl][:, c * TT:(c + 1) * TT],
                    ALU.mult, ALU.mult),
                    wait=[("t", tok_rb), ("s", tok_sg)]
                    + ([("t", free_qind[sl])] if c == 0 else []), inc=[("v", 1)])
                tok_qin = P.op("vector", lambda v, c=c, sl=sl: v.scalar_tensor_tensor(
                    qind[sl][:, c * TT:(c + 1) * TT], t1b[:, :], env.bcol["nw"][:, c:c + 1],
                    sig[sl][:, :], ALU.mult, ALU.mult),
                    wait=[("v", tv1)], inc=[("v", 1)])
            rd_ps[0] = ("v", tok_qin)
            free_o3d[sl] = tok_qin
            return tok_qin, tok_xld

        tok_front = {0: emit_d2_front(0)}
        for i in range(NTILES):
            sl = i % 2
            t0 = i * TT
            tok_qin, tok_xld = tok_front[i]
            if i + 1 < NTILES:
                pass
            def emit_wo_mm(mc):
                pbk = 2 + (mc % 2)
                tok_mm = 0
                for c in range(NCH):
                    tok_mm = P.op("tensor", lambda t, mc=mc, c=c, pbk=pbk, sl=sl: t.matmul(
                        ps[pbk][:, 0:TT],
                        warena[:, c * 1024 + mc * 128: c * 1024 + mc * 128 + 128],
                        qind[sl][:, c * TT:(c + 1) * TT],
                        start=(c == 0), stop=(c == NCH - 1)),
                        wait=([("kw0", tok_wo), ("v", tok_qin),
                               (rd_ps[pbk][0], rd_ps[pbk][1])] if c == 0 else []),
                        inc=[("t", 1)] if c == NCH - 1 else [])
                return pbk, tok_mm

            def emit_wo_ev(mc, pbk, tok_mm):
                tok_ev = P.op("vector", lambda v, mc=mc, pbk=pbk, sl=sl: v.scalar_tensor_tensor(
                    xo[mc % 2][:, :], ps[pbk][:, 0:TT], env.bcol["bb_o"][:, mc:mc + 1],
                    xtd[sl][:, mc * TT:(mc + 1) * TT], ALU.add, ALU.add),
                    wait=[("t", tok_mm), ("ax", tok_xld),
                          ("xo%d" % (mc % 2), free_xo[mc % 2])], inc=[("v", 1)])
                rd_ps[pbk] = ("v", tok_ev)
                free_xo[mc % 2] = P.op("gpsimd", lambda e, mc=mc, t0=t0: e.dma_start(
                    out=dst[mc * 128:(mc + 1) * 128, t0:t0 + TT], in_=xo[mc % 2][:, :]),
                    wait=[("v", tok_ev)], inc=[("xo%d" % (mc % 2), 16)])

            mm_info = {}
            mm_info[0] = emit_wo_mm(0)
            mm_info[1] = emit_wo_mm(1)
            if i + 1 < NTILES:
                tok_front[i + 1] = emit_d2_front(i + 1)
            for mc in range(2, NCH):
                emit_wo_ev(mc - 2, *mm_info[mc - 2])
                mm_info[mc] = emit_wo_mm(mc)
            emit_wo_ev(NCH - 2, *mm_info[NCH - 2])
            emit_wo_ev(NCH - 1, *mm_info[NCH - 1])
            free_qind[sl] = mm_info[NCH - 1][1]
            free_xtd[sl] = P.val("v")
            store_toks[i] = [("xo0", free_xo[0]), ("xo1", free_xo[1])]
    return store_toks


def build_nc(phases=("full",)):
    nc = bass.Bass("TRN2", target_bir_lowering=False, detect_race_conditions=False)
    P = Prog()
    env = Env()
    stack = contextlib.ExitStack()
    emit_setup(P, nc, stack, env)
    nobar = lambda i: []
    if "full" in phases:
        sw = emit_stats(P, nc, stack, env, env.xT, nobar, tag="0")
        emit_barrier(P)
        emit_attn_a(P, nc, stack, env, env.xT, env.xs, nobar, lambda i: list(sw))
        emit_barrier(P)
        sw = emit_stats(P, nc, stack, env, env.xs, nobar, tag="1")
        emit_barrier(P)
        emit_ffn(P, nc, stack, env, env.xs, env.xs, nobar, lambda i: list(sw), phase=0)
        emit_barrier(P)
        sw = emit_stats(P, nc, stack, env, env.xs, nobar, tag="2")
        emit_barrier(P)
        emit_attn_b(P, nc, stack, env, env.xs, env.xs, nobar, lambda i: list(sw))
        emit_barrier(P)
        sw = emit_stats(P, nc, stack, env, env.xs, nobar, tag="3")
        emit_barrier(P)
        emit_ffn(P, nc, stack, env, env.xs, env.out, nobar, lambda i: list(sw), phase=1)
    else:
        sw = emit_stats(P, nc, stack, env, env.xT, nobar)
        emit_barrier(P)
        if "attn_a" in phases:
            emit_attn_a(P, nc, stack, env, env.xT, env.out, nobar, lambda i: list(sw))
        elif "attn_b" in phases:
            emit_attn_b(P, nc, stack, env, env.xT, env.out, nobar, lambda i: list(sw))
        else:
            emit_ffn(P, nc, stack, env, env.xT, env.out, nobar, lambda i: list(sw), phase=0)
    final = [(sname, P.val(sname)) for sname in P.counts]
    P.op("gpsimd", lambda e: e.nop(), wait=final)
    P.build(nc, stack)
    stack.close()
    return nc


_NC_CACHE = {}


def _get_nc():
    if "nc" not in _NC_CACHE:
        _NC_CACHE["nc"] = build_nc(phases=("full",))
    return _NC_CACHE["nc"]


def kernel(**inputs):
    in_maps = host_layout(inputs)
    nc = _get_nc()
    res = run_bass_kernel_spmd(nc, in_maps, core_ids=list(range(NCORES)))
    return gather_out(res.results)


def host_layout(inputs):
    f = np.ascontiguousarray
    x = np.asarray(inputs["L0_attn_a_x"], np.float32)
    key = np.asarray(inputs["L0_attn_b_key"], np.float32)
    val = np.asarray(inputs["L0_attn_b_value"], np.float32)
    keyT = f(key.reshape(2 * T, D).T)
    valT = f(val.reshape(2 * T, D).T)
    wsgT = f(np.asarray(inputs["L0_ffn_shared_gate_w"], np.float32).T
             .reshape(D, SH // 128, 128).transpose(1, 0, 2))
    wsuT = f(np.asarray(inputs["L0_ffn_shared_up_w"], np.float32).T
             .reshape(D, SH // 128, 128).transpose(1, 0, 2))
    wsdT = f(np.asarray(inputs["L0_ffn_shared_down_w"], np.float32).T
             .reshape(SH, NCH, 128).transpose(1, 0, 2))
    common = {
        "wa_q": f(np.asarray(inputs["L0_attn_a_wq"], np.float32)),
        "wa_k": f(np.asarray(inputs["L0_attn_a_wk"], np.float32)),
        "wa_v": f(np.asarray(inputs["L0_attn_a_wv"], np.float32)),
        "wa_o": f(np.asarray(inputs["L0_attn_a_wo"], np.float32)),
        "ba_q": np.asarray(inputs["L0_attn_a_bq"], np.float32).reshape(1, D),
        "ba_k": np.asarray(inputs["L0_attn_a_bk"], np.float32).reshape(1, D),
        "ba_v": np.asarray(inputs["L0_attn_a_bv"], np.float32).reshape(1, D),
        "ba_o": np.asarray(inputs["L0_attn_a_bo"], np.float32).reshape(1, D),
        "keyT": keyT, "valT": valT,
        "wb_q": f(np.asarray(inputs["L0_attn_b_Wq"], np.float32).T),
        "wb_k": f(np.asarray(inputs["L0_attn_b_Wk"], np.float32).T),
        "wb_v": f(np.asarray(inputs["L0_attn_b_Wv"], np.float32).T),
        "wb_o": f(np.asarray(inputs["L0_attn_b_Wo"], np.float32).T),
        "bb_q": np.asarray(inputs["L0_attn_b_bq"], np.float32).reshape(1, D),
        "bb_k": np.asarray(inputs["L0_attn_b_bk"], np.float32).reshape(1, D),
        "bb_v": np.asarray(inputs["L0_attn_b_bv"], np.float32).reshape(1, D),
        "bb_o": np.asarray(inputs["L0_attn_b_bo"], np.float32).reshape(1, D),
        "nw": np.asarray(inputs["L0_attn_b_norm_weight"], np.float32).reshape(1, D),
        "ng": np.asarray(inputs["L0_attn_b_norm_gate"], np.float32).reshape(1, D),
        "routerT": f(np.asarray(inputs["L0_ffn_router_w"], np.float32).T),
        "cbias": np.asarray(inputs["L0_ffn_correction_bias"], np.float32).reshape(1, E),
        "wsgT": wsgT, "wsuT": wsuT, "wsdT": wsdT,
        "gw": f(np.asarray(inputs["L0_ffn_gate_w"], np.float32)
                .reshape(E, D, 2, 128).transpose(0, 2, 1, 3)),
        "uw": f(np.asarray(inputs["L0_ffn_up_w"], np.float32)
                .reshape(E, D, 2, 128).transpose(0, 2, 1, 3)),
        "dw": f(np.asarray(inputs["L0_ffn_down_w"], np.float32)),
    }
    in_maps = []
    for core in range(NCORES):
        xc = x[core * B_LOC:(core + 1) * B_LOC]
        xTc = f(xc.reshape(NTOK, D).T)
        m = dict(common)
        m["xT"] = xTc
        in_maps.append(m)
    return in_maps


def gather_out(results):
    outs = []
    for core in range(NCORES):
        o = results[core]["out"]
        outs.append(np.ascontiguousarray(o.T).reshape(B_LOC, T, D))
    return np.concatenate(outs, axis=0).astype(np.float32)


# version 13
# speedup vs baseline: 1.0486x; 1.0313x over previous
import contextlib
import numpy as np

import concourse.bass as bass
import concourse.mybir as mybir
from concourse.bass_utils import run_bass_kernel_spmd

F32 = mybir.dt.float32
BF16 = mybir.dt.bfloat16
AF = mybir.ActivationFunctionType
ALU = mybir.AluOpType
AX = mybir.AxisListType

B_FULL = 16
NCORES = 8
B_LOC = B_FULL // NCORES
T = 2048
D = 1024
NH = 16
HD = 64
NTOK = B_LOC * T
NCH = D // 128
TT = 512
NTILES = NTOK // TT
E = 16
EH = 256
SH = 2048
SOFT_C = 12.0
NEGBIG = -1.0e30

ENGINES = ("sync", "gpsimd", "vector", "scalar", "tensor")


class Prog:

    def __init__(self):
        self.q = {e: [] for e in ENGINES}
        self.counts = {}

    def val(self, s):
        return self.counts.get(s, 0)

    def op(self, eng, fn, wait=(), inc=()):
        wait = [(s, v) for (s, v) in wait if v and v > 0]
        inc = list(inc)
        self.q[eng].append((wait, fn, inc))
        out = []
        for s, d in inc:
            self.counts[s] = self.counts.get(s, 0) + d
            out.append(self.counts[s])
        return out[0] if len(out) == 1 else out

    def build(self, nc, stack):
        sems = {s: stack.enter_context(nc.semaphore(s)) for s in self.counts}

        def mk(ename):
            def emit(eng):
                for wait, fn, inc in self.q[ename]:
                    for s, v in wait:
                        eng.wait_ge(sems[s], v)
                    inst = fn(eng)
                    for s, d in inc:
                        inst.then_inc(sems[s], d)
            return emit

        block = stack.enter_context(nc.Block())
        for ename in ENGINES:
            if self.q[ename]:
                getattr(block, ename)(mk(ename))
        return sems


class Env:
    pass


def _sb(nc, stack, name, shape, dtype=F32):
    return stack.enter_context(nc.sbuf_tensor(name, shape, dtype))


def _psum(nc, stack, name):
    return stack.enter_context(nc.psum_tensor(name, [128, 512], F32))


def strided(ap, extra_off, dims):
    base = ap[:, :] if not isinstance(ap, bass.AP) else ap
    return bass.AP(base.tensor, base.offset + extra_off, [list(base.ap[0])] + dims)



def emit_setup(P, nc, stack, env):
    dram = lambda n, s, d=F32, k="ExternalInput": nc.dram_tensor(n, s, d, kind=k).ap()

    env.xT = dram("xT", [D, NTOK])
    env.wa = {k: dram("wa_" + k, [D, D]) for k in ("q", "k", "v", "o")}
    env.ba = {k: dram("ba_" + k, [1, D]) for k in ("q", "k", "v", "o")}
    env.keyT = dram("keyT", [D, 2 * T])
    env.valT = dram("valT", [D, 2 * T])
    env.wb = {k: dram("wb_" + k, [D, D]) for k in ("q", "k", "v", "o")}
    env.bb = {k: dram("bb_" + k, [1, D]) for k in ("q", "k", "v", "o")}
    env.nw = dram("nw", [1, D])
    env.ng = dram("ng", [1, D])
    env.routerT = dram("routerT", [D, E])
    env.cbias = dram("cbias", [1, E])
    env.wsgT = dram("wsgT", [SH // 128, D, 128])
    env.wsuT = dram("wsuT", [SH // 128, D, 128])
    env.wsdT = dram("wsdT", [NCH, SH, 128])
    env.gw = dram("gw", [E, 2, D, 128])
    env.uw = dram("uw", [E, 2, D, 128])
    env.dw = dram("dw", [E, EH, D])

    dint = lambda n, s, d: nc.dram_tensor(n, s, d, kind="Internal").ap()
    env.wab = {k: dint("wab_" + k, [D, D], BF16) for k in env.wa}
    env.wbb = {k: dint("wbb_" + k, [D, D], BF16) for k in env.wb}
    env.keyTb = dint("keyTb", [D, 2 * T], BF16)
    env.valTb = dint("valTb", [D, 2 * T], BF16)
    env.routerTb = dint("routerTb", [D, E], BF16)
    env.wsgTb = dint("wsgTb", [SH // 128, D, 128], BF16)
    env.wsuTb = dint("wsuTb", [SH // 128, D, 128], BF16)
    env.wsdTb = dint("wsdTb", [NCH, SH, 128], BF16)
    env.gwb = dint("gwb", [E, 2, D, 128], BF16)
    env.uwb = dint("uwb", [E, 2, D, 128], BF16)
    env.dwb = dint("dwb", [E, EH, D], BF16)
    env.qscr = dint("qscr", [D, T], BF16)
    env.kbs = dint("kbs", [D, 2 * T], BF16)
    env.vbs = dint("vbs", [D, 2 * T], BF16)
    env.o3 = dint("o3", [D, NTOK], BF16)
    env.xs = dint("xs", [D, NTOK], F32)
    env.srows = dint("srows", [2, NTOK], F32)
    env.ssrow = dint("ssrow", [1, NTOK], F32)
    env.rrow = dint("rrow", [1, NTOK], F32)
    env.out = dram("out", [D, NTOK], F32, k="ExternalOutput")

    groups = {
        "caA": [("wa_q", env.wa["q"], env.wab["q"]), ("wa_k", env.wa["k"], env.wab["k"]),
                ("wa_v", env.wa["v"], env.wab["v"]), ("wa_o", env.wa["o"], env.wab["o"])],
        "caB": [("wb_q", env.wb["q"], env.wbb["q"]), ("wb_k", env.wb["k"], env.wbb["k"]),
                ("wb_v", env.wb["v"], env.wbb["v"]), ("wb_o", env.wb["o"], env.wbb["o"]),
                ("keyT", env.keyT, env.keyTb), ("valT", env.valT, env.valTb)],
        "caF": [("routerT", env.routerT, env.routerTb),
                ("wsgT", env.wsgT, env.wsgTb), ("wsuT", env.wsuT, env.wsuTb),
                ("wsdT", env.wsdT, env.wsdTb),
                ("gw", env.gw, env.gwb), ("uw", env.uw, env.uwb),
                ("dw", env.dw, env.dwb)],
    }
    env.cast_grp = {}
    for grp, casts in groups.items():
        for name, src, dst in casts:
            if len(src.shape) >= 3:
                for i in range(src.shape[0]):
                    P.op("gpsimd", lambda g, s=src, d=dst, i=i:
                         g.dma_start(out=d[i], in_=s[i]), inc=[(grp, 16)])
            else:
                P.op("gpsimd", lambda g, s=src, d=dst:
                     g.dma_start(out=d[:, :], in_=s[:, :]), inc=[(grp, 16)])
        env.cast_grp[grp] = P.val(grp)
    env.cw = lambda grp: (grp, env.cast_grp[grp])

    env.identf = _sb(nc, stack, "identf", [128, 128])
    env.ones_col_b = _sb(nc, stack, "ones_col_b", [128, 1], BF16)
    env.ones_row_f = _sb(nc, stack, "ones_row_f", [1, 128])
    env.ones_row_b = _sb(nc, stack, "ones_row_b", [1, 128], BF16)
    env.hsum_b = _sb(nc, stack, "hsum_b", [128, 2], BF16)
    env.eps1 = _sb(nc, stack, "eps1", [1, 1])
    env.negC = _sb(nc, stack, "negC", [128, 1])
    env.bcol = {}
    for nm in ("ba_q", "ba_k", "ba_o", "bb_q", "bb_k", "bb_v", "bb_o", "nw", "ng"):
        env.bcol[nm] = _sb(nc, stack, "bcol_" + nm, [128, NCH])
    env.bqb_s = _sb(nc, stack, "bqb_s", [128, NCH])
    env.cbrow = _sb(nc, stack, "cbrow", [1, E])
    env.cb128 = _sb(nc, stack, "cb128", [128, E])

    vv = lambda fn, **kw: P.op("vector", fn, **kw)
    g = lambda fn, **kw: P.op("gpsimd", fn, **kw)
    vv(lambda e: e.memset(env.identf[:, :], 0.0))
    vv(lambda e: e.memset(env.ones_col_b[:, :], 1.0))
    vv(lambda e: e.memset(env.ones_row_f[:, :], 1.0))
    vv(lambda e: e.memset(env.ones_row_b[:, :], 1.0))
    vv(lambda e: e.memset(env.eps1[:, :], 1e-5))
    vv(lambda e: e.memset(env.negC[:, :], -SOFT_C))
    vv(lambda e: e.memset(env.hsum_b[:, 0:1], 0.0))
    tok_ms = vv(lambda e: e.memset(env.hsum_b[:, 1:2], 1.0), inc=[("v", 1)])
    g(lambda e: e.affine_select(out=env.identf[:, :], in_=env.identf[:, :],
                                compare_op=ALU.not_equal, fill=1.0, base=0,
                                pattern=[[-1, 128]], channel_multiplier=1),
      wait=[("v", tok_ms)])
    g(lambda e: e.affine_select(out=env.hsum_b[:, 0:1], in_=env.hsum_b[:, 0:1],
                                compare_op=ALU.is_ge, fill=1.0, base=-64,
                                pattern=[[0, 1]], channel_multiplier=1))
    g(lambda e: e.affine_select(out=env.hsum_b[:, 1:2], in_=env.hsum_b[:, 1:2],
                                compare_op=ALU.is_ge, fill=0.0, base=-64,
                                pattern=[[0, 1]], channel_multiplier=1))
    env.tok_consts = g(lambda e: e.nop(), inc=[("g", 1)])

    for nm, srcap in [("ba_q", env.ba["q"]), ("ba_k", env.ba["k"]), ("ba_o", env.ba["o"]),
                      ("bb_q", env.bb["q"]), ("bb_k", env.bb["k"]), ("bb_v", env.bb["v"]),
                      ("bb_o", env.bb["o"]), ("nw", env.nw), ("ng", env.ng)]:
        dst = env.bcol[nm]
        src = bass.AP(srcap.tensor, 0, [[1, 128], [128, NCH]])
        P.op("sync", lambda e, d=dst, s=src: e.dma_start(
            out=d[:, :], in_=s, allow_slow_non_contiguous=True), inc=[("bia", 16)])
    P.op("sync", lambda e: e.dma_start(out=env.cbrow[:, :], in_=env.cbias[:, :]),
         inc=[("bia", 16)])
    tok_bld = P.val("bia")
    tok_bqs = P.op("vector", lambda v: v.tensor_scalar(
        env.bqb_s[:, :], env.bcol["bb_q"][:, :], 1.0 / np.sqrt(HD), None, ALU.mult),
        wait=[("bia", tok_bld)], inc=[("v", 1)])
    with contextlib.ExitStack() as setup_st:
        pst = setup_st.enter_context(nc.psum_tensor("ps_setup", [128, 512], F32))
        ttok = P.op("tensor", lambda t: t.matmul(
            pst[:, 0:E], env.ones_row_f[:, :], env.cbrow[0:1, :], start=True, stop=True),
            wait=[("bia", tok_bld), ("g", env.tok_consts), ("v", tok_bqs)], inc=[("t", 1)])
        env.tok_bcasts = P.op("vector", lambda v: v.tensor_copy(env.cb128[:, :], pst[:, 0:E]),
                              wait=[("t", ttok)], inc=[("v", 1)])



def emit_stats(P, nc, stack, env, src, src_ready, tag=""):
    with contextlib.ExitStack() as st:
        xt = [_sb(nc, st, f"sa{tag}_x{i}", [128, NCH * TT]) for i in range(2)]
        xb = _sb(nc, st, "sa" + tag + "_xb", [128, NCH * TT], BF16)
        sq = _sb(nc, st, "sa" + tag + "_sq", [128, NCH * TT], BF16)
        row = [_sb(nc, st, f"sa{tag}_row{i}", [1, 2 * TT]) for i in range(2)]
        tmp = _sb(nc, st, "sa" + tag + "_tmp", [1, TT])
        p_mu = _psum(nc, st, "sa" + tag + "_pmu")
        p_is = _psum(nc, st, "sa" + tag + "_pis")
        free_x = [0, 0]
        free_row = [0, 0]
        tok_is_mm_prev = 0
        tok_var_prev = 0
        for i in range(NTILES):
            sl = i % 2
            xsem = "xt%d" % sl
            rsem = "ss%d" % sl
            t0 = i * TT
            tok_ld = P.op("sync", lambda e, sl=sl, t0=t0: e.dma_start(
                out=xt[sl].rearrange("p (c t) -> p c t", t=TT),
                in_=src[:, t0:t0 + TT].rearrange("(c p) t -> p c t", p=128)),
                wait=src_ready(i) + [("v", free_x[sl])], inc=[(xsem, 16)])
            tok_cast = P.op("vector", lambda v, sl=sl: v.tensor_copy(xb[:, :], xt[sl][:, :]),
                            wait=[(xsem, tok_ld), ("t", tok_is_mm_prev)], inc=[("v", 1)])
            free_x[sl] = tok_cast
            tok_sq = P.op("scalar", lambda s: s.activation(sq[:, :], xb[:, :], AF.Square),
                          wait=[("v", tok_cast)], inc=[("s", 1)])
            tok_mu_mm = 0
            for c in range(NCH):
                tok_mu_mm = P.op("tensor", lambda t, c=c: t.matmul(
                    p_mu[0:1, 0:TT], env.ones_col_b[:, :], xb[:, c * TT:(c + 1) * TT],
                    start=(c == 0), stop=(c == NCH - 1)),
                    wait=[("v", tok_cast), ("v", tok_var_prev)] if c == 0 else [],
                    inc=[("t", 1)] if c == NCH - 1 else [])
            tok_is_mm = 0
            for c in range(NCH):
                tok_is_mm = P.op("tensor", lambda t, c=c: t.matmul(
                    p_is[0:1, 0:TT], env.ones_col_b[:, :], sq[:, c * TT:(c + 1) * TT],
                    start=(c == 0), stop=(c == NCH - 1)),
                    wait=[("s", tok_sq)] if c == 0 else [],
                    inc=[("t", 1)] if c == NCH - 1 else [])
            tok_is_mm_prev = tok_is_mm
            tv = P.op("vector", lambda v, sl=sl: v.tensor_scalar(
                row[sl][0:1, 0:TT], p_mu[0:1, 0:TT], 1.0 / D, None, ALU.mult),
                wait=[("t", tok_mu_mm), (rsem, free_row[sl])], inc=[("v", 1)])
            tv = P.op("vector", lambda v, sl=sl: v.tensor_mul(
                tmp[0:1, 0:TT], row[sl][0:1, 0:TT], row[sl][0:1, 0:TT]),
                wait=[("v", tv)], inc=[("v", 1)])
            tok_var = tok_var_prev = P.op("vector", lambda v, sl=sl: v.scalar_tensor_tensor(
                row[sl][0:1, TT:2 * TT], p_is[0:1, 0:TT], 1.0 / D, tmp[0:1, 0:TT],
                ALU.mult, ALU.subtract), wait=[("t", tok_is_mm), ("v", tv)], inc=[("v", 1)])
            ts_ = P.op("scalar", lambda s, sl=sl: s.activation(
                row[sl][0:1, TT:2 * TT], row[sl][0:1, TT:2 * TT], AF.Ln,
                bias=env.eps1[0:1, 0:1]), wait=[("v", tok_var)], inc=[("s", 1)])
            tok_inv = P.op("scalar", lambda s, sl=sl: s.activation(
                row[sl][0:1, TT:2 * TT], row[sl][0:1, TT:2 * TT], AF.Exp, scale=-0.5),
                wait=[("s", ts_)], inc=[("s", 1)])
            P.op("gpsimd", lambda e, sl=sl, t0=t0: e.dma_start(
                out=env.srows[0:1, t0:t0 + TT], in_=row[sl][0:1, 0:TT]),
                wait=[("s", tok_inv)], inc=[(rsem, 16)])
            free_row[sl] = P.op("gpsimd", lambda e, sl=sl, t0=t0: e.dma_start(
                out=env.srows[1:2, t0:t0 + TT], in_=row[sl][0:1, TT:2 * TT]),
                inc=[(rsem, 16)])
    return [("ss0", P.val("ss0")), ("ss1", P.val("ss1"))]



def emit_ffn(P, nc, stack, env, src, dst, src_ready, stats_done, phase, free_dws=0):
    store_toks = {}
    with contextlib.ExitStack() as st:
        xt = [_sb(nc, st, f"f{phase}_x{i}", [128, NCH * TT]) for i in range(2)]
        xn = _sb(nc, st, f"f{phase}_xn", [128, NCH * TT], BF16)
        hsh = _sb(nc, st, f"f{phase}_hsh", [128, 16 * TT], BF16)
        hall = _sb(nc, st, f"f{phase}_hall", [128, 32 * TT], BF16)
        dws = _sb(nc, st, f"f{phase}_dws", [128, E * 2048], BF16)
        geb = [_sb(nc, st, f"f{phase}_ge{i}", [128, 2048], BF16) for i in range(2)]
        ueb = [_sb(nc, st, f"f{phase}_ue{i}", [128, 2048], BF16) for i in range(2)]
        wsg = [_sb(nc, st, f"f{phase}_wsg{i}", [128, D], BF16) for i in range(2)]
        wsu = [_sb(nc, st, f"f{phase}_wsu{i}", [128, D], BF16) for i in range(2)]
        wsd = [_sb(nc, st, f"f{phase}_wsd{i}", [128, SH], BF16) for i in range(2)]
        srow = [_sb(nc, st, f"f{phase}_sr0", [1, 2 * TT])]
        tmpf = _sb(nc, st, f"f{phase}_tmpf", [128, TT])
        slact = _sb(nc, st, f"f{phase}_slact", [128, TT], BF16)
        sgact = slact
        thh = _sb(nc, st, f"f{phase}_thh", [128, TT], BF16)
        cbsb = _sb(nc, st, f"f{phase}_cbsb", [128, TT], BF16)
        xout = [_sb(nc, st, f"f{phase}_xo{i}", [128, TT]) for i in range(2)]
        rtball = _sb(nc, st, f"f{phase}_rtball", [128, NCH * E], BF16)
        rlg = _sb(nc, st, f"f{phase}_rlg", [128, 4 * E])
        rsc = _sb(nc, st, f"f{phase}_rsc", [128, 4 * E])
        rsb = _sb(nc, st, f"f{phase}_rsb", [128, 4 * E])
        rsbm = _sb(nc, st, f"f{phase}_rsbm", [128, 4 * E])
        gm16 = _sb(nc, st, f"f{phase}_gm16", [128, 4 * E])
        rw = rlg
        combine = _sb(nc, st, f"f{phase}_comb", [128, 4 * E])
        rgm = _sb(nc, st, f"f{phase}_rgm", [128, 16])
        rt1 = _sb(nc, st, f"f{phase}_rt1", [128, 16])
        rt2 = _sb(nc, st, f"f{phase}_rt2", [128, 16])
        rt3 = _sb(nc, st, f"f{phase}_rt3", [128, 16])
        rt4 = _sb(nc, st, f"f{phase}_rt4", [128, 16])
        g4a = _sb(nc, st, f"f{phase}_g4a", [128, 4])
        g4b = _sb(nc, st, f"f{phase}_g4b", [128, 4])
        g4c = _sb(nc, st, f"f{phase}_g4c", [128, 4])
        g4d = _sb(nc, st, f"f{phase}_g4d", [128, 4])
        gthr = _sb(nc, st, f"f{phase}_gthr", [128, 4])
        thr = _sb(nc, st, f"f{phase}_thr", [128, 4])
        msk = _sb(nc, st, f"f{phase}_msk", [128, E])
        dnm = _sb(nc, st, f"f{phase}_dnm", [128, 4])
        cT = _sb(nc, st, f"f{phase}_cT", [16, TT], BF16)
        selcb = _sb(nc, st, f"f{phase}_selcb", [16, E * 128], BF16)

        p_g = _psum(nc, st, f"f{phase}_pg")
        p_u = _psum(nc, st, f"f{phase}_pu")
        p_cb = _psum(nc, st, f"f{phase}_pcb")
        p_mu = _psum(nc, st, f"f{phase}_pmu")
        p_is = _psum(nc, st, f"f{phase}_pis")
        p_d0 = _psum(nc, st, f"f{phase}_pd0")
        p_d1 = _psum(nc, st, f"f{phase}_pd1")
        p_r = _psum(nc, st, f"f{phase}_pr")

        tok_sel_ms = P.op("vector", lambda e: e.memset(selcb[:, :], 1.0),
                          inc=[("v", 1)])
        for e_i in range(E):
            P.op("gpsimd", lambda e, e_i=e_i: e.affine_select(
                out=selcb[:, e_i * 128:(e_i + 1) * 128],
                in_=selcb[:, e_i * 128:(e_i + 1) * 128],
                compare_op=ALU.is_equal, fill=0.0, base=-e_i,
                pattern=[[0, 128]], channel_multiplier=1),
                wait=[("v", tok_sel_ms)] if e_i == 0 else [])
        tok_selcb = P.op("gpsimd", lambda e: e.nop(), inc=[("g", 1)])
        tok_rt = P.op("sync", lambda e: e.dma_start(
            out=rtball.rearrange("p (c e) -> p c e", e=E),
            in_=env.routerTb[:, :].rearrange("(c p) e -> p c e", p=128)),
            wait=[env.cw("caF")], inc=[("rt", 16)])
        for e_i in range(E):
            P.op("sync", lambda e, e_i=e_i: e.dma_start(
                out=dws[:, e_i * 2048:(e_i + 1) * 2048].rearrange("p (k c) -> p k c", k=2),
                in_=env.dwb[e_i].rearrange("(k p) c -> p k c", p=128)),
                wait=[env.cw("caF"), ("t", free_dws)], inc=[("dwl", 16)])
        tok_dwl = P.val("dwl")

        free_x = [0, 0]
        free_sr = [0]
        free_ge = [0, 0]; free_ue = [0, 0]
        free_wsg = [0, 0]; free_wsu = [0, 0]; free_wsd = [0, 0]
        free_xo = [0, 0]
        c_rows = lambda mc: slice(mc * 128, (mc + 1) * 128)
        bz = env.tok_bcasts
        rd_pg = ("v", bz); rd_pu = ("v", bz); rd_pcb = ("v", bz)
        rd_pmuis = ("v", bz); rd_pr = ("v", bz)
        rd_pd = {0: ("v", bz), 1: ("v", bz)}
        tok_store_prev = 0
        tok_tensor_tile_prev = 0

        for i in range(NTILES):
            sl = i % 2
            rl = 0
            t0 = i * TT
            xsem = "xt%d" % sl
            tok_xld = P.op("sync", lambda e, sl=sl, t0=t0: e.dma_start(
                out=xt[sl].rearrange("p (c t) -> p c t", t=TT),
                in_=src[:, t0:t0 + TT].rearrange("(c p) t -> p c t", p=128)),
                wait=src_ready(i) + [("v", free_x[sl])], inc=[(xsem, 16)])
            P.op("sync", lambda e, rl=rl, t0=t0: e.dma_start(
                out=srow[rl][0:1, 0:TT], in_=env.srows[0:1, t0:t0 + TT]),
                wait=stats_done(i) + [("t", free_sr[rl])], inc=[("sr", 16)])
            tok_srld = P.op("sync", lambda e, rl=rl, t0=t0: e.dma_start(
                out=srow[rl][0:1, TT:2 * TT], in_=env.srows[1:2, t0:t0 + TT]),
                inc=[("sr", 16)])
            P.op("tensor", lambda t, rl=rl: t.matmul(
                p_mu[:, 0:TT], env.ones_row_f[:, :], srow[rl][0:1, 0:TT],
                start=True, stop=True),
                wait=[("sr", tok_srld), (rd_pmuis[0], rd_pmuis[1])])
            tok_bc = P.op("tensor", lambda t, rl=rl: t.matmul(
                p_is[:, 0:TT], env.ones_row_f[:, :], srow[rl][0:1, TT:2 * TT],
                start=True, stop=True), inc=[("t", 1)])
            free_sr[rl] = tok_bc
            for c in range(NCH):
                P.op("vector", lambda v, sl=sl, c=c: v.tensor_sub(
                    tmpf[:, :], xt[sl][:, c * TT:(c + 1) * TT], p_mu[:, 0:TT]),
                    wait=([("t", tok_bc), (xsem, tok_xld), ("t", tok_tensor_tile_prev)]
                          if c == 0 else []))
                P.op("vector", lambda v, c=c: v.tensor_mul(
                    xn[:, c * TT:(c + 1) * TT], tmpf[:, :], p_is[:, 0:TT]))
            tok_xn = P.op("vector", lambda v: v.engine_nop(), inc=[("v", 1)])
            rd_pmuis = ("v", tok_xn)

            tok_rl = 0
            for gtok in range(4):
                for c in range(NCH):
                    tok_rl = P.op("tensor", lambda t, gtok=gtok, c=c: t.matmul(
                        p_r[:, gtok * E:(gtok + 1) * E],
                        xn[:, c * TT + gtok * 128: c * TT + gtok * 128 + 128],
                        rtball[:, c * E:(c + 1) * E],
                        start=(c == 0), stop=(c == NCH - 1)),
                        wait=([("v", tok_xn), ("rt", tok_rt), (rd_pr[0], rd_pr[1])]
                              if (gtok == 0 and c == 0) else []),
                        inc=[("t", 1)] if (gtok == 3 and c == NCH - 1) else [])
            tok_tanh = P.op("scalar", lambda s: s.activation(
                rlg[:, :], p_r[:, 0:4 * E], AF.Tanh, scale=0.5),
                wait=[("t", tok_rl)], inc=[("s", 1)])
            rd_pr = ("s", tok_tanh)
            _lastv = [P.val("v")]
            def vv(fn, wait=(), **kw):
                tok = P.op("vector", fn, wait=list(wait) + [("v", _lastv[0])],
                           inc=[("v", 1)])
                _lastv[0] = tok
                return tok
            vv(lambda v: v.tensor_scalar(rsc[:, :], rlg[:, :], 0.5, 0.5, ALU.mult, ALU.add),
               wait=[("s", tok_tanh)])
            for gtok in range(4):
                vv(lambda v, gtok=gtok: v.tensor_add(
                    rsb[:, gtok * E:(gtok + 1) * E], rsc[:, gtok * E:(gtok + 1) * E],
                    env.cb128[:, :]))
            s3 = lambda ap, j: strided(ap[:, :], j, [[4, 16]])
            s4 = lambda ap, j: strided(ap[:, :], j, [[4, 4]])
            vv(lambda v: v.tensor_tensor(rt1[:, :], s3(rsb, 0), s3(rsb, 1), ALU.max))
            vv(lambda v: v.tensor_tensor(rt2[:, :], s3(rsb, 0), s3(rsb, 1), ALU.min))
            vv(lambda v: v.tensor_tensor(rt3[:, :], s3(rsb, 2), s3(rsb, 3), ALU.max))
            vv(lambda v: v.tensor_tensor(rt4[:, :], s3(rsb, 2), s3(rsb, 3), ALU.min))
            vv(lambda v: v.tensor_tensor(rgm[:, :], rt1[:, :], rt3[:, :], ALU.min))
            vv(lambda v: v.tensor_tensor(rt2[:, :], rt2[:, :], rt4[:, :], ALU.max))
            vv(lambda v: v.tensor_tensor(rgm[:, :], rgm[:, :], rt2[:, :], ALU.max))
            vv(lambda v: v.tensor_tensor(rt1[:, :], rt1[:, :], rt3[:, :], ALU.max))
            vv(lambda v: v.tensor_add(rgm[:, :], rt1[:, :], rgm[:, :]))
            vv(lambda v: v.tensor_tensor(g4a[:, :], s4(rgm, 0), s4(rgm, 1), ALU.max))
            vv(lambda v: v.tensor_tensor(g4b[:, :], s4(rgm, 0), s4(rgm, 1), ALU.min))
            vv(lambda v: v.tensor_tensor(g4c[:, :], s4(rgm, 2), s4(rgm, 3), ALU.max))
            vv(lambda v: v.tensor_tensor(g4d[:, :], s4(rgm, 2), s4(rgm, 3), ALU.min))
            vv(lambda v: v.tensor_tensor(gthr[:, :], g4a[:, :], g4c[:, :], ALU.min))
            vv(lambda v: v.tensor_tensor(g4b[:, :], g4b[:, :], g4d[:, :], ALU.max))
            vv(lambda v: v.tensor_tensor(gthr[:, :], gthr[:, :], g4b[:, :], ALU.max))
            for b in range(4):
                for j in range(4):
                    dstv = strided(gm16[:, :], b * 16 + j, [[4, 4]])
                    vv(lambda v, b=b, dstv=dstv: v.tensor_scalar(
                        dstv, rgm[:, b * 4:(b + 1) * 4], gthr[:, b:b + 1], None, ALU.is_ge))
            vv(lambda v: v.tensor_mul(rsbm[:, :], rsb[:, :], gm16[:, :]))
            vv(lambda v: v.tensor_scalar(gm16[:, :], gm16[:, :], 1.0e30, -1.0e30,
                                         ALU.mult, ALU.add))
            vv(lambda v: v.tensor_add(rsbm[:, :], rsbm[:, :], gm16[:, :]))
            vv(lambda v: v.tensor_copy(rw[:, :], rsbm[:, :]))
            for it in range(4):
                vv(lambda v: v.reduce_max(
                    thr.rearrange("p (b one) -> p b one", one=1),
                    rw.rearrange("p (b e) -> p b e", e=E), axis=AX.X))
                if it < 3:
                    for b in range(4):
                        vv(lambda v, b=b: v.tensor_scalar(
                            msk[:, :], rw[:, b * E:(b + 1) * E], thr[:, b:b + 1],
                            None, ALU.is_ge))
                        vv(lambda v, b=b: v.scalar_tensor_tensor(
                            rw[:, b * E:(b + 1) * E], msk[:, :], NEGBIG,
                            rw[:, b * E:(b + 1) * E], ALU.mult, ALU.add))
            for b in range(4):
                vv(lambda v, b=b: v.tensor_scalar(
                    combine[:, b * E:(b + 1) * E], rsbm[:, b * E:(b + 1) * E],
                    thr[:, b:b + 1], None, ALU.is_ge))
            vv(lambda v: v.tensor_mul(combine[:, :], combine[:, :], rsc[:, :]))
            vv(lambda v: v.reduce_sum(
                dnm.rearrange("p (b one) -> p b one", one=1),
                combine.rearrange("p (b e) -> p b e", e=E), axis=AX.X))
            vv(lambda v: v.tensor_scalar(dnm[:, :], dnm[:, :], 1e-20, None, ALU.add))
            vv(lambda v: v.reciprocal(dnm[:, :], dnm[:, :]))
            for b in range(4):
                vv(lambda v, b=b: v.tensor_scalar(
                    combine[:, b * E:(b + 1) * E], combine[:, b * E:(b + 1) * E],
                    dnm[:, b:b + 1], None, ALU.mult))
            tok_comb = P.op("vector", lambda v: v.engine_nop(), inc=[("v", 1)])
            for b in range(4):
                ttok = P.op("tensor", lambda t, b=b: t.transpose(
                    p_cb[0:E, 0:128], combine[:, b * E:(b + 1) * E], env.identf[:, :]),
                    wait=[("v", tok_comb), (rd_pcb[0], rd_pcb[1])] if b == 0
                    else [(rd_pcb[0], rd_pcb[1])], inc=[("t", 1)])
                stok = P.op("scalar", lambda s, b=b: s.copy(
                    cT[:, b * 128:(b + 1) * 128], p_cb[0:E, 0:128]),
                    wait=[("t", ttok)], inc=[("s", 1)])
                rd_pcb = ("s", stok)
            tok_cT = P.val("s")
            tok_hsh_parts = 0
            for mc in range(SH // 128):
                wslot = mc % 2
                tok_wg = P.op("sync", lambda e, mc=mc, wslot=wslot: e.dma_start(
                    out=wsg[wslot].rearrange("p (c x) -> p c x", x=128),
                    in_=env.wsgTb[mc].rearrange("(c p) x -> p c x", p=128)),
                    wait=[env.cw("caF"), ("t", free_wsg[wslot])],
                    inc=[("wg%d" % wslot, 16)])
                tok_wu = P.op("sync", lambda e, mc=mc, wslot=wslot: e.dma_start(
                    out=wsu[wslot].rearrange("p (c x) -> p c x", x=128),
                    in_=env.wsuTb[mc].rearrange("(c p) x -> p c x", p=128)),
                    wait=[env.cw("caF"), ("t", free_wsu[wslot])],
                    inc=[("wu%d" % wslot, 16)])
                tok_gmm = 0
                for c in range(NCH):
                    tok_gmm = P.op("tensor", lambda t, c=c, wslot=wslot: t.matmul(
                        p_g[:, 0:TT], wsg[wslot][:, c * 128:(c + 1) * 128],
                        xn[:, c * TT:(c + 1) * TT],
                        start=(c == 0), stop=(c == NCH - 1)),
                        wait=([("wg%d" % wslot, tok_wg), ("v", tok_xn),
                               (rd_pg[0], rd_pg[1])] if c == 0 else []),
                        inc=[("t", 1)] if c == NCH - 1 else [])
                free_wsg[wslot] = tok_gmm
                tok_sg = P.op("scalar", lambda s: s.activation(
                    sgact[:, :], p_g[:, 0:TT], AF.Silu),
                    wait=[("t", tok_gmm), ("v", tok_hsh_parts)], inc=[("s", 1)])
                rd_pg = ("s", tok_sg)
                tok_umm = 0
                for c in range(NCH):
                    tok_umm = P.op("tensor", lambda t, c=c, wslot=wslot: t.matmul(
                        p_u[:, 0:TT], wsu[wslot][:, c * 128:(c + 1) * 128],
                        xn[:, c * TT:(c + 1) * TT],
                        start=(c == 0), stop=(c == NCH - 1)),
                        wait=([("wu%d" % wslot, tok_wu), (rd_pu[0], rd_pu[1])]
                              if c == 0 else []),
                        inc=[("t", 1)] if c == NCH - 1 else [])
                free_wsu[wslot] = tok_umm
                tok_hsh_parts = P.op("vector", lambda v, mc=mc: v.scalar_tensor_tensor(
                    hsh[:, mc * TT:(mc + 1) * TT], p_u[:, 0:TT], 1.0, sgact[:, :],
                    ALU.mult, ALU.mult),
                    wait=[("t", tok_umm), ("s", tok_sg)], inc=[("v", 1)])
                rd_pu = ("v", tok_hsh_parts)
            tok_hsh = tok_hsh_parts

            tok_h_parts = 0
            for e_i in range(E):
                wslot = e_i % 2
                tok_ge = P.op("sync", lambda e, e_i=e_i, wslot=wslot: e.dma_start(
                    out=geb[wslot].rearrange("p (h c x) -> p h c x", h=2, x=128),
                    in_=env.gwb[e_i].rearrange("h (c p) x -> p h c x", p=128)),
                    wait=[env.cw("caF"), ("t", free_ge[wslot])],
                    inc=[("ge%d" % wslot, 16)])
                tok_ue = P.op("sync", lambda e, e_i=e_i, wslot=wslot: e.dma_start(
                    out=ueb[wslot].rearrange("p (h c x) -> p h c x", h=2, x=128),
                    in_=env.uwb[e_i].rearrange("h (c p) x -> p h c x", p=128)),
                    wait=[env.cw("caF"), ("t", free_ue[wslot])],
                    inc=[("ue%d" % wslot, 16)])
                tok_cbmm = P.op("tensor", lambda t, e_i=e_i: t.matmul(
                    p_cb[:, 0:TT], selcb[:, e_i * 128:(e_i + 1) * 128], cT[0:16, 0:TT],
                    start=True, stop=True),
                    wait=[("s", tok_cT), ("g", tok_selcb), (rd_pcb[0], rd_pcb[1])],
                    inc=[("t", 1)])
                tok_cbev = P.op("scalar", lambda s: s.copy(cbsb[:, :], p_cb[:, 0:TT]),
                                wait=[("t", tok_cbmm), ("v", tok_h_parts)], inc=[("s", 1)])
                rd_pcb = ("s", tok_cbev)
                for hc in range(2):
                    tok_gmm = 0
                    for c in range(NCH):
                        tok_gmm = P.op("tensor", lambda t, c=c, hc=hc, wslot=wslot: t.matmul(
                            p_g[:, 0:TT],
                            geb[wslot][:, hc * 1024 + c * 128: hc * 1024 + c * 128 + 128],
                            xn[:, c * TT:(c + 1) * TT],
                            start=(c == 0), stop=(c == NCH - 1)),
                            wait=([("ge%d" % wslot, tok_ge), (rd_pg[0], rd_pg[1])]
                                  if c == 0 else []),
                            inc=[("t", 1)] if c == NCH - 1 else [])
                    if hc == 1:
                        free_ge[wslot] = tok_gmm
                    tok_sl = P.op("scalar", lambda s, hc=hc: s.activation(
                        slact[:, :], p_g[:, 0:TT], AF.Silu),
                        wait=[("t", tok_gmm), ("v", tok_h_parts)], inc=[("s", 1)])
                    rd_pg = ("s", tok_sl)
                    tok_umm = 0
                    for c in range(NCH):
                        tok_umm = P.op("tensor", lambda t, c=c, hc=hc, wslot=wslot: t.matmul(
                            p_u[:, 0:TT],
                            ueb[wslot][:, hc * 1024 + c * 128: hc * 1024 + c * 128 + 128],
                            xn[:, c * TT:(c + 1) * TT],
                            start=(c == 0), stop=(c == NCH - 1)),
                            wait=([("ue%d" % wslot, tok_ue), (rd_pu[0], rd_pu[1])]
                                  if c == 0 else []),
                            inc=[("t", 1)] if c == NCH - 1 else [])
                    if hc == 1:
                        free_ue[wslot] = tok_umm
                    tok_last_xn_rd = tok_umm
                    P.op("vector", lambda v, hc=hc: v.scalar_tensor_tensor(
                        thh[:, :], p_u[:, 0:TT], 1.0, slact[:, :],
                        ALU.mult, ALU.mult),
                        wait=[("t", tok_umm), ("s", max(tok_sl, tok_cbev))])
                    tok_h_parts = P.op("vector", lambda v, e_i=e_i, hc=hc: v.tensor_mul(
                        hall[:, (e_i * 2 + hc) * TT:(e_i * 2 + hc + 1) * TT],
                        thh[:, :], cbsb[:, :]), inc=[("v", 1)])
                    rd_pu = ("v", tok_h_parts)
            tok_hall = tok_h_parts

            tok_last_dmm = 0
            for mc in range(NCH):
                dslot = mc % 2
                pd = p_d0 if dslot == 0 else p_d1
                tok_wd = P.op("sync", lambda e, mc=mc, dslot=dslot: e.dma_start(
                    out=wsd[dslot].rearrange("p (c x) -> p c x", x=128),
                    in_=env.wsdTb[mc].rearrange("(c p) x -> p c x", p=128)),
                    wait=[env.cw("caF"), ("t", free_wsd[dslot])],
                    inc=[("wd%d" % dslot, 16)])
                for kc in range(SH // 128):
                    P.op("tensor", lambda t, kc=kc, dslot=dslot, pd=pd: t.matmul(
                        pd[:, 0:TT], wsd[dslot][:, kc * 128:(kc + 1) * 128],
                        hsh[:, kc * TT:(kc + 1) * TT],
                        start=(kc == 0), stop=False),
                        wait=([("wd%d" % dslot, tok_wd), ("v", tok_hsh),
                               (rd_pd[dslot][0], rd_pd[dslot][1])] if kc == 0 else []))
                tok_dmm = 0
                for e_i in range(E):
                    for kc in range(2):
                        last = (e_i == E - 1 and kc == 1)
                        tok_dmm = P.op("tensor", lambda t, e_i=e_i, kc=kc, mc=mc, last=last,
                                       pd=pd: t.matmul(
                                           pd[:, 0:TT],
                                           dws[:, e_i * 2048 + kc * 1024 + mc * 128:
                                                e_i * 2048 + kc * 1024 + mc * 128 + 128],
                                           hall[:, (e_i * 2 + kc) * TT:(e_i * 2 + kc + 1) * TT],
                                           start=False, stop=last),
                                       wait=([("v", tok_hall), ("dwl", tok_dwl)]
                                             if (e_i == 0 and kc == 0 and mc == 0) else []),
                                       inc=[("t", 1)] if last else [])
                free_wsd[dslot] = tok_dmm
                tok_last_dmm = tok_dmm
                tok_ev = P.op("vector", lambda v, mc=mc, sl=sl, pd=pd: v.scalar_tensor_tensor(
                    xout[mc % 2][:, :], pd[:, 0:TT], 1.0,
                    xt[sl][:, mc * TT:(mc + 1) * TT], ALU.mult, ALU.add),
                    wait=[("t", tok_dmm), ("xo%d" % (mc % 2), free_xo[mc % 2])],
                    inc=[("v", 1)])
                rd_pd[dslot] = ("v", tok_ev)
                free_xo[mc % 2] = P.op("gpsimd", lambda e, mc=mc, t0=t0: e.dma_start(
                    out=dst[c_rows(mc), t0:t0 + TT], in_=xout[mc % 2][:, :]),
                    wait=[("v", tok_ev)], inc=[("xo%d" % (mc % 2), 16)])
            free_x[sl] = P.val("v")
            tok_tensor_tile_prev = tok_last_dmm
            store_toks[i] = [("xo0", free_xo[0]), ("xo1", free_xo[1])]
    return store_toks, P.val("t")


def emit_barrier(P):
    snap = [(s, P.val(s)) for s in P.counts]
    for e in ENGINES:
        P.op(e, lambda x: x.nop(), wait=snap)





def emit_attn_a(P, nc, stack, env, src, dst, src_ready, stats_done):
    store_toks = {}
    NKC = T // 128
    NQG = T // TT
    with contextlib.ExitStack() as st:
        masksM = _sb(nc, st, "aa_masksM", [128, 4 * TT], BF16)
        selz = _sb(nc, st, "aa_selz", [128, NCH * 128])
        kts = _sb(nc, st, "aa_kts", [128, NCH * T], BF16)
        vts = _sb(nc, st, "aa_vts", [128, NKC * 16 * 80], BF16)
        warena = [_sb(nc, st, f"aa_w{i}", [128, NCH * 1024], BF16) for i in range(3)]
        xt = _sb(nc, st, "aa_xt", [128, NCH * TT])
        xn = _sb(nc, st, "aa_xn", [128, NCH * TT], BF16)
        qstage = _sb(nc, st, "aa_qst", [128, TT], BF16)
        qsl = _sb(nc, st, "aa_qsl", [128, NCH * TT], BF16)
        pt = [_sb(nc, st, f"aa_pt{i}", [128, 2 * TT], BF16) for i in range(2)]
        oraw = _sb(nc, st, "aa_oraw", [128, NCH * TT], BF16)
        onrm = _sb(nc, st, "aa_onrm", [128, NCH * TT], BF16)
        zbuf = _sb(nc, st, "aa_zbuf", [128, TT])
        srow = _sb(nc, st, "aa_srow", [1, 2 * TT])
        xo = [_sb(nc, st, f"aa_xo{i}", [128, TT]) for i in range(2)]
        bva_row = _sb(nc, st, "aa_bvar", [1, D])
        bva128 = _sb(nc, st, "aa_bva", [128, D])
        pd = [st.enter_context(nc.psum_tensor(f"aa_pd{i}", [128, 1024], F32))
              for i in range(2)]
        ps4 = [_psum(nc, st, f"aa_p{i}") for i in range(4)]
        ps = [pd[0][:, 0:512], pd[0][:, 512:1024],
              pd[1][:, 0:512], pd[1][:, 512:1024],
              ps4[0][:, :], ps4[1][:, :], ps4[2][:, :], ps4[3][:, :]]

        tok_bv = P.op("sync", lambda e: e.dma_start(out=bva_row[:, :], in_=env.ba["v"][:, :]),
                      inc=[("bv", 16)])
        tok_bvv = env.tok_bcasts
        for half in range(2):
            tb = P.op("tensor", lambda t, half=half: t.matmul(
                ps[7][:, 0:TT], env.ones_row_f[:, :],
                bva_row[0:1, half * TT:(half + 1) * TT], start=True, stop=True),
                wait=[("bv", tok_bv), ("g", env.tok_consts), ("v", tok_bvv)],
                inc=[("t", 1)])
            tok_bvv = P.op("vector", lambda v, half=half: v.tensor_copy(
                bva128[:, half * TT:(half + 1) * TT], ps[7][:, 0:TT]),
                wait=[("t", tb)], inc=[("v", 1)])
        rd_p7_init = ("v", tok_bvv)

        tok_ms = P.op("vector", lambda v: v.memset(masksM[:, :], 0.0), inc=[("v", 1)])
        for j in range(4):
            off = j * 128
            P.op("gpsimd", lambda e, j=j, off=off: e.affine_select(
                out=masksM[:, j * TT:(j + 1) * TT], in_=masksM[:, j * TT:(j + 1) * TT],
                compare_op=ALU.is_ge, fill=1.0, base=off - 1,
                pattern=[[-1, TT]], channel_multiplier=1),
                wait=[("v", tok_ms)] if j == 0 else [])
        tok_sz = P.op("vector", lambda v: v.memset(selz[:, :], 1.0), inc=[("v", 1)])
        first_sel = True
        for c in range(NCH):
            for half in range(2):
                h = 2 * c + half
                P.op("gpsimd", lambda e, c=c, half=half, h=h: e.affine_select(
                    out=selz[:, c * 128 + half * 64: c * 128 + half * 64 + 64],
                    in_=selz[:, c * 128 + half * 64: c * 128 + half * 64 + 64],
                    compare_op=ALU.is_equal, fill=0.0, base=-(64 + h),
                    pattern=[[0, 64]], channel_multiplier=1),
                    wait=[("v", tok_sz)] if first_sel else [])
                first_sel = False
        tok_consts = P.op("gpsimd", lambda e: e.nop(), inc=[("g", 1)])

        P.op("vector", lambda v: v.memset(vts[:, :], 0.0))
        for h in range(NH):
            base0 = vts[:, :]
            ap = bass.AP(base0.tensor, base0.offset + h * 80 + 64 + h,
                         [list(base0.ap[0]), [1280, NKC], [1, 1]])
            P.op("vector", lambda v, ap=ap: v.memset(ap, 1.0))
        tok_vts_ones = P.op("vector", lambda v: v.engine_nop(), inc=[("v", 1)])

        kstage = [_sb(nc, st, f"aa_kst{i}", [128, TT], BF16) for i in range(2)]
        p1done = [0]

        def kv_gen():
            free_kst = [0, 0]
            last_mm = [0]
            rd_kv = ("v", env.tok_bcasts)
            for which, wsrc, kvsrc, kvdst, bname in (
                    (0, env.wbb["k"], env.keyTb, env.kbs, "bb_k"),
                    (1, env.wbb["v"], env.valTb, env.vbs, "bb_v")):
                tok_wl = P.op("sync", lambda e, wsrc=wsrc: e.dma_start(
                    out=warena[1].rearrange("p (c x) -> p c x", x=1024),
                    in_=wsrc.rearrange("(c p) x -> p c x", p=128)),
                    wait=[env.cw("caB"), ("t", max(last_mm[0], p1done[0]))],
                    inc=[("kw1", 16)])
                for rt in range(2 * T // TT):
                    tok_kv = P.op("sync", lambda e, kvsrc=kvsrc, rt=rt: e.dma_start(
                        out=warena[2][:, 0:NCH * TT].rearrange("p (c t) -> p c t", t=TT),
                        in_=kvsrc[:, rt * TT:(rt + 1) * TT]
                        .rearrange("(c p) t -> p c t", p=128)),
                        wait=[env.cw("caB"), ("t", max(last_mm[0], p1done[0]))],
                        inc=[("kv", 16)])
                    for mc in range(NCH):
                        tok_mm = 0
                        for c in range(NCH):
                            tok_mm = P.op("tensor", lambda t, mc=mc, c=c: t.matmul(
                                ps[6][:, 0:TT],
                                warena[1][:, c * 1024 + mc * 128: c * 1024 + mc * 128 + 128],
                                warena[2][:, c * TT:(c + 1) * TT],
                                start=(c == 0), stop=(c == NCH - 1)),
                                wait=([("kw1", tok_wl), ("kv", tok_kv),
                                       (rd_kv[0], rd_kv[1])] if c == 0 else []),
                                inc=[("t", 1)] if c == NCH - 1 else [])
                        last_mm[0] = tok_mm
                        ksl2 = mc % 2
                        tok_ev = P.op("scalar", lambda sc, mc=mc, bname=bname,
                                      ksl2=ksl2: sc.activation(
                            kstage[ksl2][:, :], ps[6][:, 0:TT], AF.Identity,
                            bias=env.bcol[bname][:, mc:mc + 1]),
                            wait=[("t", tok_mm), ("kvs", free_kst[ksl2])], inc=[("s", 1)])
                        rd_kv = ("s", tok_ev)
                        free_kst[ksl2] = P.op(
                            "gpsimd", lambda e, kvdst=kvdst, mc=mc, rt=rt, ksl2=ksl2:
                            e.dma_start(
                                out=kvdst[mc * 128:(mc + 1) * 128, rt * TT:(rt + 1) * TT],
                                in_=kstage[ksl2][:, :]),
                            wait=[("s", P.val("s"))], inc=[("kvs", 16)])
                        yield
            while True:
                yield

        kvg = kv_gen()
        free_w = [0, 0, 0]
        free_qst = 0
        free_xo = [0, 0]
        free_xt = 0
        free_qsl = 0
        rd_ps = [("v", env.tok_bcasts)] * 8
        rd_ps[7] = rd_p7_init
        rd_pd = [("v", env.tok_bcasts), ("v", env.tok_bcasts)]
        rd_ps_av = list(rd_ps)
        rd_pt = [0, 0]
        tok_kts_rd = 0
        tok_vts_rd = 0
        tok_xn_rd = 0
        tok_srow_rd = 0
        for s_i in range(B_LOC):
            base = s_i * T
            tok_w = [0, 0, 0]
            for wi, wname in enumerate(("q", "k", "v")):
                tok_w[wi] = P.op("sync", lambda e, wi=wi, wname=wname: e.dma_start(
                    out=warena[wi].rearrange("p (c x) -> p c x", x=1024),
                    in_=env.wab[wname].rearrange("(c p) x -> p c x", p=128)),
                    wait=[env.cw("caA"), ("t", free_w[wi])], inc=[("kw%d" % wi, 16)])
            tok_kts_wr = 0
            tok_vts_wr = 0
            for j in range(NQG):
                t0 = base + j * TT
                tok_xld = P.op("sync", lambda e, t0=t0: e.dma_start(
                    out=xt.rearrange("p (c t) -> p c t", t=TT),
                    in_=src[:, t0:t0 + TT].rearrange("(c p) t -> p c t", p=128)),
                    wait=src_ready(t0 // TT) + [("v", free_xt)], inc=[("ax", 16)])
                P.op("sync", lambda e, t0=t0: e.dma_start(
                    out=srow[0:1, 0:TT], in_=env.srows[0:1, t0:t0 + TT]),
                    wait=stats_done(t0 // TT) + [("t", tok_srow_rd)], inc=[("sr", 16)])
                tok_sr = P.op("sync", lambda e, t0=t0: e.dma_start(
                    out=srow[0:1, TT:2 * TT], in_=env.srows[1:2, t0:t0 + TT]),
                    inc=[("sr", 16)])
                P.op("tensor", lambda t: t.matmul(
                    ps[0][:, 0:TT], env.ones_row_f[:, :], srow[0:1, 0:TT],
                    start=True, stop=True),
                    wait=[("sr", tok_sr), (rd_ps[0][0], rd_ps[0][1])])
                tok_bc = P.op("tensor", lambda t: t.matmul(
                    ps[1][:, 0:TT], env.ones_row_f[:, :], srow[0:1, TT:2 * TT],
                    start=True, stop=True),
                    wait=[(rd_ps[1][0], rd_ps[1][1])], inc=[("t", 1)])
                tok_srow_rd = tok_bc
                for c in range(NCH):
                    P.op("vector", lambda v, c=c: v.tensor_sub(
                        xo[0][:, :], xt[:, c * TT:(c + 1) * TT], ps[0][:, 0:TT]),
                        wait=([("t", tok_bc), ("ax", tok_xld), ("t", tok_xn_rd),
                               ("xo0", free_xo[0])] if c == 0 else []))
                    P.op("vector", lambda v, c=c: v.tensor_mul(
                        xn[:, c * TT:(c + 1) * TT], xo[0][:, :], ps[1][:, 0:TT]))
                tok_xn = P.op("vector", lambda v: v.engine_nop(), inc=[("v", 1)])
                free_xt = tok_xn
                rd_ps[0] = ("v", tok_xn)
                rd_ps[1] = ("v", tok_xn)
                for wi in range(2):
                    for mc in range(NCH):
                        pb = 2 + (mc % 2)
                        tok_mm = 0
                        for c in range(NCH):
                            tok_mm = P.op("tensor", lambda t, wi=wi, mc=mc, c=c, pb=pb: t.matmul(
                                ps[pb][:, 0:TT],
                                warena[wi][:, c * 1024 + mc * 128: c * 1024 + mc * 128 + 128],
                                xn[:, c * TT:(c + 1) * TT],
                                start=(c == 0), stop=(c == NCH - 1)),
                                wait=([("kw%d" % wi, tok_w[wi]), ("v", tok_xn),
                                       (rd_ps[pb][0], rd_ps[pb][1])] if c == 0 else []),
                                inc=[("t", 1)] if c == NCH - 1 else [])
                        if wi == 0:
                            tok_ev = P.op("scalar", lambda sc, mc=mc, pb=pb: sc.activation(
                                qstage[:, :], ps[pb][:, 0:TT], AF.Identity,
                                bias=env.bcol["ba_q"][:, mc:mc + 1]),
                                wait=[("t", tok_mm), ("qst", free_qst)], inc=[("s", 1)])
                            rd_ps[pb] = ("s", tok_ev)
                            free_qst = P.op("gpsimd", lambda e, mc=mc, j=j: e.dma_start(
                                out=env.qscr[mc * 128:(mc + 1) * 128, j * TT:(j + 1) * TT],
                                in_=qstage[:, :]),
                                wait=[("s", tok_ev)] + ([("qsl", P.val("qsl"))]
                                                        if (j == 0 and mc == 0) else []),
                                inc=[("qst", 16)])
                        else:
                            tok_ev = P.op("scalar", lambda sc, mc=mc, pb=pb, j=j: sc.activation(
                                kts[:, mc * T + j * TT: mc * T + (j + 1) * TT],
                                ps[pb][:, 0:TT], AF.Identity,
                                bias=env.bcol["ba_k"][:, mc:mc + 1]),
                                wait=[("t", tok_mm)] + ([("t", tok_kts_rd)]
                                                        if (mc == 0 and j == 0) else []),
                                inc=[("s", 1)])
                            rd_ps[pb] = ("s", tok_ev)
                tok_kts_wr = P.val("s")
                for tsl in range(4):
                    kcv = j * 4 + tsl
                    for half in range(2):
                        pb = 4 + half
                        tok_mm = 0
                        for c in range(NCH):
                            tok_mm = P.op("tensor", lambda t, tsl=tsl, half=half, c=c, pb=pb: t.matmul(
                                ps[pb][:, 0:TT],
                                xn[:, c * TT + tsl * 128: c * TT + tsl * 128 + 128],
                                warena[2][:, c * 1024 + half * 512: c * 1024 + (half + 1) * 512],
                                start=(c == 0), stop=(c == NCH - 1)),
                                wait=([("kw2", tok_w[2]), (rd_ps[pb][0], rd_ps[pb][1]),
                                       ("v", tok_vts_ones)] if c == 0 else []),
                                inc=[("t", 1)] if c == NCH - 1 else [])
                        dstb = vts[:, :]
                        dap = bass.AP(dstb.tensor, dstb.offset + kcv * 1280 + half * 8 * 80,
                                      [list(dstb.ap[0]), [80, 8], [1, 64]])
                        tok_ev = P.op("vector", lambda v, pb=pb, dap=dap, half=half: v.tensor_add(
                            dap, ps[pb][:, 0:TT].rearrange("p (h d) -> p h d", d=64),
                            bva128[:, half * 512:(half + 1) * 512]
                            .rearrange("p (h d) -> p h d", d=64)),
                            wait=[("t", tok_mm)] + ([("t", tok_vts_rd)]
                                                    if (j == 0 and tsl == 0 and half == 0)
                                                    else []),
                            inc=[("v", 1)])
                        rd_ps[pb] = ("v", tok_ev)
                tok_vts_wr = P.val("v")
                tok_xn_rd = P.val("t")
            rd_pd[0] = rd_ps[0]
            rd_pd[1] = (rd_ps[2][0], max(rd_ps[2][1], rd_ps[3][1]))
            rd_ps_av[4] = rd_ps[4]
            rd_ps_av[5] = rd_ps[5]
            p1done[0] = P.val("t")
            tok_w3 = P.op("sync", lambda e: e.dma_start(
                out=warena[0].rearrange("p (c x) -> p c x", x=1024),
                in_=env.wab["o"].rearrange("(c p) x -> p c x", p=128)),
                wait=[env.cw("caA"), ("t", P.val("t"))], inc=[("kw0", 16)])
            tok_zb_rd = 0
            for qg in range(NQG):
                n_kc = 4 * (qg + 1)
                tok_q = 0
                for h in range(NH):
                    tok_q = P.op("sync", lambda e, h=h, qg=qg: e.dma_start(
                        out=qsl[(h % 2) * 64:(h % 2) * 64 + 64,
                                (h // 2) * TT:(h // 2 + 1) * TT],
                        in_=env.qscr[h * 64:(h + 1) * 64, qg * TT:(qg + 1) * TT]),
                        wait=([("qst", free_qst), ("t", free_qsl)] if h == 0 else []),
                        inc=[("qsl", 16)])
                tok_xld = P.op("sync", lambda e, qg=qg, base=base: e.dma_start(
                    out=xt.rearrange("p (c t) -> p c t", t=TT),
                    in_=src[:, base + qg * TT: base + (qg + 1) * TT]
                    .rearrange("(c p) t -> p c t", p=128)),
                    wait=[("v", free_xt)], inc=[("ax", 16)])
                tok_zz = P.op("vector", lambda v: v.memset(zbuf[64:80, 0:TT], 0.0),
                              wait=[("t", tok_zb_rd)], inc=[("v", 1)])
                for h in range(NH):
                    pb_av = 4 + (h % 2)
                    n_pair = n_kc // 2
                    tok_exp = [0] * n_pair
                    tok_av = 0

                    def emit_sc_pair(pr):
                        X = pr % 2
                        tok_sc = 0
                        for sub in range(2):
                            kcv = 2 * pr + sub
                            tok_sc = P.op("tensor", lambda t, h=h, kcv=kcv, X=X, sub=sub: t.matmul(
                                pd[X][:, sub * TT:(sub + 1) * TT],
                                kts[(h % 2) * 64:(h % 2) * 64 + 64,
                                    (h // 2) * T + kcv * 128: (h // 2) * T + kcv * 128 + 128],
                                qsl[(h % 2) * 64:(h % 2) * 64 + 64,
                                    (h // 2) * TT:(h // 2 + 1) * TT],
                                start=True, stop=True),
                                wait=([("qsl", tok_q), ("s", tok_kts_wr)]
                                      if (kcv == 0 and h == 0) else [])
                                + ([(rd_pd[X][0], rd_pd[X][1])] if sub == 0 else []),
                                inc=[("t", 1)])
                        tok_exp[pr] = P.op("scalar", lambda sc, X=X: sc.activation(
                            pt[X][:, :], pd[X][:, 0:2 * TT], AF.Exp,
                            bias=env.negC[:, 0:1], scale=0.125),
                            wait=[("t", tok_sc), ("t", rd_pt[X])], inc=[("s", 1)])
                        rd_pd[X] = ("s", tok_exp[pr])
                        if 2 * pr >= qg * 4:
                            off0 = 2 * pr - qg * 4
                            tok_exp[pr] = ("m", P.op("vector",
                                lambda v, X=X, off0=off0: v.tensor_mul(
                                    pt[X][:, 0:2 * TT], pt[X][:, 0:2 * TT],
                                    masksM[:, off0 * TT:(off0 + 2) * TT]),
                                wait=[("s", tok_exp[pr]), ("g", tok_consts)],
                                inc=[("v", 1)]))

                    def emit_av_pair(pr):
                        nonlocal tok_av
                        X = pr % 2
                        for sub in range(2):
                            kcv = 2 * pr + sub
                            te = tok_exp[pr]
                            wexp = [("v", te[1])] if isinstance(te, tuple) else [("s", te)]
                            tok_av = P.op("tensor", lambda t, h=h, kcv=kcv, X=X, sub=sub,
                                          pb_av=pb_av,
                                          first=(kcv == 0), last=(kcv == n_kc - 1): t.matmul(
                                              ps[pb_av][0:80, 0:TT],
                                              vts[:, kcv * 1280 + h * 80: kcv * 1280 + h * 80 + 80],
                                              pt[X][:, sub * TT:(sub + 1) * TT],
                                              start=first, stop=last),
                                          wait=wexp
                                          + ([(rd_ps_av[pb_av][0], rd_ps_av[pb_av][1])]
                                             if kcv == 0 else [])
                                          + ([("v", tok_vts_wr)] if (kcv == 0 and h == 0) else []),
                                          inc=[("t", 1)])
                        rd_pt[X] = tok_av

                    skew = 2 if n_pair >= 3 else 1
                    for pr in range(skew):
                        emit_sc_pair(pr)
                    for pr in range(skew, n_pair):
                        emit_sc_pair(pr)
                        emit_av_pair(pr - skew)
                    for pr in range(n_pair - skew, n_pair):
                        emit_av_pair(pr)
                    tok_z = P.op("vector", lambda v, pb_av=pb_av: v.tensor_add(
                        zbuf[64:80, 0:TT], zbuf[64:80, 0:TT], ps[pb_av][64:80, 0:TT]),
                        wait=[("t", tok_av), ("v", tok_zz)], inc=[("v", 1)])
                    tok_oraw = P.op("vector", lambda v, h=h, pb_av=pb_av: v.tensor_copy(
                        oraw[(h % 2) * 64:(h % 2) * 64 + 64, (h // 2) * TT:(h // 2 + 1) * TT],
                        ps[pb_av][0:64, 0:TT]),
                        wait=[("v", tok_z)], inc=[("v", 1)])
                    rd_ps_av[pb_av] = ("v", tok_oraw)
                    next(kvg)
                free_qsl = P.val("t")
                tz1 = P.op("scalar", lambda sc: sc.activation(
                    zbuf[64:80, 0:TT], zbuf[64:80, 0:TT], AF.Ln),
                    wait=[("v", P.val("v"))], inc=[("s", 1)])
                tz2 = P.op("scalar", lambda sc: sc.activation(
                    zbuf[64:80, 0:TT], zbuf[64:80, 0:TT], AF.Exp, scale=-1.0),
                    wait=[("s", tz1)], inc=[("s", 1)])
                for c in range(NCH):
                    pb = c % 2
                    tok_zb = P.op("tensor", lambda t, c=c, pb=pb: t.matmul(
                        ps[pb][:, 0:TT], selz[64:80, c * 128:(c + 1) * 128],
                        zbuf[64:80, 0:TT], start=True, stop=True),
                        wait=[("s", tz2), ("g", tok_consts),
                              (rd_pd[pb][0], rd_pd[pb][1])], inc=[("t", 1)])
                    tok_on = P.op("vector", lambda v, c=c, pb=pb: v.scalar_tensor_tensor(
                        onrm[:, c * TT:(c + 1) * TT], ps[pb][:, 0:TT], 1.0,
                        oraw[:, c * TT:(c + 1) * TT], ALU.mult, ALU.mult),
                        wait=[("t", tok_zb)], inc=[("v", 1)])
                    rd_pd[pb] = ("v", tok_on)
                tok_on = P.val("v")
                tok_zb_rd = P.val("t")
                for mc in range(NCH):
                    pb = 7 if mc % 2 == 0 else 4
                    tok_mm = 0
                    for c in range(NCH):
                        tok_mm = P.op("tensor", lambda t, mc=mc, c=c, pb=pb: t.matmul(
                            ps[pb][:, 0:TT],
                            warena[0][:, c * 1024 + mc * 128: c * 1024 + mc * 128 + 128],
                            onrm[:, c * TT:(c + 1) * TT],
                            start=(c == 0), stop=(c == NCH - 1)),
                            wait=([("kw0", tok_w3), ("v", tok_on),
                                   (rd_ps_av[pb][0], rd_ps_av[pb][1])] if c == 0 else []),
                            inc=[("t", 1)] if c == NCH - 1 else [])
                    tok_ev = P.op("vector", lambda v, mc=mc, pb=pb: v.scalar_tensor_tensor(
                        xo[mc % 2][:, :], ps[pb][:, 0:TT], env.bcol["ba_o"][:, mc:mc + 1],
                        xt[:, mc * TT:(mc + 1) * TT], ALU.add, ALU.add),
                        wait=[("t", tok_mm), ("ax", tok_xld),
                              ("xo%d" % (mc % 2), free_xo[mc % 2])], inc=[("v", 1)])
                    rd_ps_av[pb] = ("v", tok_ev)
                    free_xo[mc % 2] = P.op("gpsimd", lambda e, mc=mc, qg=qg, base=base: e.dma_start(
                        out=dst[mc * 128:(mc + 1) * 128, base + qg * TT: base + (qg + 1) * TT],
                        in_=xo[mc % 2][:, :]),
                        wait=[("v", tok_ev)], inc=[("xo%d" % (mc % 2), 16)])
                free_xt = P.val("v")
                store_toks[4 * s_i + qg] = [("xo0", free_xo[0]), ("xo1", free_xo[1])]
            tok_kts_rd = P.val("t")
            tok_vts_rd = P.val("t")
            tok_xn_rd = P.val("t")
            free_w = [P.val("t")] * 3
            rd_ps[0] = rd_pd[0]
            rd_ps[1] = rd_pd[0]
            rd_ps[2] = rd_pd[1]
            rd_ps[3] = rd_pd[1]
            for _b in range(4, 8):
                rd_ps[_b] = rd_ps_av[_b]
        env.kv_done = ("kvs", P.val("kvs"))
    return store_toks





def emit_attn_b(P, nc, stack, env, src, dst, src_ready, stats_done):
    store_toks = {}
    with contextlib.ExitStack() as st:
        warena = _sb(nc, st, "ab_w", [128, NCH * 1024], BF16)
        xn = _sb(nc, st, "ab_xn", [128, NCH * TT], BF16)
        qb = _sb(nc, st, "ab_qb", [128, NCH * TT], BF16)
        kv0 = _sb(nc, st, "ab_kv0", [128, NCH * TT], BF16)
        kv1 = _sb(nc, st, "ab_kv1", [128, NCH * TT], BF16)
        vv0 = _sb(nc, st, "ab_vv0", [128, NCH * TT], BF16)
        vv1 = _sb(nc, st, "ab_vv1", [128, NCH * TT], BF16)
        tmp = _sb(nc, st, "ab_tmp", [128, NCH * TT], BF16)
        tmp2 = _sb(nc, st, "ab_tmp2", [128, NCH * TT], BF16)
        pb2 = _sb(nc, st, "ab_pb", [128, 4 * TT], BF16)
        o3s = _sb(nc, st, "ab_o3", [128, NCH * TT], BF16)
        sq = _sb(nc, st, "ab_sq", [128, NCH * TT], BF16)
        hblk = _sb(nc, st, "ab_hblk", [128, 128], BF16)
        srow = [_sb(nc, st, f"ab_srow{i}", [1, 2 * TT]) for i in range(2)]
        rrst = _sb(nc, st, "ab_rrst", [1, TT])
        t1b = _sb(nc, st, "ab_t1", [128, TT], BF16)
        t2b = _sb(nc, st, "ab_t2", [128, TT], BF16)
        sig = [_sb(nc, st, f"ab_sig{i}", [128, TT], BF16) for i in range(2)]
        xo = [_sb(nc, st, f"ab_xo{i}", [128, TT]) for i in range(2)]
        ps = [_psum(nc, st, f"ab_p{i}") for i in range(8)]

        P.op("vector", lambda v: v.memset(hblk[:, 0:64], 0.0))
        tok_hb = P.op("vector", lambda v: v.memset(hblk[:, 64:128], 1.0), inc=[("v", 1)])
        P.op("gpsimd", lambda e: e.affine_select(
            out=hblk[:, 0:64], in_=hblk[:, 0:64], compare_op=ALU.is_ge,
            fill=1.0, base=-64, pattern=[[0, 64]], channel_multiplier=1),
            wait=[("v", tok_hb)])
        P.op("gpsimd", lambda e: e.affine_select(
            out=hblk[:, 64:128], in_=hblk[:, 64:128], compare_op=ALU.is_ge,
            fill=0.0, base=-64, pattern=[[0, 64]], channel_multiplier=1))
        tok_hblk = P.op("gpsimd", lambda e: e.nop(), inc=[("g", 1)])

        tok_kvdone = env.kv_done[1]
        rd_pre_last = {2: ("v", env.tok_bcasts), 3: ("v", env.tok_bcasts)}
        tok_wq = P.op("sync", lambda e: e.dma_start(
            out=warena.rearrange("p (c x) -> p c x", x=1024),
            in_=env.wbb["q"].rearrange("(c p) x -> p c x", p=128)),
            wait=[env.cw("caB"), ("t", P.val("t"))], inc=[("kw0", 16)])

        o3d = [_sb(nc, st, f"ab_o3d{i}", [128, NCH * TT], BF16) for i in range(2)]
        xtd = [_sb(nc, st, f"ab_xtd{i}", [128, NCH * TT]) for i in range(2)]
        qind = [_sb(nc, st, f"ab_qin{i}", [128, NCH * TT], BF16) for i in range(2)]
        free_xtd = [0, 0]
        tok_ss_prev = 0
        tok_o3st_prev = 0
        tok_sq_prev = 0
        free_rrst = 0
        rd_ps = [("v", env.tok_bcasts)] * 8
        rd_ps[2] = rd_pre_last[2]
        rd_ps[3] = rd_pre_last[3]
        tok_srow_rd = [0, 0]
        tok_kvld = 0
        free_kvld = 0
        o3_toks = {}
        tok_wq = P.op("sync", lambda e: e.dma_start(
            out=warena.rearrange("p (c x) -> p c x", x=1024),
            in_=env.wbb["q"].rearrange("(c p) x -> p c x", p=128)),
            wait=[env.cw("caB"), ("t", P.val("t"))], inc=[("kw0", 16)])
        tile_list = []
        for j in range(NTILES // 2):
            for l in range(B_LOC):
                tile_list.append((j, l))
        fronts = {}

        def emit_d1_loads(idx):
            j, l = tile_list[idx]
            i = l * (NTILES // 2) + j
            t0 = i * TT
            sl = idx % 2
            tok_xld = P.op("sync", lambda e, t0=t0, sl=sl: e.dma_start(
                out=xtd[sl].rearrange("p (c t) -> p c t", t=TT),
                in_=src[:, t0:t0 + TT].rearrange("(c p) t -> p c t", p=128)),
                wait=src_ready(i) + [("v", free_xtd[sl])], inc=[("ax", 16)])
            P.op("sync", lambda e, t0=t0, sl=sl: e.dma_start(
                out=srow[sl][0:1, 0:TT], in_=env.srows[0:1, t0:t0 + TT]),
                wait=stats_done(i) + [("t", tok_srow_rd[sl])], inc=[("sr", 16)])
            tok_sr = P.op("sync", lambda e, t0=t0, sl=sl: e.dma_start(
                out=srow[sl][0:1, TT:2 * TT], in_=env.srows[1:2, t0:t0 + TT]),
                inc=[("sr", 16)])
            return tok_xld, tok_sr

        def emit_d1_front(idx, loads, gate_t=0):
            nonlocal tok_kvld, free_kvld
            j, l = tile_list[idx]
            i = l * (NTILES // 2) + j
            t0 = i * TT
            sl = idx % 2
            tok_xld, tok_sr = loads
            if l == 0:
                tok_kvld = P.op("sync", lambda e, j=j: e.dma_start(
                    out=kv0.rearrange("p (c t) -> p c t", t=TT),
                    in_=env.kbs[:, j * TT:(j + 1) * TT].rearrange("(c p) t -> p c t", p=128)),
                    wait=[("kvs", tok_kvdone), ("t", free_kvld)], inc=[("kv", 16)])
                P.op("sync", lambda e, j=j: e.dma_start(
                    out=kv1.rearrange("p (c t) -> p c t", t=TT),
                    in_=env.kbs[:, T + j * TT: T + (j + 1) * TT]
                    .rearrange("(c p) t -> p c t", p=128)), inc=[("kv", 16)])
                P.op("sync", lambda e, j=j: e.dma_start(
                    out=vv0.rearrange("p (c t) -> p c t", t=TT),
                    in_=env.vbs[:, j * TT:(j + 1) * TT].rearrange("(c p) t -> p c t", p=128)),
                    inc=[("kv", 16)])
                tok_kvld = P.op("sync", lambda e, j=j: e.dma_start(
                    out=vv1.rearrange("p (c t) -> p c t", t=TT),
                    in_=env.vbs[:, T + j * TT: T + (j + 1) * TT]
                    .rearrange("(c p) t -> p c t", p=128)), inc=[("kv", 16)])
            P.op("tensor", lambda t, sl=sl: t.matmul(
                ps[0][:, 0:TT], env.ones_row_f[:, :], srow[sl][0:1, 0:TT],
                start=True, stop=True),
                wait=[("sr", tok_sr), (rd_ps[0][0], rd_ps[0][1])])
            tok_bc = P.op("tensor", lambda t, sl=sl: t.matmul(
                ps[1][:, 0:TT], env.ones_row_f[:, :], srow[sl][0:1, TT:2 * TT],
                start=True, stop=True),
                wait=[(rd_ps[1][0], rd_ps[1][1])], inc=[("t", 1)])
            tok_srow_rd[sl] = tok_bc
            for c in range(NCH):
                P.op("vector", lambda v, c=c, sl=sl: v.tensor_sub(
                    xo[0][:, :], xtd[sl][:, c * TT:(c + 1) * TT], ps[0][:, 0:TT]),
                    wait=([("t", tok_bc), ("ax", tok_xld)] if c == 0 else []))
                P.op("vector", lambda v, c=c: v.tensor_mul(
                    xn[:, c * TT:(c + 1) * TT], xo[0][:, :], ps[1][:, 0:TT]))
            tok_xn = P.op("vector", lambda v: v.engine_nop(), inc=[("v", 1)])
            free_xtd[sl] = tok_xn
            rd_ps[0] = ("v", tok_xn)
            rd_ps[1] = ("v", tok_xn)
            for mc in range(NCH):
                pbk = 2 + (mc % 2)
                tok_mm = 0
                for c in range(NCH):
                    tok_mm = P.op("tensor", lambda t, mc=mc, c=c, pbk=pbk: t.matmul(
                        ps[pbk][:, 0:TT],
                        warena[:, c * 1024 + mc * 128: c * 1024 + mc * 128 + 128],
                        xn[:, c * TT:(c + 1) * TT],
                        start=(c == 0), stop=(c == NCH - 1)),
                        wait=([("kw0", tok_wq), ("v", tok_xn),
                               (rd_ps[pbk][0], rd_ps[pbk][1])] if c == 0 else []),
                        inc=[("t", 1)] if c == NCH - 1 else [])
                tok_ev = P.op("scalar", lambda sc, mc=mc, pbk=pbk: sc.activation(
                    qb[:, mc * TT:(mc + 1) * TT], ps[pbk][:, 0:TT], AF.Identity,
                    bias=env.bqb_s[:, mc:mc + 1], scale=0.125),
                    wait=[("t", tok_mm)], inc=[("s", 1)])
                rd_ps[pbk] = ("s", tok_ev)
            tok_qb = P.val("s")
            tok_tmp = 0
            for c in range(NCH):
                P.op("vector", lambda v, c=c: v.tensor_mul(
                    tmp[:, c * TT:(c + 1) * TT], qb[:, c * TT:(c + 1) * TT],
                    kv0[:, c * TT:(c + 1) * TT]),
                    wait=([("s", tok_qb), ("kv", tok_kvld), ("t", gate_t)]
                          if c == 0 else []))
                tok_tmp = P.op("vector", lambda v, c=c: v.tensor_mul(
                    tmp2[:, c * TT:(c + 1) * TT],
                    qb[:, c * TT:(c + 1) * TT], kv1[:, c * TT:(c + 1) * TT]),
                    inc=[("v", 1)])
            return i, t0, tok_tmp

        def emit_d1_back(idx, front_info, interleave=None):
            nonlocal tok_ss_prev, tok_o3st_prev, tok_sq_prev, free_rrst, free_kvld
            i, t0, tok_tmp = front_info
            tok_o3 = 0
            rd_pb2 = [0, 0]
            for c in range(NCH):
                sl2 = c % 2
                pbk = 4 + (c % 2)
                tok_l0 = P.op("tensor", lambda t, c=c, pbk=pbk: t.matmul(
                    ps[pbk][:, 0:TT], hblk[:, :], tmp[:, c * TT:(c + 1) * TT],
                    start=True, stop=True),
                    wait=([("v", tok_tmp), ("g", tok_hblk)] if c == 0 else [])
                    + [(rd_ps[pbk][0], rd_ps[pbk][1])], inc=[("t", 1)])
                tok_l1 = P.op("tensor", lambda t, c=c: t.matmul(
                    ps[6][:, 0:TT], hblk[:, :], tmp2[:, c * TT:(c + 1) * TT],
                    start=True, stop=True),
                    wait=[(rd_ps[6][0], rd_ps[6][1])], inc=[("t", 1)])
                tok_r0 = P.op("scalar", lambda sc, pbk=pbk, sl2=sl2: sc.activation(
                    pb2[:, sl2 * 2 * TT: sl2 * 2 * TT + TT], ps[pbk][:, 0:TT], AF.Relu),
                    wait=[("t", tok_l0), ("v", rd_pb2[sl2])], inc=[("s", 1)])
                rd_ps[pbk] = ("s", tok_r0)
                tok_r1 = P.op("scalar", lambda sc, sl2=sl2: sc.activation(
                    pb2[:, sl2 * 2 * TT + TT: sl2 * 2 * TT + 2 * TT], ps[6][:, 0:TT],
                    AF.Relu),
                    wait=[("t", tok_l1)], inc=[("s", 1)])
                rd_ps[6] = ("s", tok_r1)
                P.op("vector", lambda v, c=c, sl2=sl2: v.tensor_mul(
                    t1b[:, :], pb2[:, sl2 * 2 * TT: sl2 * 2 * TT + TT],
                    vv0[:, c * TT:(c + 1) * TT]),
                    wait=[("s", tok_r1)])
                P.op("vector", lambda v, c=c, sl2=sl2: v.tensor_mul(
                    t2b[:, :], pb2[:, sl2 * 2 * TT + TT: sl2 * 2 * TT + 2 * TT],
                    vv1[:, c * TT:(c + 1) * TT]))
                tok_o3 = P.op("vector", lambda v, c=c: v.tensor_add(
                    o3s[:, c * TT:(c + 1) * TT], t1b[:, :], t2b[:, :]),
                    inc=[("v", 1)])
                rd_pb2[sl2] = tok_o3
            free_kvld = P.val("t")
            if interleave is not None:
                interleave(P.val("t"))
            tok_o3st = P.op("gpsimd", lambda e, t0=t0: e.dma_start(
                out=env.o3[:, t0:t0 + TT].rearrange("(c p) t -> p c t", p=128),
                in_=o3s.rearrange("p (c t) -> p c t", t=TT)),
                wait=[("v", tok_o3)], inc=[("o3s", 16)])
            tok_o3st_prev = tok_o3st
            o3_toks[i] = tok_o3st
            tok_sq = tok_sq_prev = P.op("scalar", lambda sc: sc.activation(
                sq[:, :], o3s[:, :], AF.Square),
                wait=[("v", tok_o3)], inc=[("s", 1)])
            tok_ss = 0
            for c in range(NCH):
                tok_ss = P.op("tensor", lambda t, c=c: t.matmul(
                    ps[7][0:1, 0:TT], env.ones_col_b[:, :], sq[:, c * TT:(c + 1) * TT],
                    start=(c == 0), stop=(c == NCH - 1)),
                    wait=[("s", tok_sq), (rd_ps[7][0], rd_ps[7][1])] if c == 0 else [],
                    inc=[("t", 1)] if c == NCH - 1 else [])
            tok_ss_prev = tok_ss
            tv = P.op("vector", lambda v: v.tensor_scalar(
                rrst[0:1, 0:TT], ps[7][0:1, 0:TT], 1e-30, None, ALU.add),
                wait=[("t", tok_ss), ("rrs", free_rrst)], inc=[("v", 1)])
            rd_ps[7] = ("v", tv)
            ts1 = P.op("scalar", lambda sc: sc.activation(
                rrst[0:1, 0:TT], rrst[0:1, 0:TT], AF.Ln),
                wait=[("v", tv)], inc=[("s", 1)])
            ts2 = P.op("scalar", lambda sc: sc.activation(
                rrst[0:1, 0:TT], rrst[0:1, 0:TT], AF.Exp, scale=0.5),
                wait=[("s", ts1)], inc=[("s", 1)])
            tv2 = P.op("vector", lambda v: v.tensor_scalar(
                rrst[0:1, 0:TT], rrst[0:1, 0:TT], 1.0 / 32.0, 1e-8,
                ALU.mult, ALU.add), wait=[("s", ts2)], inc=[("v", 1)])
            ts3 = P.op("scalar", lambda sc: sc.activation(
                rrst[0:1, 0:TT], rrst[0:1, 0:TT], AF.Ln),
                wait=[("v", tv2)], inc=[("s", 1)])
            ts4 = P.op("scalar", lambda sc: sc.activation(
                rrst[0:1, 0:TT], rrst[0:1, 0:TT], AF.Exp, scale=-1.0),
                wait=[("s", ts3)], inc=[("s", 1)])
            free_rrst = P.op("gpsimd", lambda e, t0=t0: e.dma_start(
                out=env.rrow[0:1, t0:t0 + TT], in_=rrst[0:1, 0:TT]),
                wait=[("s", ts4)], inc=[("rrs", 16)])

        d1_loads = {0: emit_d1_loads(0)}
        fronts[0] = emit_d1_front(0, d1_loads[0])
        for idx in range(len(tile_list)):
            if idx + 1 < len(tile_list):
                d1_loads[idx + 1] = emit_d1_loads(idx + 1)
                def ilv(gate_t, idx=idx):
                    fronts[idx + 1] = emit_d1_front(idx + 1, d1_loads[idx + 1],
                                                    gate_t=gate_t)
                emit_d1_back(idx, fronts[idx], interleave=ilv)
            else:
                emit_d1_back(idx, fronts[idx])
        tok_d1_done = [(sn, P.val(sn)) for sn in ("o3s", "rrs")]

        tok_wo = P.op("sync", lambda e: e.dma_start(
            out=warena.rearrange("p (c x) -> p c x", x=1024),
            in_=env.wbb["o"].rearrange("(c p) x -> p c x", p=128)),
            wait=[env.cw("caB"), ("t", P.val("t"))], inc=[("kw0", 16)])
        free_xo = [0, 0]
        free_o3d = [0, 0]
        free_qind = [0, 0]
        tok_rr_rd = [tok_srow_rd[0], tok_srow_rd[1]]
        toks_q = {}

        def emit_d2_front(i):
            sl = i % 2
            t0 = i * TT
            tok_xld = P.op("sync", lambda e, t0=t0, sl=sl: e.dma_start(
                out=xtd[sl].rearrange("p (c t) -> p c t", t=TT),
                in_=src[:, t0:t0 + TT].rearrange("(c p) t -> p c t", p=128)),
                wait=[("v", free_xtd[sl])], inc=[("ax", 16)])
            tok_o3ld = P.op("sync", lambda e, t0=t0, sl=sl: e.dma_start(
                out=o3d[sl].rearrange("p (c t) -> p c t", t=TT),
                in_=env.o3[:, t0:t0 + TT].rearrange("(c p) t -> p c t", p=128)),
                wait=tok_d1_done + [("s", free_o3d[sl][0] if False else 0),
                                    ("v", free_o3d[sl])], inc=[("o3l", 16)])
            tok_rr = P.op("sync", lambda e, t0=t0, sl=sl: e.dma_start(
                out=srow[sl][0:1, 0:TT], in_=env.rrow[0:1, t0:t0 + TT]),
                wait=[("t", tok_rr_rd[sl])], inc=[("sr", 16)])
            tok_rb = P.op("tensor", lambda t, sl=sl: t.matmul(
                ps[0][:, 0:TT], env.ones_row_f[:, :], srow[sl][0:1, 0:TT],
                start=True, stop=True),
                wait=[("sr", tok_rr), (rd_ps[0][0], rd_ps[0][1])], inc=[("t", 1)])
            tok_rr_rd[sl] = tok_rb
            free_o3d_guard = free_o3d[sl]
            tok_qin = 0
            for c in range(NCH):
                tok_sg = P.op("scalar", lambda sc, c=c, sl=sl: sc.activation(
                    sig[sl][:, :], o3d[sl][:, c * TT:(c + 1) * TT], AF.Sigmoid,
                    scale=env.bcol["ng"][:, c:c + 1]),
                    wait=([("o3l", tok_o3ld), ("v", free_o3d_guard)] if c == 0
                          else [("v", tok_qin)]),
                    inc=[("s", 1)])
                tv1 = P.op("vector", lambda v, c=c, sl=sl: v.scalar_tensor_tensor(
                    t1b[:, :], ps[0][:, 0:TT], 1.0, o3d[sl][:, c * TT:(c + 1) * TT],
                    ALU.mult, ALU.mult),
                    wait=[("t", tok_rb), ("s", tok_sg)]
                    + ([("t", free_qind[sl])] if c == 0 else []), inc=[("v", 1)])
                tok_qin = P.op("vector", lambda v, c=c, sl=sl: v.scalar_tensor_tensor(
                    qind[sl][:, c * TT:(c + 1) * TT], t1b[:, :], env.bcol["nw"][:, c:c + 1],
                    sig[sl][:, :], ALU.mult, ALU.mult),
                    wait=[("v", tv1)], inc=[("v", 1)])
            rd_ps[0] = ("v", tok_qin)
            free_o3d[sl] = tok_qin
            return tok_qin, tok_xld

        tok_front = {0: emit_d2_front(0)}
        for i in range(NTILES):
            sl = i % 2
            t0 = i * TT
            tok_qin, tok_xld = tok_front[i]
            if i + 1 < NTILES:
                pass
            def emit_wo_mm(mc):
                pbk = 2 + (mc % 2)
                tok_mm = 0
                for c in range(NCH):
                    tok_mm = P.op("tensor", lambda t, mc=mc, c=c, pbk=pbk, sl=sl: t.matmul(
                        ps[pbk][:, 0:TT],
                        warena[:, c * 1024 + mc * 128: c * 1024 + mc * 128 + 128],
                        qind[sl][:, c * TT:(c + 1) * TT],
                        start=(c == 0), stop=(c == NCH - 1)),
                        wait=([("kw0", tok_wo), ("v", tok_qin),
                               (rd_ps[pbk][0], rd_ps[pbk][1])] if c == 0 else []),
                        inc=[("t", 1)] if c == NCH - 1 else [])
                return pbk, tok_mm

            def emit_wo_ev(mc, pbk, tok_mm):
                tok_ev = P.op("vector", lambda v, mc=mc, pbk=pbk, sl=sl: v.scalar_tensor_tensor(
                    xo[mc % 2][:, :], ps[pbk][:, 0:TT], env.bcol["bb_o"][:, mc:mc + 1],
                    xtd[sl][:, mc * TT:(mc + 1) * TT], ALU.add, ALU.add),
                    wait=[("t", tok_mm), ("ax", tok_xld),
                          ("xo%d" % (mc % 2), free_xo[mc % 2])], inc=[("v", 1)])
                rd_ps[pbk] = ("v", tok_ev)
                free_xo[mc % 2] = P.op("gpsimd", lambda e, mc=mc, t0=t0: e.dma_start(
                    out=dst[mc * 128:(mc + 1) * 128, t0:t0 + TT], in_=xo[mc % 2][:, :]),
                    wait=[("v", tok_ev)], inc=[("xo%d" % (mc % 2), 16)])

            mm_info = {}
            mm_info[0] = emit_wo_mm(0)
            mm_info[1] = emit_wo_mm(1)
            if i + 1 < NTILES:
                tok_front[i + 1] = emit_d2_front(i + 1)
            for mc in range(2, NCH):
                emit_wo_ev(mc - 2, *mm_info[mc - 2])
                mm_info[mc] = emit_wo_mm(mc)
            emit_wo_ev(NCH - 2, *mm_info[NCH - 2])
            emit_wo_ev(NCH - 1, *mm_info[NCH - 1])
            free_qind[sl] = mm_info[NCH - 1][1]
            free_xtd[sl] = P.val("v")
            store_toks[i] = [("xo0", free_xo[0]), ("xo1", free_xo[1])]
    return store_toks


def build_nc(phases=("full",)):
    nc = bass.Bass("TRN2", target_bir_lowering=False, detect_race_conditions=False)
    P = Prog()
    env = Env()
    stack = contextlib.ExitStack()
    emit_setup(P, nc, stack, env)
    nobar = lambda i: []
    if "full" in phases:
        sw = emit_stats(P, nc, stack, env, env.xT, nobar, tag="0")
        emit_barrier(P)
        emit_attn_a(P, nc, stack, env, env.xT, env.xs, nobar, lambda i: list(sw))
        emit_barrier(P)
        sw = emit_stats(P, nc, stack, env, env.xs, nobar, tag="1")
        emit_barrier(P)
        emit_ffn(P, nc, stack, env, env.xs, env.xs, nobar, lambda i: list(sw), phase=0)
        emit_barrier(P)
        sw = emit_stats(P, nc, stack, env, env.xs, nobar, tag="2")
        emit_barrier(P)
        emit_attn_b(P, nc, stack, env, env.xs, env.xs, nobar, lambda i: list(sw))
        emit_barrier(P)
        sw = emit_stats(P, nc, stack, env, env.xs, nobar, tag="3")
        emit_barrier(P)
        emit_ffn(P, nc, stack, env, env.xs, env.out, nobar, lambda i: list(sw), phase=1)
    else:
        sw = emit_stats(P, nc, stack, env, env.xT, nobar)
        emit_barrier(P)
        if "attn_a" in phases:
            emit_attn_a(P, nc, stack, env, env.xT, env.out, nobar, lambda i: list(sw))
        elif "attn_b" in phases:
            emit_attn_b(P, nc, stack, env, env.xT, env.out, nobar, lambda i: list(sw))
        else:
            emit_ffn(P, nc, stack, env, env.xT, env.out, nobar, lambda i: list(sw), phase=0)
    final = [(sname, P.val(sname)) for sname in P.counts]
    P.op("gpsimd", lambda e: e.nop(), wait=final)
    P.build(nc, stack)
    stack.close()
    return nc


_NC_CACHE = {}


def _get_nc():
    if "nc" not in _NC_CACHE:
        _NC_CACHE["nc"] = build_nc(phases=("full",))
    return _NC_CACHE["nc"]


def kernel(**inputs):
    in_maps = host_layout(inputs)
    nc = _get_nc()
    res = run_bass_kernel_spmd(nc, in_maps, core_ids=list(range(NCORES)))
    return gather_out(res.results)


def host_layout(inputs):
    f = np.ascontiguousarray
    x = np.asarray(inputs["L0_attn_a_x"], np.float32)
    key = np.asarray(inputs["L0_attn_b_key"], np.float32)
    val = np.asarray(inputs["L0_attn_b_value"], np.float32)
    keyT = f(key.reshape(2 * T, D).T)
    valT = f(val.reshape(2 * T, D).T)
    wsgT = f(np.asarray(inputs["L0_ffn_shared_gate_w"], np.float32).T
             .reshape(D, SH // 128, 128).transpose(1, 0, 2))
    wsuT = f(np.asarray(inputs["L0_ffn_shared_up_w"], np.float32).T
             .reshape(D, SH // 128, 128).transpose(1, 0, 2))
    wsdT = f(np.asarray(inputs["L0_ffn_shared_down_w"], np.float32).T
             .reshape(SH, NCH, 128).transpose(1, 0, 2))
    common = {
        "wa_q": f(np.asarray(inputs["L0_attn_a_wq"], np.float32)),
        "wa_k": f(np.asarray(inputs["L0_attn_a_wk"], np.float32)),
        "wa_v": f(np.asarray(inputs["L0_attn_a_wv"], np.float32)),
        "wa_o": f(np.asarray(inputs["L0_attn_a_wo"], np.float32)),
        "ba_q": np.asarray(inputs["L0_attn_a_bq"], np.float32).reshape(1, D),
        "ba_k": np.asarray(inputs["L0_attn_a_bk"], np.float32).reshape(1, D),
        "ba_v": np.asarray(inputs["L0_attn_a_bv"], np.float32).reshape(1, D),
        "ba_o": np.asarray(inputs["L0_attn_a_bo"], np.float32).reshape(1, D),
        "keyT": keyT, "valT": valT,
        "wb_q": f(np.asarray(inputs["L0_attn_b_Wq"], np.float32).T),
        "wb_k": f(np.asarray(inputs["L0_attn_b_Wk"], np.float32).T),
        "wb_v": f(np.asarray(inputs["L0_attn_b_Wv"], np.float32).T),
        "wb_o": f(np.asarray(inputs["L0_attn_b_Wo"], np.float32).T),
        "bb_q": np.asarray(inputs["L0_attn_b_bq"], np.float32).reshape(1, D),
        "bb_k": np.asarray(inputs["L0_attn_b_bk"], np.float32).reshape(1, D),
        "bb_v": np.asarray(inputs["L0_attn_b_bv"], np.float32).reshape(1, D),
        "bb_o": np.asarray(inputs["L0_attn_b_bo"], np.float32).reshape(1, D),
        "nw": np.asarray(inputs["L0_attn_b_norm_weight"], np.float32).reshape(1, D),
        "ng": np.asarray(inputs["L0_attn_b_norm_gate"], np.float32).reshape(1, D),
        "routerT": f(np.asarray(inputs["L0_ffn_router_w"], np.float32).T),
        "cbias": np.asarray(inputs["L0_ffn_correction_bias"], np.float32).reshape(1, E),
        "wsgT": wsgT, "wsuT": wsuT, "wsdT": wsdT,
        "gw": f(np.asarray(inputs["L0_ffn_gate_w"], np.float32)
                .reshape(E, D, 2, 128).transpose(0, 2, 1, 3)),
        "uw": f(np.asarray(inputs["L0_ffn_up_w"], np.float32)
                .reshape(E, D, 2, 128).transpose(0, 2, 1, 3)),
        "dw": f(np.asarray(inputs["L0_ffn_down_w"], np.float32)),
    }
    in_maps = []
    for core in range(NCORES):
        xc = x[core * B_LOC:(core + 1) * B_LOC]
        xTc = f(xc.reshape(NTOK, D).T)
        m = dict(common)
        m["xT"] = xTc
        in_maps.append(m)
    return in_maps


def gather_out(results):
    outs = []
    for core in range(NCORES):
        o = results[core]["out"]
        outs.append(np.ascontiguousarray(o.T).reshape(B_LOC, T, D))
    return np.concatenate(outs, axis=0).astype(np.float32)
